# version 14
# baseline (speedup 1.0000x reference)
import math
import os
import traceback
from contextlib import ExitStack

import numpy as np
import ml_dtypes

import concourse.bass as bass
import concourse.mybir as mybir
from concourse.bass import IndirectOffsetOnAxis
from concourse.bass_utils import run_bass_kernel_spmd

F32 = mybir.dt.float32
BF16 = mybir.dt.bfloat16
I32 = mybir.dt.int32
AF = mybir.ActivationFunctionType
ALU = mybir.AluOpType
AX = mybir.AxisListType

B, N, D = 8, 2048, 1024
H, DH = 8, 64
NCT = 16
ND = 8
NSEL = 1024
DEBUG = bool(int(os.environ.get("MK_DEBUG", "0")))

MAGIC = float(np.float32(1.5 * 2.0 ** 23))
NEG_BIG = float(np.float32(-1.0e38))

def _split12(x):
    hi = np.float32(x)
    u = np.frombuffer(np.float32(hi).tobytes(), dtype=np.uint32)[0]
    u &= np.uint32(0xFFFFF000)
    hi = np.frombuffer(np.uint32(u).tobytes(), dtype=np.float32)[0]
    lo = np.float32(np.float64(x) - np.float64(hi))
    lo2 = np.float32(np.float64(x) - np.float64(hi) - np.float64(lo))
    return float(hi), float(lo), float(lo2)

TWO_PI_HI, TWO_PI_LO, TWO_PI_LO2 = _split12(2.0 * math.pi)
LN2_HI, LN2_LO, _ = _split12(math.log(2.0))
L2E4_HI, L2E4_LO, L2E4_LO2 = _split12(math.log2(10000.0))
INV_2PI = float(np.float32(1.0 / (2.0 * math.pi)))
PI = float(np.float32(math.pi))
HALF_PI = float(np.float32(math.pi / 2.0))

CURS = []
_c = 4.0
for _k in range(20):
    CURS.append(_c)
    _c = max(_c * 0.7, 0.03)
LOGK = math.log(1152.0)
A1 = float(np.float32(np.float32(4.0) * (np.float32(LOGK) - np.float32(math.log(2048.0)))))

ENGINES = ("tensor", "vector", "scalar", "gpsimd", "sync")


class Prog:

    def __init__(self, nc, es):
        self.nc = nc
        self.es = es
        self.sem = {e: es.enter_context(nc.semaphore("S" + e[:2])) for e in ENGINES}
        self.cnt = {e: 0 for e in ENGINES}
        self.body = {e: [] for e in ENGINES}
        self.dma_groups = {}
        self.last_w = {}
        self.readers = {}
        self.waited = {e: {} for e in ENGINES}
        self.fp32_mm_pending = False

    def _names(self, aps):
        out = []
        for ap in aps:
            if ap is None:
                continue
            t = ap.tensor if hasattr(ap, "tensor") else ap
            out.append(t.name)
        return out

    def _srckey(self, dep):
        return ("eng", dep[1]) if dep[0] == "eng" else ("dma", dep[1])

    def _collect_waits(self, eng, ins, outs, bump_dma=False):
        need = {}
        def add(dep):
            if dep is None:
                return
            k = self._srckey(dep)
            if k == ("eng", eng) and eng == "tensor":
                return
            if need.get(k, -1) < dep[2]:
                need[k] = dep[2]
        for nm in self._names(ins):
            add(self.last_w.get(nm))
        for nm in self._names(outs):
            add(self.last_w.get(nm))
            for rk, rv in self.readers.get(nm, {}).items():
                add((rk[0], rk[1], rv))
        waits = []
        for (kind, src), v in sorted(need.items(), key=lambda kv: str(kv[0])):
            if kind == "dma" and bump_dma:
                v = self.dma_groups[src][1]
            if self.waited[eng].get(src, -1) >= v:
                continue
            waits.append((self.sem[src] if kind == "eng" else self.dma_groups[src][0], v))
            self.waited[eng][src] = v
        return waits

    def _record(self, ins, outs, dep):
        for nm in self._names(ins):
            self.readers.setdefault(nm, {})[self._srckey(dep)] = dep[2]
        for nm in self._names(outs):
            self.last_w[nm] = dep
            self.readers[nm] = {}

    def op(self, eng, builder, ins=(), outs=()):
        builder._src = traceback.extract_stack()[-3]
        waits = self._collect_waits(eng, ins, outs)
        self.cnt[eng] += 1
        dep = ("eng", eng, self.cnt[eng])
        self._record(ins, outs, dep)
        self.body[eng].append((waits, builder, self.sem[eng], 1))

    def mm(self, out_ap, lhsT_ap, rhs_ap, start, stop, fp32=False):
        small_k = lhsT_ap.shape[0] < 128
        if small_k and self.fp32_mm_pending:
            v = self.cnt["tensor"]
            if self.waited["tensor"].get("tensor", -1) < v:
                self.body["tensor"].append(([(self.sem["tensor"], v)], None, None, 0))
                self.waited["tensor"]["tensor"] = v
            self.fp32_mm_pending = False
        if fp32:
            self.fp32_mm_pending = True
        self.op("tensor",
                lambda t, o=out_ap, l=lhsT_ap, r=rhs_ap, s0=start, s1=stop:
                t.matmul(o, l, r, start=s0, stop=s1),
                ins=[lhsT_ap, rhs_ap], outs=[out_ap])

    def transpose_pe(self, out_ap, in_ap, ident_ap):
        self.op("tensor", lambda t, o=out_ap, i=in_ap, d=ident_ap: t.transpose(o, i, d),
                ins=[in_ap, ident_ap], outs=[out_ap])

    def _grp(self, group):
        if group not in self.dma_groups:
            self.dma_groups[group] = [
                self.es.enter_context(self.nc.semaphore("D" + group[:6])), 0]
        return self.dma_groups[group]

    def dma(self, queue, group, out_ap, in_ap, **kw):
        g = self._grp(group)
        waits = self._collect_waits(queue, [in_ap], [out_ap], bump_dma=True)
        g[1] += 16
        self._record([in_ap], [out_ap], ("dma", group, g[1]))
        def b(eng, _o=out_ap, _i=in_ap, _kw=kw):
            return eng.dma_start(out=_o, in_=_i, **_kw)
        b._src = traceback.extract_stack()[-2]
        self.body[queue].append((waits, b, g[0], 16))

    def idma(self, group, out_ap, out_off, in_ap, in_off, **kw):
        g = self._grp(group)
        ins = [in_ap] + ([out_off.ap] if out_off is not None else []) + \
              ([in_off.ap] if in_off is not None else [])
        waits = self._collect_waits("gpsimd", ins, [out_ap], bump_dma=True)
        g[1] += 16
        self._record(ins, [out_ap], ("dma", group, g[1]))
        def b(eng, _o=out_ap, _oo=out_off, _i=in_ap, _io=in_off, _kw=kw):
            try:
                return eng.indirect_dma_start(out=_o, out_offset=_oo, in_=_i,
                                              in_offset=_io, **_kw)
            except Exception:
                print("IDMA FAIL out=", _o, "\nin=", _i,
                      "\noo=", None if _oo is None else _oo.ap,
                      "\nio=", None if _io is None else _io.ap)
                raise
        b._src = traceback.extract_stack()[-2]
        self.body["gpsimd"].append((waits, b, g[0], 16))

    def barrier_dma(self, eng, group):
        g = self.dma_groups[group]
        if self.waited[eng].get(group, -1) < g[1]:
            self.body[eng].append(([(g[0], g[1])], None, None, 0))
            self.waited[eng][group] = g[1]

    def full_barrier(self):
        for eng in ENGINES:
            waits = []
            for e2 in ENGINES:
                if e2 != eng and self.waited[eng].get(e2, -1) < self.cnt[e2] \
                        and self.cnt[e2] > 0:
                    waits.append((self.sem[e2], self.cnt[e2]))
                    self.waited[eng][e2] = self.cnt[e2]
            for gname, g in self.dma_groups.items():
                if g[1] > 0 and self.waited[eng].get(gname, -1) < g[1]:
                    waits.append((g[0], g[1]))
                    self.waited[eng][gname] = g[1]
            if waits:
                self.body[eng].append((waits, None, None, 0))

    def finalize(self, block):
        for eng in ENGINES:
            items = self.body[eng]
            def make(seq_items):
                def run(handle):
                    for waits, builder, sem, n_inc in seq_items:
                        for s, v in waits:
                            handle.wait_ge(s, v)
                        if builder is not None:
                            try:
                                inst = builder(handle)
                            except Exception as e:
                                src = getattr(builder, "_src", None)
                                raise RuntimeError(f"emit failed at {src}") from e
                            if sem is not None and inst is not None:
                                inst.then_inc(sem, n_inc)
                return run
            getattr(block, eng)(make(items))


def build_core():
    nc = bass.Bass("TRN2", target_bir_lowering=False)
    es = ExitStack()

    def din(name, shape, dtype=F32):
        return nc.dram_tensor(name, shape, dtype, kind="ExternalInput").ap()

    x_hbm = din("x", [N, D])
    wqkv_hbm = din("wqkv", [D, 1536])
    wlout_hbm = din("wlout", [512, D])
    lgam_hbm = din("lgam", [128, ND])
    lbeta_hbm = din("lbeta", [128, ND])
    qtok_row = din("qtok", [1, D])
    kvtok_row = din("kvtok", [1, D])
    hgam_hbm = din("hgam", [128, ND])
    wq_hbm = din("wq", [D, 512])
    wkv_hbm = din("wkv", [D, 1024])
    nullk_hbm = din("nullk", [DH, H])
    nullv_hbm = din("nullv", [1, H * DH])
    whout_hbm = din("whout", [512, D])
    nullq_row = din("nullq", [1, D])
    wffn_hbm = din("wffn", [D, D])
    bffn_row = din("bffn", [1, D])
    dw_hbm = din("dw", [D, 32])
    dact_hbm = din("dact", [1, 1])
    lmask_hbm = din("lmask", [128, 384])
    e0f_hbm = din("e0f", [128, 128])
    trie_hbm = din("trie", [128, 128], BF16)
    patkv_hbm = din("patkv", [1, 32])
    pm8_hbm = din("pm8", [128, 1])
    atab_hbm = din("atab", [N, 384], BF16)

    out_hbm = nc.dram_tensor("out", [N, D], F32, kind="ExternalOutput").ap()
    dbg = {}
    def dout(name, shape, dtype=F32):
        t = nc.dram_tensor(name, shape, dtype, kind="ExternalOutput").ap()
        dbg[name] = t
        return t

    d_light = nc.dram_tensor("d_light", [N, D], BF16, kind="Internal").ap()
    d_g = nc.dram_tensor("d_g", [N, D], BF16, kind="Internal").ap()
    d_selq = nc.dram_tensor("d_selq", [NSEL, D + 388], BF16, kind="Internal").ap()
    d_selkv = nc.dram_tensor("d_selkv", [NSEL, D + 388], BF16, kind="Internal").ap()
    d_heavy2 = nc.dram_tensor("d_heavy2", [N, D], BF16, kind="Internal").ap()
    if DEBUG:
        dbg["dbg_attn"] = nc.dram_tensor("dbg_attn", [128, H * 384], BF16,
                                         kind="ExternalOutput").ap()
        dbg["dbg_g"] = nc.dram_tensor("dbg_g", [N, D], BF16,
                                      kind="ExternalOutput").ap()

    with es:
        def sb(name, shape, dtype=F32):
            return es.enter_context(nc.sbuf_tensor(name, shape, dtype))
        def psum(scope, name, shape, dtype=F32):
            return scope.enter_context(nc.psum_tensor(name, shape, dtype))

        p = Prog(nc, es)

        xn = [sb(f"xn{c}", [128, D]) for c in range(NCT)]
        scr_v = sb("scr_v", [128, D])
        scr_a = sb("scr_a", [128, D])
        stat_sum = sb("stat_sum", [128, NCT])
        stat_sq = sb("stat_sq", [128, NCT])
        ln_m = sb("ln_m", [128, NCT])
        ln_r = sb("ln_r", [128, NCT])
        lmask = sb("lmask_s", [128, 384])
        p.dma("sync", "gwa", lmask[:], lmask_hbm)

        def ln_stats(src_tile, c, sumcol, sqcol):
            p.op("vector", lambda v, t=src_tile, c=c: v.tensor_scalar(
                out=scr_v[:], in0=t[:], scalar1=1.0, scalar2=None,
                op0=ALU.mult, op1=ALU.add, accum_out=sumcol),
                ins=[src_tile], outs=[scr_v, sumcol])
            p.op("scalar", lambda s, t=src_tile, c=c: s.activation(
                out=scr_a[:], in_=t[:], func=AF.Square, accum_out=sqcol),
                ins=[src_tile], outs=[scr_a, sqcol])

        def ln_finish(c, sumcol, sqcol, mcol, rcol, eps=1e-5):
            p.op("vector", lambda v: v.tensor_scalar(
                out=mcol, in0=sumcol, scalar1=float(1.0 / D), scalar2=None,
                op0=ALU.mult), ins=[sumcol], outs=[mcol])
            p.op("vector", lambda v: v.scalar_tensor_tensor(
                out=scr_v[:, 0:1], in0=mcol, scalar=-1.0, in1=mcol,
                op0=ALU.mult, op1=ALU.mult), ins=[mcol], outs=[scr_v])
            p.op("vector", lambda v: v.tensor_scalar(
                out=scr_v[:, 1:2], in0=sqcol, scalar1=float(1.0 / D),
                scalar2=float(eps), op0=ALU.mult, op1=ALU.add),
                ins=[sqcol], outs=[scr_v])
            p.op("vector", lambda v: v.tensor_tensor(
                out=scr_v[:, 2:3], in0=scr_v[:, 1:2], in1=scr_v[:, 0:1],
                op=ALU.add), ins=[scr_v], outs=[scr_v])
            p.op("scalar", lambda s: s.activation(
                out=scr_a[:, 0:1], in_=scr_v[:, 2:3], func=AF.Sqrt),
                ins=[scr_v], outs=[scr_a])
            p.op("vector", lambda v: v.reciprocal(out=rcol, in_=scr_a[:, 0:1]),
                ins=[scr_a], outs=[rcol])

        l_m = sb("l_m", [128, NCT])
        l_r = sb("l_r", [128, NCT])
        lgam_s = sb("lgam_s", [128, ND])
        lbeta_s = sb("lbeta_s", [128, ND])
        b0qk = sb("b0qk", [128, 8])
        e0b = sb("e0b", [128, 128], BF16)

        sbq = sb
        wqT_g = sbq("wqT_g", [128, 8 * 1024], BF16)
        bfrow = sbq("bfrow", [128, 1024], BF16)
        dcol = sbq("dcol", [128, 1])
        qx1 = sbq("qx1", [128, D]); qx2 = sbq("qx2", [128, D])
        dwt = qx2
        qx4_box = [scr_v]
        wqst = scr_a
        ytile = [sbq("ytile0", [128, D], BF16)]
        yT = [sbq("yT0", [128, D], BF16)]
        ytile_box = [ytile]; yT_box = [yT]
        g_st = [scr_a]

        p.op("vector", lambda v: v.memset(bfrow[:], 0.0), outs=[bfrow])
        p.dma("scalar", "gw6b", scr_v[0:1, 0:D], bffn_row)
        p.barrier_dma("vector", "gw6b")
        p.op("vector", lambda v: v.tensor_copy(out=bfrow[0:1, :], in_=scr_v[0:1, 0:D]),
             ins=[scr_v], outs=[bfrow])
        p.dma("scalar", "gw6c", dcol[:], dact_hbm.broadcast_to([128, 1]))
        p.barrier_dma("scalar", "gw6c")
        p.barrier_dma("vector", "gw6c")
        p.op("scalar", lambda sc: sc.activation(out=dcol[:], in_=dcol[:],
             func=AF.Tanh, scale=4.0), ins=[dcol], outs=[dcol])
        p.op("vector", lambda v: v.tensor_scalar(out=dcol[:], in0=dcol[:],
             scalar1=0.5, scalar2=None, op0=ALU.mult), ins=[dcol], outs=[dcol])

        def quant_pipe(src_ap, dst_tile, src_names, qsc_ap=None, qrs_ap=None,
                       qnames=()):
            bc = lambda t: t[:, :, None].broadcast_to([128, 32, 32])
            p.op("vector", lambda v, src_ap=src_ap, qrs_ap=qrs_ap: v.tensor_tensor(
                 out=qx1[:].rearrange("p (b k) -> p b k", b=32), in0=src_ap,
                 in1=bc(qrs_ap), op=ALU.mult),
                 ins=list(src_names) + list(qnames), outs=[qx1])
            p.op("scalar", lambda sc: sc.activation(out=qx2[:], in_=qx1[:],
                 func=AF.Abs), ins=[qx1], outs=[qx2])
            qx4 = qx4_box[0]
            p.op("vector", lambda v, qx4=qx4: v.tensor_scalar(out=qx4[:], in0=qx2[:],
                 scalar1=0.5, scalar2=None, op0=ALU.is_gt), ins=[qx2], outs=[qx4])
            for thr, cmp in ((1.5, ALU.is_ge), (2.5, ALU.is_gt), (3.5, ALU.is_ge),
                             (5.0, ALU.is_gt), (5.0, ALU.is_gt),
                             (7.0, ALU.is_ge), (7.0, ALU.is_ge),
                             (10.0, ALU.is_gt), (10.0, ALU.is_gt),
                             (10.0, ALU.is_gt), (10.0, ALU.is_gt)):
                p.op("vector", lambda v, thr=thr, cmp=cmp, qx4=qx4: v.scalar_tensor_tensor(
                     out=qx4[:], in0=qx2[:], scalar=thr, in1=qx4[:],
                     op0=cmp, op1=ALU.add), ins=[qx2, qx4], outs=[qx4])
            p.op("scalar", lambda sc: sc.activation(out=qx2[:], in_=qx1[:],
                 func=AF.Sign), ins=[qx1], outs=[qx2])
            p.op("vector", lambda v, qx4=qx4: v.tensor_tensor(out=qx2[:], in0=qx2[:],
                 in1=qx4[:], op=ALU.mult), ins=[qx2, qx4], outs=[qx2])
            p.op("vector", lambda v, dst_tile=dst_tile, qsc_ap=qsc_ap: v.tensor_tensor(
                 out=dst_tile[:].rearrange("p (b k) -> p b k", b=32),
                 in0=qx2[:].rearrange("p (b k) -> p b k", b=32),
                 in1=bc(qsc_ap), op=ALU.mult),
                 ins=[qx2] + list(qnames), outs=[dst_tile])

        amw = qx1
        qscw = sbq("qscw", [128, 256])
        qrsw = sbq("qrsw", [128, 256])

        def w6_amax(ot):
            p.dma("scalar", "gw6", wqst[:], wffn_hbm[ot * 128:(ot + 1) * 128, :])
            p.dma("scalar", "gw6d", dwt[:, 0:32], dw_hbm[ot * 128:(ot + 1) * 128, :])
            p.barrier_dma("vector", "gw6")
            p.barrier_dma("scalar", "gw6d")
            p.op("vector", lambda v, ot=ot: v.reduce_max(
                 out=amw[:, ot * 32:(ot + 1) * 32, None],
                 in_=wqst[:].rearrange("p (b k) -> p b k", b=32),
                 axis=AX.X, apply_absolute_value=True), ins=[wqst], outs=[amw])
            p.op("scalar", lambda sc, ot=ot: sc.activation(
                 out=qscw[:, ot * 32:(ot + 1) * 32], in_=dwt[:, 0:32],
                 func=AF.Tanh, scale=4.0), ins=[dwt], outs=[qscw])

        def w6_scale_batch():
            p.op("vector", lambda v: v.tensor_scalar(out=qrsw[:], in0=qscw[:],
                 scalar1=0.5, scalar2=None, op0=ALU.mult), ins=[qscw], outs=[qrsw])
            p.op("vector", lambda v: v.memset(qscw[:], -10.0), outs=[qscw])
            for k in range(-9, 7):
                p.op("vector", lambda v, k=k: v.scalar_tensor_tensor(out=qscw[:],
                     in0=amw[:, 0:256], scalar=float(2.0 ** k), in1=qscw[:],
                     op0=ALU.is_ge, op1=ALU.add), ins=[amw, qscw], outs=[qscw])
            p.op("vector", lambda v: v.tensor_tensor(out=qrsw[:], in0=qscw[:],
                 in1=qrsw[:], op=ALU.add), ins=[qscw, qrsw], outs=[qrsw])
            p.op("vector", lambda v: v.tensor_scalar(out=qrsw[:], in0=qrsw[:],
                 scalar1=-2.0, scalar2=None, op0=ALU.add), ins=[qrsw], outs=[qrsw])
            p.op("vector", lambda v: v.tensor_scalar(out=amw[:, 0:256], in0=qrsw[:],
                 scalar1=LN2_HI, scalar2=None, op0=ALU.mult), ins=[qrsw], outs=[amw])
            p.op("vector", lambda v: v.scalar_tensor_tensor(out=amw[:, 0:256], in0=qrsw[:],
                 scalar=LN2_LO, in1=amw[:, 0:256], op0=ALU.mult, op1=ALU.add),
                 ins=[qrsw], outs=[amw])
            p.op("scalar", lambda sc: sc.activation(out=qscw[:], in_=amw[:, 0:256],
                 func=AF.Exp), ins=[amw], outs=[qscw])
            p.op("vector", lambda v: v.reciprocal(out=qrsw[:], in_=qscw[:]),
                 ins=[qscw], outs=[qrsw])
            p.op("vector", lambda v: v.tensor_tensor(out=amw[:, 0:256], in0=qscw[:],
                 in1=qrsw[:], op=ALU.mult), ins=[qscw, qrsw], outs=[amw])
            p.op("vector", lambda v: v.tensor_scalar(out=amw[:, 0:256], in0=amw[:, 0:256],
                 scalar1=-1.0, scalar2=2.0, op0=ALU.mult, op1=ALU.add),
                 ins=[amw], outs=[amw])
            p.op("vector", lambda v: v.tensor_tensor(out=qrsw[:], in0=qrsw[:],
                 in1=amw[:, 0:256], op=ALU.mult), ins=[amw, qrsw], outs=[qrsw])
            p.op("vector", lambda v: v.tensor_scalar(out=qrsw[:], in0=qrsw[:],
                 scalar1=2.0, scalar2=None, op0=ALU.mult), ins=[qrsw], outs=[qrsw])
            p.op("vector", lambda v: v.tensor_scalar(out=qscw[:], in0=qscw[:],
                 scalar1=0.5, scalar2=None, op0=ALU.mult), ins=[qscw], outs=[qscw])

        def w6_quant(ot):
            p.dma("scalar", "gw6", wqst[:], wffn_hbm[ot * 128:(ot + 1) * 128, :])
            p.barrier_dma("vector", "gw6")
            yt = ytile_box[0][ot % len(ytile_box[0])]
            quant_pipe(wqst[:].rearrange("p (b k) -> p b k", b=32), yt, [wqst],
                       qsc_ap=qscw[:, ot * 32:(ot + 1) * 32],
                       qrs_ap=qrsw[:, ot * 32:(ot + 1) * 32], qnames=(qscw, qrsw))
            p.dma("sync", "twq",
                  wqT_g[:, ot * 1024:(ot + 1) * 1024].rearrange("p (b t) -> p b t", b=8),
                  yt[:].rearrange("p (b t) -> p b t", b=8), transpose=True)

        def ffn_chunk(c, pf):
            yt = ytile_box[0][c % len(ytile_box[0])]
            quant_pipe(xn[c][:].rearrange("p (b k) -> p b k", b=32), yt, [xn[c]],
                       qsc_ap=qscx[:, c * 32:(c + 1) * 32],
                       qrs_ap=qrsx[:, c * 32:(c + 1) * 32], qnames=(qscx, qrsx))
            ytw = yT_box[0][c % len(yT_box[0])]
            p.dma("sync", f"tyq{c % 2}", ytw[:].rearrange("p (b t) -> p b t", b=8),
                  yt[:].rearrange("p (b t) -> p b t", b=8), transpose=True)
            ytr = ytw
            pp = pf[c % 2]
            for dc in range(ND):
                for sl in range(2):
                    p.mm(pp[:, sl * 512:(sl + 1) * 512],
                         ytr[:, dc * 128:(dc + 1) * 128],
                         wqT_g[:, :].rearrange("p (o b t) -> p o b t", o=8, b=8)
                             [:, sl * 4:(sl + 1) * 4, dc, :],
                         start=(dc == 0), stop=False)
            for sl in range(2):
                p.mm(pp[:, sl * 512:(sl + 1) * 512], e0b[:],
                     bfrow[:, sl * 512:(sl + 1) * 512], start=False, stop=True)
            gt = g_st[c % len(g_st)] if len(g_st) > 1 else g_st[0]
            p.op("scalar", lambda sc, pp=pp, gt=gt: sc.activation(
                 out=gt[:], in_=pp[:], func=AF.Silu), ins=[pp], outs=[gt])
            if DEBUG:
                p.dma("sync", "dbg", dbg["dbg_g"][c * 128:(c + 1) * 128, :], gt[:])
            p.dma("sync", f"sg{c % 2}", d_g[c * 128:(c + 1) * 128, :], gt[:])

        light_arena = ExitStack()
        def sbl(name, shape, dtype=F32):
            return light_arena.enter_context(nc.sbuf_tensor(name, shape, dtype))
        vtok = [sbl(f"vtok{c}", [128, 520], BF16) for c in range(NCT + 1)]
        qkT = [sbl(f"qkT{oc}", [128, N], BF16) for oc in range(8)]
        avT = sbl("avT", [128, 4 * N], BF16)
        avTv = avT[:].rearrange("p (k c t) -> p k c t", k=4, c=NCT)
        den = sbl("den", [128, H * NCT])
        rden = sbl("rden", [128, H * NCT])

        arena1 = ExitStack()
        xtile = [arena1.enter_context(nc.sbuf_tensor(f"xt{i}", [128, D], F32))
                 for i in range(2)]
        for c in range(NCT):
            xt = xtile[c % 2]
            p.dma("sync", f"gx{c % 2}", xt[:], x_hbm[c * 128:(c + 1) * 128, :])
            p.barrier_dma("vector", f"gx{c % 2}")
            p.barrier_dma("scalar", f"gx{c % 2}")
            ln_stats(xt, c, stat_sum[:, c:c + 1], stat_sq[:, c:c + 1])
            ln_finish(c, stat_sum[:, c:c + 1], stat_sq[:, c:c + 1],
                      ln_m[:, c:c + 1], ln_r[:, c:c + 1])
            p.op("vector", lambda v, xt=xt, c=c: v.tensor_scalar(
                out=xn[c][:], in0=xt[:], scalar1=ln_m[:, c:c + 1],
                scalar2=ln_r[:, c:c + 1], op0=ALU.subtract, op1=ALU.mult),
                ins=[xt, ln_m, ln_r], outs=[xn[c]])

        if DEBUG:
            o_xn = dout("dbg_xn", [N, D])
            for c in range(NCT):
                p.dma("sync", "dbg", o_xn[c * 128:(c + 1) * 128, :], xn[c][:])

        p.full_barrier()
        arena1.close()

        for c in range(NCT):
            ln_stats(xn[c], c, stat_sum[:, c:c + 1], stat_sq[:, c:c + 1])
            ln_finish(c, stat_sum[:, c:c + 1], stat_sq[:, c:c + 1],
                      l_m[:, c:c + 1], l_r[:, c:c + 1])

        p.dma("sync", "gwb", lgam_s[:], lgam_hbm)
        p.dma("sync", "gwc", lbeta_s[:], lbeta_hbm)
        arena2 = ExitStack()
        def sb2(name, shape, dtype=F32):
            return arena2.enter_context(nc.sbuf_tensor(name, shape, dtype))
        wqkvb = [sb2(f"wqkvb{dc}", [128, 1536], BF16) for dc in range(ND)]
        for dc in range(ND):
            for lo, hi in ((0, 1024), (1024, 1536)):
                p.dma("scalar", f"gw2{dc % 2}", scr_v[:, 0:hi - lo],
                      wqkv_hbm[dc * 128:(dc + 1) * 128, lo:hi])
                p.barrier_dma("vector", f"gw2{dc % 2}")
                p.barrier_dma("vector", "gwb")
                p.barrier_dma("vector", "gwc")
                p.op("vector", lambda v, dc=dc, lo=lo, hi=hi: v.tensor_scalar(
                    out=wqkvb[dc][:, lo:hi], in0=scr_v[:, 0:hi - lo],
                    scalar1=lgam_s[:, dc:dc + 1],
                    scalar2=None, op0=ALU.mult),
                    ins=[scr_v, lgam_s], outs=[wqkvb[dc]])

        z1T = [sb2("z1T0", [128, ND * 4 * 128], BF16)]
        z1Tv = [t[:].rearrange("p (c dc t) -> p dc c t", dc=ND, c=4) for t in z1T]
        z1T = z1T * 2; z1Tv = z1Tv * 2
        z1b = [sb2("z1b0", [128, D], BF16)]
        z1b = z1b * 2
        def emit_z1(c):
            sl, ci = c // 4, c % 4
            zb = z1b[c % 2]
            p.op("vector", lambda v, c=c, zb=zb: v.tensor_scalar(
                out=zb[:], in0=xn[c][:], scalar1=l_m[:, c:c + 1],
                scalar2=l_r[:, c:c + 1], op0=ALU.subtract, op1=ALU.mult),
                ins=[xn[c], l_m, l_r], outs=[zb])
            p.dma("sync", "tz", z1T[sl % 2][:, ci * 1024:(ci + 1) * 1024]
                      .rearrange("p (dc t) -> p dc t", dc=ND),
                  zb[:].rearrange("p (dc t) -> p dc t", dc=ND), transpose=True)

        lbeta_b = sb2("lbeta_b", [128, ND], BF16)
        p.op("vector", lambda v: v.reciprocal(out=scr_v[:, 0:ND], in_=lgam_s[:]),
             ins=[lgam_s], outs=[scr_v])
        p.op("vector", lambda v: v.tensor_tensor(
            out=lbeta_b[:], in0=lbeta_s[:], in1=scr_v[:, 0:ND], op=ALU.mult),
            ins=[lbeta_s, scr_v], outs=[lbeta_b])
        b0v_row = sb2("b0v_row", [128, 512], BF16)
        with ExitStack() as ph:
            pb = psum(ph, "pb0", [128, 12])
            pbv = psum(ph, "pb0v", [1, 512])
            for j in range(12):
                for dc in range(ND):
                    p.mm(pb[:, j:j + 1], wqkvb[dc][:, j * 128:(j + 1) * 128],
                         lbeta_b[:, dc:dc + 1], start=(dc == 0),
                         stop=(dc == ND - 1))
            for dc in range(ND):
                p.mm(pbv[:], lbeta_b[:, dc:dc + 1],
                     wqkvb[dc][:, 1024:1536], start=(dc == 0), stop=(dc == ND - 1))
            p.op("vector", lambda v: v.tensor_copy(out=b0qk[:], in_=pb[:, 0:8]),
                 ins=[pb], outs=[b0qk])
            p.op("vector", lambda v: v.memset(b0v_row[:], 0.0), outs=[b0v_row])
            p.op("vector", lambda v: v.tensor_copy(out=b0v_row[0:1, :], in_=pbv[:]),
                 ins=[pbv], outs=[b0v_row])

        p.op("vector", lambda v: v.memset(e0b[:], 0.0), outs=[e0b])
        p.op("vector", lambda v: v.memset(e0b[0:1, :], 1.0), ins=[e0b], outs=[e0b])

        p.op("vector", lambda v: v.memset(vtok[NCT][:], 0.0), outs=[vtok[NCT]])
        with ExitStack() as ph:
            pqk = [psum(ph, f"pqk{i}", [128, 512]) for i in range(6)]
            for sl in range(4):
                for ci in range(4):
                    emit_z1(sl * 4 + ci)
                zt = z1Tv[sl % 2]
                rhs_full = z1T[sl % 2]
                for oc in range(8):
                    pp = pqk[oc % 6]
                    for dc in range(ND):
                        p.mm(pp[:], wqkvb[dc][:, oc * 128:(oc + 1) * 128],
                             zt[:, dc, :, :],
                             start=(dc == 0), stop=(dc == ND - 1))
                    p.op("scalar", lambda s, pp=pp, oc=oc, sl=sl: s.activation(
                        out=qkT[oc][:, sl * 512:(sl + 1) * 512], in_=pp[:],
                        func=AF.Identity, bias=b0qk[:, oc:oc + 1]),
                        ins=[pp, b0qk], outs=[qkT[oc]])
                for ci in range(4):
                    c = sl * 4 + ci
                    pp = pqk[(8 + ci) % 6]
                    for dc in range(ND):
                        p.mm(pp[:], zt[:, dc, ci, :], wqkvb[dc][:, 1024:1536],
                             start=(dc == 0), stop=False)
                    p.mm(pp[:], e0b[:], b0v_row[:], start=False, stop=True)
                    p.op("vector", lambda v, pp=pp, c=c: v.tensor_copy(
                        out=vtok[c][:].rearrange("p (h e) -> p h e", e=65)[:, :, 0:64],
                        in_=pp[:].rearrange("p (h d) -> p h d", d=64)),
                        ins=[pp], outs=[vtok[c]])
                    p.op("vector", lambda v, c=c: v.memset(
                        vtok[c][:].rearrange("p (h e) -> p h e", e=65)[:, :, 64:65], 1.0),
                        ins=[vtok[c]], outs=[vtok[c]])

        if DEBUG:
            o_qk0 = dout("dbg_qk0", [128, N], BF16)
            o_qk4 = dout("dbg_qk4", [128, N], BF16)
            o_vt = dout("dbg_vt", [128, 520], BF16)
            p.dma("sync", "dbg", o_qk0, qkT[0][:])
            p.dma("sync", "dbg", o_qk4, qkT[4][:])
            p.dma("sync", "dbg", o_vt, vtok[3][:])

        p.full_barrier()
        arena2.close()

        print("SBUF remaining at light sims:", nc.sbuf_bytes_remaining)
        arena3 = ExitStack()
        def sb3(name, shape, dtype=F32):
            return arena3.enter_context(nc.sbuf_tensor(name, shape, dtype))
        attn = [sb3(f"attn{i}", [128, H * 384], BF16) for i in range(2)]
        attnT = [sb3(f"attnT{i}", [128, H * 384], BF16) for i in range(2)]
        with ExitStack() as ph:
            psim = [psum(ph, f"psim{i}", [128, 384]) for i in range(4)]
            pav = [psum(ph, f"pav{i}", [64, 1024]) for i in range(2)]
            for c in range(NCT):
                at = attn[c % 2]
                att = attnT[c % 2]
                for h in range(H):
                    pp = psim[(c * H + h) % 4]
                    qoc, qrow = h // 2, (h % 2) * 64
                    lo_dlt, hi_dlt = max(-1, -c), min(1, NCT - 1 - c)
                    if lo_dlt > -1:
                        p.op("vector", lambda v, pp=pp: v.memset(
                            pp[:, 0:128], NEG_BIG), ins=[pp], outs=[pp])
                    if hi_dlt < 1:
                        p.op("vector", lambda v, pp=pp: v.memset(
                            pp[:, 256:384], NEG_BIG), ins=[pp], outs=[pp])
                    p.mm(pp[:, (lo_dlt + 1) * 128:(hi_dlt + 2) * 128],
                         qkT[qoc][qrow:qrow + 64, c * 128:(c + 1) * 128],
                         qkT[4 + qoc][qrow:qrow + 64,
                                      (c + lo_dlt) * 128:(c + hi_dlt + 1) * 128],
                         start=True, stop=True)
                    p.op("vector", lambda v, pp=pp: v.tensor_tensor(
                        out=pp[:], in0=pp[:], in1=lmask[:], op=ALU.add),
                        ins=[pp, lmask], outs=[pp])
                    p.op("scalar", lambda s, pp=pp, at=at, h=h, c=c: s.activation(
                        out=at[:, h * 384:(h + 1) * 384], in_=pp[:], func=AF.Exp,
                        scale=0.125, accum_out=den[:, c * H + h:c * H + h + 1]),
                        ins=[pp], outs=[at, den])
                p.op("vector", lambda v, c=c: v.reciprocal(
                    out=rden[:, c * H:(c + 1) * H], in_=den[:, c * H:(c + 1) * H]),
                    ins=[den], outs=[rden])
                for h in range(H):
                    p.op("vector", lambda v, at=at, h=h, c=c: v.tensor_scalar(
                        out=at[:, h * 384:(h + 1) * 384],
                        in0=at[:, h * 384:(h + 1) * 384],
                        scalar1=rden[:, c * H + h:c * H + h + 1], scalar2=None,
                        op0=ALU.mult), ins=[at, rden], outs=[at])
                if DEBUG and c == 3:
                    p.dma("sync", "dbg", dbg["dbg_attn"], at[:])
                p.dma("sync", "tat", att[:].rearrange("p (b t) -> p b t", b=24),
                      at[:].rearrange("p (b t) -> p b t", b=24), transpose=True)
                if c < 8:
                    w6_amax(c)
                elif c == 8:
                    w6_scale_batch()
                if 8 <= c < 16:
                    w6_quant(c - 8)
                pa = pav[c % 2]
                for h in range(H):
                    first = True
                    for di, dlt in enumerate((-1, 0, 1)):
                        kc = c + dlt
                        vt = vtok[kc] if 0 <= kc < NCT else vtok[NCT]
                        p.mm(pa[:, h * 128:(h + 1) * 128],
                             vt[:, h * 65:h * 65 + 64],
                             att[:, h * 384 + di * 128: h * 384 + (di + 1) * 128],
                             start=first, stop=(di == 2))
                        first = False
                for par in range(2):
                    p.op("vector", lambda v, pa=pa, par=par, c=c: v.tensor_copy(
                        out=avTv[par * 64:(par + 1) * 64, :, c, :],
                        in_=pa[:, :].rearrange("p (h t) -> p h t", t=128)[:, par::2, :]),
                        ins=[pa], outs=[avT])

        if DEBUG:
            o_avT = dout("dbg_avT", [128, 4 * N], BF16)
            p.dma("sync", "dbg", o_avT, avT[:])

        p.full_barrier()
        arena3.close()

        arena4 = ExitStack()
        def sb4(name, shape, dtype=F32):
            return arena4.enter_context(nc.sbuf_tensor(name, shape, dtype))
        wloutb = [sb4(f"wloutb{k}", [128, D], BF16) for k in range(4)]
        for k in range(4):
            p.dma("scalar", "gw3", scr_v[:], wlout_hbm[k * 128:(k + 1) * 128, :])
            p.barrier_dma("vector", "gw3")
            p.op("vector", lambda v, k=k: v.tensor_copy(
                out=wloutb[k][:], in_=scr_v[:]), ins=[scr_v], outs=[wloutb[k]])
        lout = [sb4(f"lout{i}", [128, D], BF16) for i in range(2)]
        with ExitStack() as ph:
            plo = [psum(ph, f"plo{i}", [128, D]) for i in range(2)]
            for c in range(NCT):
                pp = plo[c % 2]
                lo = lout[c % 2]
                for k in range(4):
                    for sl in range(2):
                        p.mm(pp[:, sl * 512:(sl + 1) * 512], avTv[:, k, c, :],
                             wloutb[k][:, sl * 512:(sl + 1) * 512],
                             start=(k == 0), stop=(k == 3))
                p.op("vector", lambda v, pp=pp, lo=lo: v.tensor_copy(
                    out=lo[:], in_=pp[:]), ins=[pp], outs=[lo])
                p.dma("sync", "slight", d_light[c * 128:(c + 1) * 128, :], lo[:])

        if DEBUG:
            o_light = dout("dbg_light", [N, D], BF16)
            p.barrier_dma("sync", "slight")
            lo2 = lout[0]
            for c in range(NCT):
                p.dma("sync", "dbg2", lo2[:], d_light[c * 128:(c + 1) * 128, :])
                p.dma("sync", "dbg", o_light[c * 128:(c + 1) * 128, :], lo2[:])

        p.full_barrier()
        arena4.close()
        light_arena.close()

        print("SBUF remaining before routing:", nc.sbuf_bytes_remaining)
        e0f = sb("e0f_s", [128, 128])
        trie = sb("trie_s", [128, 128], BF16)
        patkv = sb("patkv_s", [1, 32])
        onesf = sb("onesf", [128, 1])
        onesb = sb("onesb", [128, 1], BF16)
        t32a = sb("t32a", [32, 32])
        t32b = sb("t32b", [32, 32])
        p.dma("sync", "gwd", e0f[:], e0f_hbm)
        p.dma("sync", "gwe", trie[:], trie_hbm)
        p.dma("sync", "gwf", patkv[:], patkv_hbm)
        p.op("vector", lambda v: v.memset(onesf[:], 1.0), outs=[onesf])
        p.op("vector", lambda v: v.memset(onesb[:], 1.0), outs=[onesb])
        p.op("vector", lambda v: v.memset(t32a[:], 0.0), outs=[t32a])

        s_cm = sb("s_cm", [128, 32])
        rt1 = sb("rt1", [128, 32]); rt2 = sb("rt2", [128, 32])
        ret = sb("ret", [128, 32]); rred = sb("rred", [128, 4])
        prow = sb("prow", [128, 64])
        rowA = sb("rowA", [1, 2]); rowNA = sb("rowNA", [1, 2])
        rowMB = sb("rowMB", [1, 2]); rowT = sb("rowT", [1, 2])
        rowLN = sb("rowLN", [1, 2]); rowMX = sb("rowMX", [1, 2])
        rowLO = sb("rowLO", [1, 2]); rowHI = sb("rowHI", [1, 2])
        rowMD = sb("rowMD", [1, 2]); rowGE = sb("rowGE", [1, 2])
        rowGEi = sb("rowGEi", [1, 2], I32)
        rowLTi = sb("rowLTi", [1, 2], I32)
        rowQM = sb("rowQM", [1, 2]); rowQZ = sb("rowQZ", [1, 2])
        rowN1 = sb("rowN1", [1, 4])
        cl1 = sb("cl1", [128, 32]); clz = sb("clz", [128, 32])
        clm = sb("clm", [128, 32]); smsk = sb("smsk", [128, 32])
        scx = sb("scx", [128, 32])
        selm = sb("selm", [128, 32]); sel = sb("sel", [128, 32])
        pfx1 = sb("pfx1", [128, 32]); pfxz = sb("pfxz", [128, 32])
        slot = sb("slot", [128, 32])
        maskb = sb("maskb", [128, 32], BF16)
        tfcol = sb("tfcol", [128, 1])
        gidx_i = sb("gidx_i", [128, NCT], I32)
        idxq_i = sb("idxq_i", [128, NCT], I32)
        idxkv_i = sb("idxkv_i", [128, NCT], I32)

        arena_tok = ExitStack()
        tokb = [arena_tok.enter_context(nc.sbuf_tensor(f"tokb{r}", [128, D], F32))
                for r in range(2)]
        p.dma("sync", "gtok0", tokb[0][:], qtok_row.broadcast_to([128, D]))
        p.dma("sync", "gtok1", tokb[1][:], kvtok_row.broadcast_to([128, D]))
        for r in range(2):
            for c in range(NCT):
                p.op("vector", lambda v, r=r, c=c: v.scalar_tensor_tensor(
                    out=scr_v[:], in0=xn[c][:], scalar=1.0, in1=tokb[r][:],
                    op0=ALU.mult, op1=ALU.mult,
                    accum_out=s_cm[:, r * 16 + c:r * 16 + c + 1]),
                    ins=[xn[c], tokb[r]], outs=[scr_v, s_cm])
        p.full_barrier()
        arena_tok.close()

        s_v = s_cm[:].rearrange("p (r c) -> p r c", r=2)

        p.op("vector", lambda v: v.memset(prow[:], 0.0), outs=[prow])
        p.op("vector", lambda v: v.memset(rowA[:], A1), outs=[rowA])

        ffn_arena = ExitStack()
        qx4_box[0] = ffn_arena.enter_context(nc.sbuf_tensor("qx4w", [128, D], F32))
        g_st[0] = ffn_arena.enter_context(nc.sbuf_tensor("g_w4", [128, D], BF16))
        g_st.append(ffn_arena.enter_context(nc.sbuf_tensor("g_w4b", [128, D], BF16)))
        ytile_box[0] = [ffn_arena.enter_context(
            nc.sbuf_tensor(f"ytw4_{i}", [128, D], BF16)) for i in range(2)]
        yT_box[0] = [ffn_arena.enter_context(
            nc.sbuf_tensor(f"yTw4_{i}", [128, D], BF16)) for i in range(2)]
        XW = NCT * 32
        amx = ffn_arena.enter_context(nc.sbuf_tensor("amx", [128, XW], F32))
        qscx = ffn_arena.enter_context(nc.sbuf_tensor("qscx", [128, XW], F32))
        qrsx = ffn_arena.enter_context(nc.sbuf_tensor("qrsx", [128, XW], F32))
        qt1 = ffn_arena.enter_context(nc.sbuf_tensor("qt1", [128, XW], F32))
        qt2 = ffn_arena.enter_context(nc.sbuf_tensor("qt2", [128, XW], F32))
        for c in range(NCT):
            p.op("vector", lambda v, c=c: v.reduce_max(
                 out=amx[:, c * 32:(c + 1) * 32, None],
                 in_=xn[c][:].rearrange("p (b k) -> p b k", b=32),
                 axis=AX.X, apply_absolute_value=True), ins=[xn[c]], outs=[amx])
        p.op("vector", lambda v: v.memset(qt2[:], -10.0), outs=[qt2])
        for k in range(-9, 7):
            p.op("vector", lambda v, k=k: v.scalar_tensor_tensor(out=qt2[:],
                 in0=amx[:], scalar=float(2.0 ** k), in1=qt2[:],
                 op0=ALU.is_ge, op1=ALU.add), ins=[amx, qt2], outs=[qt2])
        p.op("vector", lambda v: v.tensor_scalar(out=qt1[:], in0=qt2[:],
             scalar1=dcol[:, 0:1], scalar2=-2.0, op0=ALU.add, op1=ALU.add),
             ins=[qt2, dcol], outs=[qt1])
        p.op("vector", lambda v: v.tensor_scalar(out=qt2[:], in0=qt1[:],
             scalar1=LN2_HI, scalar2=None, op0=ALU.mult), ins=[qt1], outs=[qt2])
        p.op("vector", lambda v: v.scalar_tensor_tensor(out=qt2[:], in0=qt1[:],
             scalar=LN2_LO, in1=qt2[:], op0=ALU.mult, op1=ALU.add),
             ins=[qt1], outs=[qt2])
        p.op("scalar", lambda sc: sc.activation(out=qscx[:], in_=qt2[:],
             func=AF.Exp), ins=[qt2], outs=[qscx])
        p.op("vector", lambda v: v.reciprocal(out=qrsx[:], in_=qscx[:]),
             ins=[qscx], outs=[qrsx])
        p.op("vector", lambda v: v.tensor_tensor(out=qt1[:], in0=qscx[:],
             in1=qrsx[:], op=ALU.mult), ins=[qscx, qrsx], outs=[qt1])
        p.op("vector", lambda v: v.tensor_scalar(out=qt1[:], in0=qt1[:],
             scalar1=-1.0, scalar2=2.0, op0=ALU.mult, op1=ALU.add),
             ins=[qt1], outs=[qt1])
        p.op("vector", lambda v: v.tensor_tensor(out=qrsx[:], in0=qrsx[:],
             in1=qt1[:], op=ALU.mult), ins=[qt1, qrsx], outs=[qrsx])
        p.op("vector", lambda v: v.tensor_scalar(out=qrsx[:], in0=qrsx[:],
             scalar1=2.0, scalar2=None, op0=ALU.mult), ins=[qrsx], outs=[qrsx])
        p.op("vector", lambda v: v.tensor_scalar(out=qscx[:], in0=qscx[:],
             scalar1=0.5, scalar2=None, op0=ALU.mult), ins=[qscx], outs=[qscx])

        _ffn_pending = list(range(NCT))
        _pump_ctr = [0]

        def pump_ffn(pf):
            _pump_ctr[0] += 1
            if _pump_ctr[0] % 3 != 1:
                return
            if _ffn_pending:
                ffn_chunk(_ffn_pending.pop(0), pf)

        with ExitStack() as ph:
            pbc = psum(ph, "pbc", [128, 64])
            psc = psum(ph, "psc", [1, 4])
            pf = [psum(ph, f"pf{i}", [128, D]) for i in range(2)]
            cur = 4.0 * 0.7
            for it in range(1, 20):
                ic = float(np.float32(1.0 / cur))
                p.op("vector", lambda v: v.tensor_scalar(out=rowNA[:], in0=rowA[:],
                     scalar1=-1.0, scalar2=None, op0=ALU.mult), ins=[rowA], outs=[rowNA])
                p.op("vector", lambda v, ic=ic: v.tensor_scalar(out=rowMB[:], in0=rowNA[:],
                     scalar1=ic, scalar2=None, op0=ALU.mult), ins=[rowNA], outs=[rowMB])
                p.op("vector", lambda v: v.tensor_copy(
                     out=prow[0:1, 0:32].rearrange("p (r c) -> p r c", r=2),
                     in_=rowNA[0:1, :, None].broadcast_to([1, 2, 16])),
                     ins=[rowNA], outs=[prow])
                p.op("vector", lambda v: v.tensor_copy(
                     out=prow[0:1, 32:64].rearrange("p (r c) -> p r c", r=2),
                     in_=rowMB[0:1, :, None].broadcast_to([1, 2, 16])),
                     ins=[rowMB], outs=[prow])
                p.mm(pbc[:], e0f[:], prow[:], start=True, stop=True, fp32=True)
                p.op("vector", lambda v: v.tensor_tensor(out=rt1[:], in0=s_cm[:],
                     in1=pbc[:, 0:32], op=ALU.min), ins=[s_cm, pbc], outs=[rt1])
                p.op("vector", lambda v, ic=ic: v.scalar_tensor_tensor(out=rt2[:],
                     in0=rt1[:], scalar=ic, in1=pbc[:, 32:64], op0=ALU.mult,
                     op1=ALU.subtract), ins=[rt1, pbc], outs=[rt2])
                p.op("scalar", lambda sc: sc.activation(out=ret[:], in_=rt2[:],
                     func=AF.Exp), ins=[rt2], outs=[ret])
                p.op("vector", lambda v: v.reduce_sum(out=rred[:, 0:2, None],
                     in_=ret[:].rearrange("p (r c) -> p r c", r=2), axis=AX.X),
                     ins=[ret], outs=[rred])
                p.mm(psc[:, 0:2], onesf[:], rred[:, 0:2], start=True, stop=True, fp32=True)
                p.op("scalar", lambda sc: sc.activation(out=rowLN[:], in_=psc[:, 0:2],
                     func=AF.Ln), ins=[psc], outs=[rowLN])
                p.op("vector", lambda v: v.tensor_tensor(out=rowT[:], in0=rowLN[:],
                     in1=rowMB[:], op=ALU.add), ins=[rowLN, rowMB], outs=[rowT])
                p.op("vector", lambda v, cur=cur: v.tensor_scalar(out=rowA[:],
                     in0=rowT[:], scalar1=float(np.float32(LOGK)),
                     scalar2=float(np.float32(-cur)), op0=ALU.subtract,
                     op1=ALU.mult), ins=[rowT], outs=[rowA])
                cur = max(cur * 0.7, 0.03)
                pump_ffn(pf)

            p.op("vector", lambda v: v.tensor_copy(
                 out=prow[0:1, 0:32].rearrange("p (r c) -> p r c", r=2),
                 in_=rowA[0:1, :, None].broadcast_to([1, 2, 16])),
                 ins=[rowA], outs=[prow])
            p.op("vector", lambda v: v.memset(rowT[:, 0:1], 2.0 ** -75), outs=[rowT])
            p.op("vector", lambda v: v.tensor_tensor(out=rowT[:, 1:2], in0=rowT[:, 0:1],
                 in1=rowT[:, 0:1], op=ALU.mult), ins=[rowT], outs=[rowT])
            p.op("vector", lambda v: v.tensor_tensor(out=rowT[:, 0:1], in0=rowT[:, 1:2],
                 in1=rowT[:, 1:2], op=ALU.mult), ins=[rowT], outs=[rowT])
            p.op("vector", lambda v: v.tensor_scalar(out=rowLN[:, 0:1], in0=rowT[:, 0:1],
                 scalar1=0.0, scalar2=None, op0=ALU.is_gt), ins=[rowT], outs=[rowLN])
            p.op("vector", lambda v: v.tensor_tensor(out=rowT[:, 1:2], in0=rowT[:, 0:1],
                 in1=rowT[:, 0:1], op=ALU.mult), ins=[rowT], outs=[rowT])
            p.op("vector", lambda v: v.tensor_tensor(out=rowT[:, 1:2], in0=rowT[:, 1:2],
                 in1=rowT[:, 0:1], op=ALU.mult), ins=[rowT], outs=[rowT])
            p.op("vector", lambda v: v.tensor_scalar(out=rowT[:, 1:2], in0=rowT[:, 1:2],
                 scalar1=16.0, scalar2=-1.0, op0=ALU.mult, op1=ALU.add),
                 ins=[rowT], outs=[rowT])
            p.op("vector", lambda v: v.tensor_tensor(out=rowT[:, 1:2], in0=rowT[:, 1:2],
                 in1=rowLN[:, 0:1], op=ALU.mult), ins=[rowT, rowLN], outs=[rowT])
            p.op("vector", lambda v: v.tensor_scalar(out=rowT[:, 1:2], in0=rowT[:, 1:2],
                 scalar1=1.0, scalar2=float(2.0 ** -126), op0=ALU.add, op1=ALU.mult),
                 ins=[rowT], outs=[rowT])
            p.op("vector", lambda v: v.tensor_copy(out=prow[0:1, 32:33],
                 in_=rowT[0:1, 1:2]), ins=[rowT], outs=[prow])
            p.mm(pbc[:], e0f[:], prow[:], start=True, stop=True, fp32=True)
            p.op("vector", lambda v: v.tensor_copy(out=tfcol[:], in_=pbc[:, 32:33]),
                 ins=[pbc], outs=[tfcol])
            p.op("vector", lambda v: v.tensor_tensor(out=rt1[:], in0=s_cm[:],
                 in1=pbc[:, 0:32], op=ALU.add), ins=[s_cm, pbc], outs=[rt1])
            p.op("vector", lambda v: v.tensor_scalar(out=cl1[:], in0=rt1[:],
                 scalar1=0.0, scalar2=None, op0=ALU.is_ge), ins=[rt1], outs=[cl1])
            p.op("vector", lambda v: v.tensor_scalar(out=rt2[:], in0=rt1[:],
                 scalar1=0.0, scalar2=None, op0=ALU.min), ins=[rt1], outs=[rt2])
            p.op("scalar", lambda sc: sc.activation(out=scx[:], in_=rt2[:],
                 func=AF.Exp, scale=float(np.float32(1.0 / 0.03))),
                 ins=[rt2], outs=[scx])
            p.op("vector", lambda v: v.scalar_tensor_tensor(out=scx[:], in0=scx[:],
                 scalar=tfcol[:], in1=scx[:], op0=ALU.is_ge, op1=ALU.mult),
                 ins=[scx, tfcol], outs=[scx])
            p.op("vector", lambda v: v.tensor_scalar(out=clz[:], in0=scx[:],
                 scalar1=0.0, scalar2=None, op0=ALU.is_equal), ins=[scx], outs=[clz])
            p.op("vector", lambda v: v.tensor_scalar(out=clm[:], in0=cl1[:],
                 scalar1=-1.0, scalar2=1.0, op0=ALU.mult, op1=ALU.add),
                 ins=[cl1], outs=[clm])
            p.op("vector", lambda v: v.tensor_tensor(out=clm[:], in0=clm[:],
                 in1=clz[:], op=ALU.subtract), ins=[clm, clz], outs=[clm])
            p.op("vector", lambda v: v.reduce_sum(out=rred[:, 0:2, None],
                 in_=cl1[:].rearrange("p (r c) -> p r c", r=2), axis=AX.X),
                 ins=[cl1], outs=[rred])
            p.op("vector", lambda v: v.reduce_sum(out=rred[:, 2:4, None],
                 in_=clm[:].rearrange("p (r c) -> p r c", r=2), axis=AX.X),
                 ins=[clm], outs=[rred])
            p.mm(psc[:, 0:4], onesf[:], rred[:, 0:4], start=True, stop=True, fp32=True)
            p.op("vector", lambda v: v.tensor_copy(out=rowN1[:], in_=psc[:, 0:4]),
                 ins=[psc], outs=[rowN1])
            p.op("vector", lambda v: v.tensor_scalar(out=rowT[:], in0=rowN1[:, 0:2],
                 scalar1=-1.0, scalar2=1024.0, op0=ALU.mult, op1=ALU.add),
                 ins=[rowN1], outs=[rowT])
            p.op("vector", lambda v: v.tensor_tensor(out=rowQM[:], in0=rowT[:],
                 in1=rowN1[:, 2:4], op=ALU.min), ins=[rowT, rowN1], outs=[rowQM])
            p.op("vector", lambda v: v.tensor_scalar(out=rowQM[:], in0=rowQM[:],
                 scalar1=0.0, scalar2=None, op0=ALU.max), ins=[rowQM], outs=[rowQM])
            p.op("vector", lambda v: v.tensor_tensor(out=rowQZ[:], in0=rowT[:],
                 in1=rowN1[:, 2:4], op=ALU.subtract), ins=[rowT, rowN1], outs=[rowQZ])
            p.op("vector", lambda v: v.tensor_scalar(out=rowQZ[:], in0=rowQZ[:],
                 scalar1=0.0, scalar2=None, op0=ALU.max), ins=[rowQZ], outs=[rowQZ])
            p.op("vector", lambda v: v.tensor_scalar(out=rt2[:], in0=clm[:],
                 scalar1=float(-NEG_BIG), scalar2=NEG_BIG, op0=ALU.mult, op1=ALU.add),
                 ins=[clm], outs=[rt2])
            p.op("vector", lambda v: v.tensor_tensor(out=smsk[:], in0=s_cm[:],
                 in1=clm[:], op=ALU.mult), ins=[s_cm, clm], outs=[smsk])
            p.op("vector", lambda v: v.tensor_tensor(out=smsk[:], in0=smsk[:],
                 in1=rt2[:], op=ALU.add), ins=[smsk, rt2], outs=[smsk])
            p.op("vector", lambda v: v.tensor_scalar(out=rowLO[:], in0=rowA[:],
                 scalar1=-1.0, scalar2=-23.0, op0=ALU.mult, op1=ALU.add),
                 ins=[rowA], outs=[rowLO])
            p.op("vector", lambda v: v.tensor_scalar(out=rowHI[:], in0=rowA[:],
                 scalar1=-1.0, scalar2=1.0, op0=ALU.mult, op1=ALU.add),
                 ins=[rowA], outs=[rowHI])
            for it in range(25):
                p.op("vector", lambda v: v.tensor_tensor(out=rowMD[:], in0=rowLO[:],
                     in1=rowHI[:], op=ALU.add), ins=[rowLO, rowHI], outs=[rowMD])
                p.op("vector", lambda v: v.tensor_scalar(out=rowMD[:], in0=rowMD[:],
                     scalar1=0.5, scalar2=None, op0=ALU.mult), ins=[rowMD], outs=[rowMD])
                p.op("vector", lambda v: v.tensor_copy(
                     out=prow[0:1, 0:32].rearrange("p (r c) -> p r c", r=2),
                     in_=rowMD[0:1, :, None].broadcast_to([1, 2, 16])),
                     ins=[rowMD], outs=[prow])
                p.mm(pbc[:, 0:32], e0f[:], prow[:, 0:32], start=True, stop=True, fp32=True)
                p.op("vector", lambda v: v.tensor_tensor(out=rt1[:], in0=smsk[:],
                     in1=pbc[:, 0:32], op=ALU.is_ge), ins=[smsk, pbc], outs=[rt1])
                p.op("vector", lambda v: v.reduce_sum(out=rred[:, 0:2, None],
                     in_=rt1[:].rearrange("p (r c) -> p r c", r=2), axis=AX.X),
                     ins=[rt1], outs=[rred])
                p.mm(psc[:, 0:2], onesf[:], rred[:, 0:2], start=True, stop=True, fp32=True)
                p.op("vector", lambda v: v.tensor_tensor(out=rowGE[:], in0=psc[:, 0:2],
                     in1=rowQM[:], op=ALU.is_ge), ins=[psc, rowQM], outs=[rowGE])
                p.op("vector", lambda v: v.tensor_copy(out=rowGEi[:], in_=rowGE[:]),
                     ins=[rowGE], outs=[rowGEi])
                p.op("vector", lambda v: v.tensor_scalar(out=rowGE[:], in0=rowGE[:],
                     scalar1=-1.0, scalar2=1.0, op0=ALU.mult, op1=ALU.add),
                     ins=[rowGE], outs=[rowGE])
                p.op("vector", lambda v: v.tensor_copy(out=rowLTi[:], in_=rowGE[:]),
                     ins=[rowGE], outs=[rowLTi])
                p.op("vector", lambda v: v.copy_predicated(out=rowLO[:],
                     mask=rowGEi[:], data=rowMD[:]),
                     ins=[rowGEi, rowMD, rowLO], outs=[rowLO])
                p.op("vector", lambda v: v.copy_predicated(out=rowHI[:],
                     mask=rowLTi[:], data=rowMD[:]),
                     ins=[rowLTi, rowMD, rowHI], outs=[rowHI])
                pump_ffn(pf)
            p.op("vector", lambda v: v.tensor_copy(
                 out=prow[0:1, 0:32].rearrange("p (r c) -> p r c", r=2),
                 in_=rowLO[0:1, :, None].broadcast_to([1, 2, 16])),
                 ins=[rowLO], outs=[prow])
            p.op("vector", lambda v: v.tensor_copy(
                 out=prow[0:1, 32:64].rearrange("p (r c) -> p r c", r=2),
                 in_=rowQZ[0:1, :, None].broadcast_to([1, 2, 16])),
                 ins=[rowQZ], outs=[prow])
            p.mm(pbc[:], e0f[:], prow[:], start=True, stop=True, fp32=True)
            p.op("vector", lambda v: v.tensor_tensor(out=selm[:], in0=smsk[:],
                 in1=pbc[:, 0:32], op=ALU.is_ge), ins=[smsk, pbc], outs=[selm])

            def prefix(mask_sb, out_sb, ppf):
                p.op("vector", lambda v: v.tensor_copy(out=maskb[:], in_=mask_sb[:]),
                     ins=[mask_sb], outs=[maskb])
                p.mm(ppf[0:1, 0:32], onesb[:], maskb[:], start=True, stop=True)
                p.op("vector", lambda v: v.tensor_copy(out=scr_v[0:1, 64:96],
                     in_=ppf[0:1, 0:32]), ins=[ppf], outs=[scr_v])
                p.op("vector", lambda v: v.tensor_tensor_scan(out=scr_v[0:1, 0:32],
                     data0=scr_v[0:1, 64:96], data1=scr_v[0:1, 64:96], initial=0.0,
                     op0=ALU.add, op1=ALU.bypass), ins=[scr_v], outs=[scr_v])
                p.op("vector", lambda v: v.tensor_scalar(out=rowT[:, 0:1],
                     in0=scr_v[0:1, 15:16], scalar1=-1.0, scalar2=None,
                     op0=ALU.mult), ins=[scr_v], outs=[rowT])
                p.op("vector", lambda v: v.scalar_tensor_tensor(out=scr_v[0:1, 32:64],
                     in0=patkv[:], scalar=rowT[:, 0:1], in1=scr_v[0:1, 0:32],
                     op0=ALU.mult, op1=ALU.add), ins=[patkv, rowT, scr_v], outs=[scr_v])
                p.op("vector", lambda v: v.tensor_tensor(out=prow[0:1, 0:32],
                     in0=scr_v[0:1, 32:64], in1=scr_v[0:1, 64:96], op=ALU.subtract),
                     ins=[scr_v], outs=[prow])
                p.mm(ppf[:, 0:32], trie[:], maskb[:], start=True, stop=False)
                p.mm(ppf[:, 0:32], e0f[:], prow[:, 0:32], start=False, stop=True,
                     fp32=True)
                p.op("vector", lambda v: v.tensor_copy(out=out_sb[:], in_=ppf[:, 0:32]),
                     ins=[ppf], outs=[out_sb])

            prefix(cl1, pfx1, pbc)
            p.op("vector", lambda v: v.tensor_scalar(out=rt1[:], in0=pfx1[:],
                 scalar1=1024.0, scalar2=None, op0=ALU.is_lt), ins=[pfx1], outs=[rt1])
            p.op("vector", lambda v: v.tensor_tensor(out=sel[:], in0=cl1[:],
                 in1=rt1[:], op=ALU.mult), ins=[cl1, rt1], outs=[sel])
            p.op("vector", lambda v: v.tensor_tensor(out=sel[:], in0=sel[:],
                 in1=selm[:], op=ALU.add), ins=[sel, selm], outs=[sel])
            prefix(clz, pfxz, pbc)
            p.op("vector", lambda v: v.tensor_copy(
                 out=prow[0:1, 0:32].rearrange("p (r c) -> p r c", r=2),
                 in_=rowQZ[0:1, :, None].broadcast_to([1, 2, 16])),
                 ins=[rowQZ], outs=[prow])
            p.mm(pbc[:, 32:64], e0f[:], prow[:, 0:32], start=True, stop=True, fp32=True)
            p.op("vector", lambda v: v.tensor_tensor(out=rt1[:], in0=pfxz[:],
                 in1=pbc[:, 32:64], op=ALU.is_lt), ins=[pfxz, pbc], outs=[rt1])
            p.op("vector", lambda v: v.tensor_tensor(out=rt1[:], in0=clz[:],
                 in1=rt1[:], op=ALU.mult), ins=[clz, rt1], outs=[rt1])
            p.op("vector", lambda v: v.tensor_tensor(out=sel[:], in0=sel[:],
                 in1=rt1[:], op=ALU.add), ins=[sel, rt1], outs=[sel])
            prefix(sel, slot, pbc)
            def mk_idx(out_i, r, oob):
                p.op("vector", lambda v, r=r: v.tensor_tensor(
                     out=rt1[:, 0:16], in0=slot[:, r * 16:(r + 1) * 16],
                     in1=sel[:, r * 16:(r + 1) * 16], op=ALU.mult),
                     ins=[slot, sel], outs=[rt1])
                p.op("vector", lambda v, r=r, oob=oob: v.tensor_scalar(
                     out=rt2[:, 0:16], in0=sel[:, r * 16:(r + 1) * 16],
                     scalar1=float(-oob), scalar2=float(oob), op0=ALU.mult,
                     op1=ALU.add), ins=[sel], outs=[rt2])
                p.op("vector", lambda v: v.tensor_tensor(out=rt1[:, 0:16],
                     in0=rt1[:, 0:16], in1=rt2[:, 0:16], op=ALU.add),
                     ins=[rt1, rt2], outs=[rt1])
                p.op("vector", lambda v, out_i=out_i: v.tensor_copy(
                     out=out_i[:], in_=rt1[:, 0:16]), ins=[rt1], outs=[out_i])
            mk_idx(idxq_i, 0, 4096)
            mk_idx(idxkv_i, 1, 4096)
            mk_idx(gidx_i, 0, 1024)
            while _ffn_pending:
                ffn_chunk(_ffn_pending.pop(0), pf)

        if DEBUG:
            o_sel = dout("dbg_sel", [128, 32])
            o_arow = dout("dbg_arow", [1, 8])
            p.dma("sync", "dbg", o_sel, sel[:])
            p.op("vector", lambda v: v.tensor_copy(out=scr_v[0:1, 0:2], in_=rowA[:]),
                 ins=[rowA], outs=[scr_v])
            p.op("vector", lambda v: v.tensor_copy(out=scr_v[0:1, 2:4], in_=rowQM[:]),
                 ins=[rowQM], outs=[scr_v])
            p.op("vector", lambda v: v.tensor_copy(out=scr_v[0:1, 4:6], in_=rowQZ[:]),
                 ins=[rowQZ], outs=[scr_v])
            p.op("vector", lambda v: v.tensor_copy(out=scr_v[0:1, 6:8], in_=rowLO[:]),
                 ins=[rowLO], outs=[scr_v])
            p.dma("sync", "dbg", o_arow, scr_v[0:1, 0:8])

        print("SBUF remaining before rope:", nc.sbuf_bytes_remaining)
        p.full_barrier()
        ffn_arena.close()
        rtab_arena = ExitStack()
        invfb = sb("invfb", [128, 64])
        pcol = sb("pcol", [128, 1])
        halfpi_c = sb("halfpi_c", [128, 1])
        p.op("vector", lambda v: v.memset(halfpi_c[:], HALF_PI), outs=[halfpi_c])
        with ExitStack() as ph:
            def sbr(name, shape, dtype=F32):
                return ph.enter_context(nc.sbuf_tensor(name, shape, dtype))
            pioc = sbr("pioc", [128, 1], I32)
            fio = sbr("fio", [1, 64], I32)
            fro = sbr("fro", [1, 64])
            invf = sbr("invf", [1, 64])
            pbc3 = psum(ph, "pbc3", [128, 64])
            p.op("gpsimd", lambda g: g.iota(pioc[:], [[0, 1]],
                 channel_multiplier=1), outs=[pioc])
            p.op("vector", lambda v: v.tensor_copy(out=pcol[:], in_=pioc[:]),
                 ins=[pioc], outs=[pcol])
            p.op("gpsimd", lambda g: g.iota(
                 fio[:].rearrange("p (a b) -> p a b", a=2), [[0, 2], [1, 32]],
                 channel_multiplier=0, allow_small_or_imprecise_dtypes=False),
                 outs=[fio])
            p.op("vector", lambda v: v.tensor_copy(out=fro[:], in_=fio[:]),
                 ins=[fio], outs=[fro])
            p.op("vector", lambda v: v.tensor_scalar(out=fro[:], in0=fro[:],
                 scalar1=float(1.0 / 32.0), scalar2=None, op0=ALU.mult),
                 ins=[fro], outs=[fro])
            p.op("vector", lambda v: v.tensor_scalar(out=invf[:], in0=fro[:],
                 scalar1=L2E4_HI, scalar2=None, op0=ALU.mult), ins=[fro], outs=[invf])
            p.op("vector", lambda v: v.tensor_copy(out=scr_v[0:1, 0:64],
                 in_=invf[:]), ins=[invf], outs=[scr_v])
            p.op("vector", lambda v: v.memset(invf[:], 1.0), ins=[invf], outs=[invf])
            for w in (16, 8, 4, 2, 1):
                p.op("vector", lambda v, w=w: v.tensor_scalar(out=scr_v[0:1, 128:192],
                     in0=scr_v[0:1, 0:64], scalar1=float(w), scalar2=None,
                     op0=ALU.is_ge), ins=[scr_v], outs=[scr_v])
                p.op("vector", lambda v, w=w: v.scalar_tensor_tensor(
                     out=scr_v[0:1, 0:64], in0=scr_v[0:1, 128:192],
                     scalar=float(-w), in1=scr_v[0:1, 0:64], op0=ALU.mult,
                     op1=ALU.add), ins=[scr_v], outs=[scr_v])
                p.op("vector", lambda v, w=w: v.tensor_scalar(out=scr_v[0:1, 128:192],
                     in0=scr_v[0:1, 128:192], scalar1=float(2.0 ** -w - 1.0),
                     scalar2=1.0, op0=ALU.mult, op1=ALU.add),
                     ins=[scr_v], outs=[scr_v])
                p.op("vector", lambda v: v.tensor_tensor(out=invf[:], in0=invf[:],
                     in1=scr_v[0:1, 128:192], op=ALU.mult), ins=[scr_v, invf],
                     outs=[invf])
            p.op("vector", lambda v: v.tensor_copy(out=scr_v[0:1, 64:128],
                 in_=scr_v[0:1, 0:64]), ins=[scr_v], outs=[scr_v])
            p.op("vector", lambda v: v.scalar_tensor_tensor(out=scr_v[0:1, 64:128],
                 in0=fro[:], scalar=L2E4_LO, in1=scr_v[0:1, 64:128],
                 op0=ALU.mult, op1=ALU.add), ins=[fro, scr_v], outs=[scr_v])
            p.op("vector", lambda v: v.scalar_tensor_tensor(out=scr_v[0:1, 64:128],
                 in0=fro[:], scalar=L2E4_LO2, in1=scr_v[0:1, 64:128],
                 op0=ALU.mult, op1=ALU.add), ins=[fro, scr_v], outs=[scr_v])
            p.op("vector", lambda v: v.tensor_scalar(out=scr_v[0:1, 0:64],
                 in0=scr_v[0:1, 64:128], scalar1=-LN2_HI, scalar2=None,
                 op0=ALU.mult), ins=[scr_v], outs=[scr_v])
            p.op("vector", lambda v: v.scalar_tensor_tensor(out=scr_v[0:1, 0:64],
                 in0=scr_v[0:1, 64:128], scalar=-LN2_LO, in1=scr_v[0:1, 0:64],
                 op0=ALU.mult, op1=ALU.add), ins=[scr_v], outs=[scr_v])
            p.op("scalar", lambda sc: sc.activation(out=scr_a[0:1, 0:64],
                 in_=scr_v[0:1, 0:64], func=AF.Exp), ins=[scr_v], outs=[scr_a])
            p.op("vector", lambda v: v.tensor_tensor(out=invf[:], in0=invf[:],
                 in1=scr_a[0:1, 0:64], op=ALU.mult), ins=[scr_a, invf], outs=[invf])
            p.op("vector", lambda v: v.memset(scr_v[:], 0.0), ins=[scr_v], outs=[scr_v])
            p.op("vector", lambda v: v.tensor_copy(out=scr_v[0:1, 0:64], in_=invf[:]),
                 ins=[invf], outs=[scr_v])
            p.mm(pbc3[:], e0f[:], scr_v[:, 0:64], start=True, stop=True, fp32=True)
            p.op("vector", lambda v: v.tensor_copy(out=invfb[:], in_=pbc3[:]),
                 ins=[pbc3], outs=[invfb])

        hs_r = rtab_arena.enter_context(nc.sbuf_tensor("hs_r", [128, NCT], F32))
        xth = [rtab_arena.enter_context(nc.sbuf_tensor(f"xth{i}", [128, D + 388], BF16))
               for i in range(2)]
        for i in range(2):
            p.op("vector", lambda v, i=i: v.memset(xth[i][:, D:D + 4], 0.0),
                 outs=[xth[i]])
        for c in range(NCT):
            p.op("scalar", lambda sc, c=c: sc.activation(out=scr_a[:], in_=xn[c][:],
                 func=AF.Square, accum_out=stat_sq[:, c:c + 1]),
                 ins=[xn[c]], outs=[scr_a, stat_sq])
            p.op("scalar", lambda sc, c=c: sc.activation(out=scr_a[:, 0:1],
                 in_=stat_sq[:, c:c + 1], func=AF.Sqrt), ins=[stat_sq], outs=[scr_a])
            p.op("vector", lambda v, c=c: v.tensor_scalar(out=scr_v[:, 0:1],
                 in0=scr_a[:, 0:1], scalar1=1e-12, scalar2=None, op0=ALU.max),
                 ins=[scr_a], outs=[scr_v])
            p.op("vector", lambda v, c=c: v.reciprocal(out=hs_r[:, c:c + 1],
                 in_=scr_v[:, 0:1]), ins=[scr_v], outs=[hs_r])
            xt = xth[c % 2]
            p.dma("sync", f"gat{c % 2}", xt[:, D + 4:D + 388],
                  atab_hbm[c * 128:(c + 1) * 128, :])
            p.op("vector", lambda v, c=c, xt=xt: v.tensor_scalar(out=xt[:, 0:D],
                 in0=xn[c][:], scalar1=hs_r[:, c:c + 1], scalar2=None,
                 op0=ALU.mult), ins=[xn[c], hs_r], outs=[xt])
            p.op("vector", lambda v, c=c, xt=xt: v.memset(xt[:, D:D + 1], float(c)),
                 ins=[xt], outs=[xt])
            p.op("vector", lambda v, xt=xt: v.tensor_copy(out=xt[:, D + 1:D + 2],
                 in_=pcol[:]), ins=[pcol, xt], outs=[xt])
            p.idma("scat", d_selq, IndirectOffsetOnAxis(ap=idxq_i[:, c:c + 1], axis=0),
                   xt[:], None, bounds_check=NSEL - 1, oob_is_err=False)
            p.idma("scat", d_selkv, IndirectOffsetOnAxis(ap=idxkv_i[:, c:c + 1], axis=0),
                   xt[:], None, bounds_check=NSEL - 1, oob_is_err=False)

        nullb = rtab_arena.enter_context(nc.sbuf_tensor("nullb", [128, D], BF16))
        p.dma("scalar", "gnq", scr_v[0:1, 0:D], nullq_row)
        with ExitStack() as ph:
            pnq = psum(ph, "pnq", [128, D])
            p.mm(pnq[:, 0:512], e0f[:], scr_v[:, 0:512], start=True, stop=True, fp32=True)
            p.mm(pnq[:, 512:1024], e0f[:], scr_v[:, 512:1024], start=True, stop=True, fp32=True)
            p.op("vector", lambda v: v.tensor_copy(out=nullb[:], in_=pnq[:]),
                 ins=[pnq], outs=[nullb])
        for c in range(NCT):
            p.dma("scalar", "pfh", d_heavy2[c * 128:(c + 1) * 128, :], nullb[:])

        p.full_barrier()
        rtab_arena.close()

        print("SBUF remaining before heavy:", nc.sbuf_bytes_remaining)
        arena5 = ExitStack()
        def sb5(name, shape, dtype=F32):
            return arena5.enter_context(nc.sbuf_tensor(name, shape, dtype))
        wqb = [sb5(f"wqb{dc}", [128, 512], BF16) for dc in range(ND)]
        wkvb = [sb5(f"wkvb{dc}", [128, 1024], BF16) for dc in range(ND)]
        hgam_s = sb5("hgam_s", [128, ND])
        wst = scr_v
        p.dma("sync", "gwg", hgam_s[:], hgam_hbm)
        for dc in range(ND):
            p.dma("scalar", "gw4", wst[:, 0:512], wq_hbm[dc * 128:(dc + 1) * 128, :])
            p.barrier_dma("vector", "gw4")
            p.barrier_dma("vector", "gwg")
            p.op("vector", lambda v, dc=dc: v.tensor_scalar(out=wqb[dc][:],
                 in0=wst[:, 0:512], scalar1=hgam_s[:, dc:dc + 1], scalar2=32.0,
                 op0=ALU.mult, op1=ALU.mult), ins=[wst, hgam_s], outs=[wqb[dc]])
            p.dma("scalar", "gw4", wst[:], wkv_hbm[dc * 128:(dc + 1) * 128, :])
            p.barrier_dma("vector", "gw4")
            p.op("vector", lambda v, dc=dc: v.tensor_scalar(out=wkvb[dc][:],
                 in0=wst[:], scalar1=hgam_s[:, dc:dc + 1], scalar2=32.0,
                 op0=ALU.mult, op1=ALU.mult), ins=[wst, hgam_s], outs=[wkvb[dc]])
        nk_s = sb5("nk_s", [64, H])
        nv_s = sb5("nv_s", [1, H * DH])
        kT8 = sb5("kT8", [128, 512], BF16)
        vaug8 = sb5("vaug8", [128, H * 65], BF16)
        pm8 = sb5("pm8_s", [128, 1])
        p.dma("sync", "gwi", nk_s[:], nullk_hbm)
        p.dma("sync", "gwj", nv_s[:], nullv_hbm)
        p.dma("sync", "gwk", pm8[:], pm8_hbm)
        p.barrier_dma("vector", "gwi")
        p.barrier_dma("vector", "gwj")
        p.barrier_dma("vector", "gwk")
        p.op("vector", lambda v: v.memset(kT8[:], 0.0), outs=[kT8])
        for h in range(H):
            hr8 = (h % 2) * 64
            p.op("vector", lambda v, h=h, hr8=hr8: v.tensor_copy(
                 out=kT8[hr8:hr8 + 64, (h // 2) * 128:(h // 2) * 128 + 1],
                 in_=nk_s[:, h:h + 1]), ins=[nk_s], outs=[kT8])
        p.op("vector", lambda v: v.memset(vaug8[:], 0.0), outs=[vaug8])
        p.op("vector", lambda v: v.tensor_copy(
             out=vaug8[0:1, :].rearrange("p (h e) -> p h e", e=65)[:, :, 0:64],
             in_=nv_s[:].rearrange("p (h d) -> p h d", d=64)),
             ins=[nv_s], outs=[vaug8])
        p.op("vector", lambda v: v.memset(
             vaug8[:].rearrange("p (h e) -> p h e", e=65)[:, :, 64:65], 1.0),
             ins=[vaug8], outs=[vaug8])

        tokof_keep = [sb5(f"tokof{i}", [128, 1], I32) for i in range(8)]
        if DEBUG:
            dbg_xsq = dout("dbg_xsq", [256, 512], BF16)
            dbg_roq = dout("dbg_roq", [256, 128])
            dbg_qro = dout("dbg_qro", [256, 512])
            dbg_rtq = dout("dbg_rtq", [256, 512])
            dbg_xsT = dout("dbg_xsT", [256, 1024], BF16)
        qT_g = sb5("qT_g", [128, 8 * 512], BF16)
        kT_g = sb5("kT_g", [128, 8 * 512], BF16)
        vaug = [sb5(f"vaug{j}", [128, H * 65], BF16) for j in range(8)]
        with ExitStack() as ph:
            xsel = [sb5(f"xsel{i}", [128, D + 388], BF16) for i in range(2)]
            xselT = [sb5(f"xselT{i}", [128, D], BF16) for i in range(2)]
            rop = [sb5(f"rop{i}", [128, 128]) for i in range(2)]
            rang = sb5("rang", [128, 64])
            rscr = sb5("rscr", [128, 64])
            rl1 = sb5("rl1", [128, 64])
            rl2 = sb5("rl2", [128, 64])
            posj = sb5("posj", [128, 1])

            rotq = sb5("rotq", [128, 1024])
            qro = sb5("qro", [128, 1024])
            pq = [psum(ph, f"pq{i}", [128, 512]) for i in range(2)]
            pkv = [psum(ph, f"pkv{i}", [128, 1024]) for i in range(2)]

            for side, dsel in ((0, d_selq), (1, d_selkv)):
                for j in range(8):
                    xs = xsel[j % 2]; xsT = xselT[j % 2]; ro = rop[j % 2]
                    p.dma("sync", f"ghx{j % 2}", xs[:], dsel[j * 128:(j + 1) * 128, :])
                    p.barrier_dma("sync", f"ghx{j % 2}")
                    p.dma("sync", f"thx{j % 2}", xsT[:].rearrange("p (dc t) -> p dc t", dc=ND),
                          xs[:, 0:D].rearrange("p (dc t) -> p dc t", dc=ND),
                          transpose=True)
                    p.op("vector", lambda v, xs=xs: v.tensor_copy(
                         out=rscr[:, 0:2], in_=xs[:, D:D + 2]), ins=[xs], outs=[rscr])
                    p.op("vector", lambda v: v.scalar_tensor_tensor(out=posj[:],
                         in0=rscr[:, 0:1], scalar=128.0, in1=rscr[:, 1:2],
                         op0=ALU.mult, op1=ALU.add), ins=[rscr], outs=[posj])
                    if side == 0:
                        p.op("vector", lambda v, j=j: v.tensor_copy(
                             out=tokof_keep[j][:], in_=posj[:]),
                             ins=[posj], outs=[tokof_keep[j]])
                    p.op("vector", lambda v, xs=xs, ro=ro: v.tensor_tensor(
                         out=ro[:], in0=xs[:, D + 4:D + 132],
                         in1=xs[:, D + 132:D + 260], op=ALU.add),
                         ins=[xs], outs=[ro])
                    p.op("vector", lambda v, xs=xs, ro=ro: v.tensor_tensor(
                         out=ro[:], in0=ro[:], in1=xs[:, D + 260:D + 388],
                         op=ALU.add), ins=[xs, ro], outs=[ro])
                    if side == 0:
                        pp = pq[j % 2]
                        for dc in range(ND):
                            p.mm(pp[:], xsT[:].rearrange("p (dc t) -> p dc t", dc=ND)[:, dc, :],
                                 wqb[dc][:], start=(dc == 0), stop=(dc == ND - 1))
                        p.op("vector", lambda v, pp=pp: v.tensor_scalar(
                             out=rotq[:, 0:512].rearrange("p (h d) -> p h d", d=64)[:, :, 0:32],
                             in0=pp[:].rearrange("p (h d) -> p h d", d=64)[:, :, 32:64],
                             scalar1=-1.0, scalar2=None, op0=ALU.mult),
                             ins=[pp], outs=[rotq])
                        p.op("vector", lambda v, pp=pp: v.tensor_copy(
                             out=rotq[:, 0:512].rearrange("p (h d) -> p h d", d=64)[:, :, 32:64],
                             in_=pp[:].rearrange("p (h d) -> p h d", d=64)[:, :, 0:32]),
                             ins=[pp], outs=[rotq])
                        p.op("vector", lambda v, pp=pp, ro=ro: v.tensor_tensor(
                             out=qro[:, 0:512],
                             in0=pp[:].rearrange("p (h d) -> p h d", d=64),
                             in1=ro[:, None, 0:64].broadcast_to([128, 8, 64]),
                             op=ALU.mult), ins=[pp, ro], outs=[qro])
                        p.op("vector", lambda v, ro=ro: v.tensor_tensor(
                             out=rotq[:, 0:512],
                             in0=rotq[:, 0:512].rearrange("p (h d) -> p h d", d=64),
                             in1=ro[:, None, 64:128].broadcast_to([128, 8, 64]),
                             op=ALU.mult), ins=[rotq, ro], outs=[rotq])
                        p.op("vector", lambda v, xs=xs: v.tensor_tensor(
                             out=xs[:, 0:512], in0=qro[:, 0:512], in1=rotq[:, 0:512],
                             op=ALU.add), ins=[qro, rotq], outs=[xs])
                        if DEBUG and j < 2:
                            p.dma("sync", f"dbx{j}a", dbg_xsq[j * 128:(j + 1) * 128, :],
                                  xs[:, 0:512])
                            p.dma("sync", f"dbx{j}b", dbg_roq[j * 128:(j + 1) * 128, :],
                                  ro[:])
                            p.dma("sync", f"dbx{j}c", dbg_qro[j * 128:(j + 1) * 128, :],
                                  qro[:, 0:512])
                            p.dma("sync", f"dbx{j}d", dbg_rtq[j * 128:(j + 1) * 128, :],
                                  rotq[:, 0:512])
                            p.dma("sync", f"dbx{j}e", dbg_xsT[j * 128:(j + 1) * 128, :],
                                  xsT[:])
                            for _g in "abcde":
                                p.barrier_dma("sync", f"dbx{j}{_g}")
                                p.barrier_dma("vector", f"dbx{j}{_g}")
                        p.dma("sync", "thq",
                              qT_g[:, j * 512:(j + 1) * 512].rearrange("p (b t) -> p b t", b=4),
                              xs[:, 0:512].rearrange("p (b t) -> p b t", b=4),
                              transpose=True)
                    else:
                        pp = pkv[j % 2]
                        for dc in range(ND):
                            for sl in range(2):
                                p.mm(pp[:, sl * 512:(sl + 1) * 512],
                                     xsT[:].rearrange("p (dc t) -> p dc t", dc=ND)[:, dc, :],
                                     wkvb[dc][:, sl * 512:(sl + 1) * 512],
                                     start=(dc == 0), stop=(dc == ND - 1))
                        kv_v = lambda t: t.rearrange("p (h d) -> p h d", d=128)
                        p.op("vector", lambda v, pp=pp: v.tensor_scalar(
                             out=kv_v(rotq[:])[:, :, 0:32],
                             in0=kv_v(pp[:])[:, :, 32:64],
                             scalar1=-1.0, scalar2=None, op0=ALU.mult),
                             ins=[pp], outs=[rotq])
                        p.op("vector", lambda v, pp=pp: v.tensor_copy(
                             out=kv_v(rotq[:])[:, :, 32:64],
                             in_=kv_v(pp[:])[:, :, 0:32]), ins=[pp], outs=[rotq])
                        p.op("vector", lambda v, pp=pp, ro=ro: v.tensor_tensor(
                             out=kv_v(qro[:])[:, :, 0:64],
                             in0=kv_v(pp[:])[:, :, 0:64],
                             in1=ro[:, None, 0:64].broadcast_to([128, 8, 64]),
                             op=ALU.mult), ins=[pp, ro], outs=[qro])
                        p.op("vector", lambda v, ro=ro: v.tensor_tensor(
                             out=kv_v(rotq[:])[:, :, 0:64],
                             in0=kv_v(rotq[:])[:, :, 0:64],
                             in1=ro[:, None, 64:128].broadcast_to([128, 8, 64]),
                             op=ALU.mult), ins=[rotq, ro], outs=[rotq])
                        p.op("vector", lambda v, xs=xs: v.tensor_tensor(
                             out=xs[:, 0:512].rearrange("p (h d) -> p h d", d=64),
                             in0=kv_v(qro[:])[:, :, 0:64],
                             in1=kv_v(rotq[:])[:, :, 0:64],
                             op=ALU.add), ins=[qro, rotq], outs=[xs])
                        p.op("vector", lambda v, pp=pp, j=j: v.tensor_copy(
                             out=vaug[j][:].rearrange("p (h e) -> p h e", e=65)[:, :, 0:64],
                             in_=kv_v(pp[:])[:, :, 64:128]), ins=[pp], outs=[vaug[j]])
                        p.op("vector", lambda v, j=j: v.memset(
                             vaug[j][:].rearrange("p (h e) -> p h e", e=65)[:, :, 64:65],
                             1.0), ins=[vaug[j]], outs=[vaug[j]])
                        p.dma("sync", "thkv",
                              kT_g[:, j * 512:(j + 1) * 512].rearrange("p (b t) -> p b t", b=4),
                              xs[:, 0:512].rearrange("p (b t) -> p b t", b=4),
                              transpose=True)

        p.full_barrier()

        avTh = sb5("avTh", [128, 4 * 1024], BF16)
        if DEBUG:
            o_qTg = dout("dbg_qTg", [128, 8 * 512], BF16)
            o_kTg = dout("dbg_kTg", [128, 8 * 512], BF16)
            _unused = (dbg_xsq, dbg_roq)
            p.dma("sync", "dbg", o_qTg, qT_g[:])
            p.dma("sync", "dbg", o_kTg, kT_g[:])
        whoutb = [sb5(f"whoutb{k}", [128, D], BF16) for k in range(4)]
        for k in range(4):
            p.dma("scalar", "gw5", wst[:], whout_hbm[k * 128:(k + 1) * 128, :])
            p.barrier_dma("vector", "gw5")
            p.op("vector", lambda v, k=k: v.tensor_copy(out=whoutb[k][:],
                 in_=wst[:]), ins=[wst], outs=[whoutb[k]])
        att_s = [sb5(f"att_s{i}", [128, 1024], BF16) for i in range(2)]
        avu = sb5("avu", [64, 1024])
        rdst = [scr_a]
        p.op("vector", lambda v: v.memset(rdst[0][:], 0.0), outs=[rdst[0]])
        with ExitStack() as ph:
            psm = [psum(ph, f"psm{i}", [128, 1024]) for i in range(2)]
            pav = psum(ph, "pav", [65, 1024])
            pbc4 = psum(ph, "pbc4", [64, 1024])
            for h in range(H):
                hh, hrow = h // 2, (h % 2) * 64
                for kt in range(9):
                    pp = psm[kt % 2]
                    hp = h // 2
                    lhs = (kT_g[hrow:hrow + 64,
                                kt * 512 + hp * 128: kt * 512 + hp * 128 + 128]
                           if kt < 8 else
                           kT8[hrow:hrow + 64, hp * 128:hp * 128 + 128])
                    for sl in range(2):
                        p.mm(pp[:, sl * 512:(sl + 1) * 512], lhs,
                             qT_g[hrow:hrow + 64, :].rearrange(
                                 "p (j b t) -> p j b t", j=8, b=4)
                                 [:, sl * 4:(sl + 1) * 4, hh, :],
                             start=True, stop=True)
                    if kt == 8:
                        p.op("vector", lambda v, pp=pp: v.tensor_tensor(
                             out=pp[:], in0=pp[:],
                             in1=pm8[:].broadcast_to([128, 1024]), op=ALU.add),
                             ins=[pp, pm8], outs=[pp])
                    at = att_s[kt % 2]
                    p.op("scalar", lambda sc, pp=pp, at=at: sc.activation(
                         out=at[:], in_=pp[:], func=AF.Exp, scale=0.125),
                         ins=[pp], outs=[at])
                    vg = vaug[kt] if kt < 8 else vaug8
                    for sl in range(2):
                        p.mm(pav[:, sl * 512:(sl + 1) * 512],
                             vg[:, h * 65:h * 65 + 65],
                             at[:, sl * 512:(sl + 1) * 512],
                             start=(kt == 0), stop=(kt == 8))
                rd = rdst[0]
                p.op("vector", lambda v: v.reciprocal(out=avu[0:1, :],
                     in_=pav[64:65, :]), ins=[pav], outs=[avu])
                p.op("vector", lambda v, rd=rd: v.tensor_copy(out=rd[0:1, :],
                     in_=avu[0:1, :]), ins=[avu], outs=[rd])
                p.mm(pbc4[:, 0:512], e0f[:, 0:64], rd[:, 0:512], start=True,
                     stop=True, fp32=True)
                p.mm(pbc4[:, 512:1024], e0f[:, 0:64], rd[:, 512:1024], start=True,
                     stop=True, fp32=True)
                p.op("vector", lambda v: v.tensor_copy(out=avu[:], in_=pav[0:64, :]),
                     ins=[pav], outs=[avu])
                p.op("vector", lambda v, h=h, hh=hh, hrow=hrow: v.tensor_tensor(
                     out=avTh[hrow:hrow + 64, hh * 1024:(hh + 1) * 1024],
                     in0=avu[:], in1=pbc4[:], op=ALU.mult),
                     ins=[avu, pbc4], outs=[avTh])

        hvout = [ytile[0], yT[0]]
        with ExitStack() as ph:
            po = [psum(ph, f"po{i}", [128, D]) for i in range(2)]
            for j in range(8):
                pp = po[j % 2]
                for k in range(4):
                    for sl in range(2):
                        p.mm(pp[:, sl * 512:(sl + 1) * 512],
                             avTh[:, k * 1024 + j * 128: k * 1024 + j * 128 + 128],
                             whoutb[k][:, sl * 512:(sl + 1) * 512],
                             start=(k == 0), stop=(k == 3))
                hv = hvout[j % 2]
                p.op("vector", lambda v, pp=pp, hv=hv: v.tensor_copy(
                     out=hv[:], in_=pp[:]), ins=[pp], outs=[hv])
                p.idma("shv", d_heavy2, IndirectOffsetOnAxis(ap=tokof_keep[j][:], axis=0),
                       hv[:], None)

        p.full_barrier()
        arena5.close()

        arena6 = ExitStack()
        def sb6(name, shape, dtype=F32):
            return arena6.enter_context(nc.sbuf_tensor(name, shape, dtype))
        g_t = [sb6(f"g_t{i}", [128, D], BF16) for i in range(2)]
        lt_t = [sb6(f"lt_t{i}", [128, D], BF16) for i in range(2)]
        hv_t = [sb6(f"hv_t{i}", [128, D], BF16) for i in range(2)]
        xx_t = [sb6(f"xx_t{i}", [128, D]) for i in range(2)]
        o_t = [sb6(f"o_t{i}", [128, D]) for i in range(2)]
        rln = sb6("rln", [128, NCT])
        p.op("vector", lambda v: v.reciprocal(out=rln[:], in_=ln_r[:]),
             ins=[ln_r], outs=[rln])
        for c in range(NCT):
            gt = g_t[c % 2]
            lt = lt_t[c % 2]; hv = hv_t[c % 2]; xx = xx_t[c % 2]; ot_ = o_t[c % 2]
            p.dma("gpsimd", f"gg{c % 2}", gt[:], d_g[c * 128:(c + 1) * 128, :])
            p.dma("scalar", f"glt{c % 2}", lt[:], d_light[c * 128:(c + 1) * 128, :])
            p.dma("sync", f"ghv2{c % 2}", hv[:], d_heavy2[c * 128:(c + 1) * 128, :])
            p.barrier_dma("vector", f"gg{c % 2}")
            p.barrier_dma("vector", f"glt{c % 2}")
            p.barrier_dma("vector", f"ghv2{c % 2}")
            p.op("vector", lambda v, c=c, xx=xx: v.tensor_scalar(
                 out=xx[:], in0=xn[c][:], scalar1=rln[:, c:c + 1],
                 scalar2=ln_m[:, c:c + 1], op0=ALU.mult, op1=ALU.add),
                 ins=[xn[c], rln, ln_m], outs=[xx])
            p.op("vector", lambda v, lt=lt, hv=hv, ot_=ot_: v.tensor_tensor(
                 out=ot_[:], in0=lt[:], in1=hv[:], op=ALU.add),
                 ins=[lt, hv], outs=[ot_])
            p.op("vector", lambda v, gt=gt, ot_=ot_: v.tensor_tensor(
                 out=ot_[:], in0=ot_[:], in1=gt[:], op=ALU.mult),
                 ins=[ot_, gt], outs=[ot_])
            p.op("vector", lambda v, xx=xx, ot_=ot_: v.tensor_tensor(
                 out=ot_[:], in0=ot_[:], in1=xx[:], op=ALU.add),
                 ins=[ot_, xx], outs=[ot_])
            p.dma("sync", "sout", out_hbm[c * 128:(c + 1) * 128, :], ot_[:])

        if DEBUG:
            o_hv = dout("dbg_heavy", [N, D], BF16)
            for c in range(NCT):
                p.dma("scalar", "dbgh", lt_t[0][:], d_heavy2[c * 128:(c + 1) * 128, :])
                p.dma("sync", "dbg", o_hv[c * 128:(c + 1) * 128, :], lt_t[0][:])
        arena6.close()
        blk = es.enter_context(nc.Block())
        for grp in list(p.dma_groups):
            p.barrier_dma("sync", grp)
        p.finalize(blk)

    return nc, dbg


def _host_inputs(inp, core):
    m = {}
    m["x"] = inp["L0_attn_x"][core]
    m["wqkv"] = inp["L0_attn_light_qkv_w"]
    m["wlout"] = inp["L0_attn_light_out_w"]
    m["lgam"] = inp["L0_attn_ln_gamma"].reshape(ND, 128).T
    m["lbeta"] = inp["L0_attn_ln_beta"].reshape(ND, 128).T
    m["qtok"] = inp["L0_attn_q_route_tok"].reshape(1, D)
    m["kvtok"] = inp["L0_attn_kv_route_tok"].reshape(1, D)
    m["hgam"] = inp["L0_attn_heavy_gamma"].reshape(ND, 128).T
    m["wq"] = inp["L0_attn_heavy_q_w"]
    m["wkv"] = np.transpose(inp["L0_attn_heavy_kv_w"], (2, 0, 1)).reshape(D, H * 2 * DH)
    m["nullk"] = inp["L0_attn_heavy_null_kv"][0].T
    m["nullv"] = inp["L0_attn_heavy_null_kv"][1].reshape(1, H * DH)
    m["whout"] = inp["L0_attn_heavy_out_w"]
    m["nullq"] = inp["L0_attn_null_q_token"].reshape(1, D)
    m["wffn"] = inp["L0_ffn_weight"]
    m["bffn"] = inp["L0_ffn_bias"].reshape(1, D)
    m["dw"] = inp["L0_ffn_delta_w"]
    m["dact"] = inp["L0_ffn_delta_act"].reshape(1, 1)
    pq = np.arange(128)[:, None]
    mask = np.full((128, 384), NEG_BIG, np.float32)
    for di in range(3):
        j = np.arange(128)[None, :]
        valid = np.abs(128 * (di - 1) + j - pq) <= 64
        mask[:, di * 128:(di + 1) * 128] = np.where(valid, 0.0, NEG_BIG)
    m["lmask"] = mask
    e0 = np.zeros((128, 128), np.float32); e0[0, :] = 1.0
    m["e0f"] = e0
    m["trie"] = (np.arange(128)[:, None] < np.arange(128)[None, :]).astype(ml_dtypes.bfloat16)
    pk = np.zeros((1, 32), np.float32); pk[0, 16:] = 1.0
    m["patkv"] = pk
    pm8 = np.full((128, 1), NEG_BIG, np.float32); pm8[0, 0] = 0.0
    m["pm8"] = pm8
    f4 = np.float32
    j32 = (np.arange(0, 64, 2).astype(f4) / f4(64.0)).astype(f4)
    powv = np.exp(j32.astype(np.float64) * np.log(10000.0)).astype(f4)
    invf32 = (f4(1.0) / powv).astype(f4)
    tab = (np.arange(N).astype(f4)[:, None] * invf32[None]).astype(f4)
    ang = np.concatenate([tab, tab], axis=1)
    try:
        import jax as _jax
        import jax.numpy as _jnp
        _cpu = _jax.devices("cpu")[0]
        with _jax.default_device(_cpu):
            ctab = np.asarray(_jnp.cos(_jnp.asarray(ang)))
            stab = np.asarray(_jnp.sin(_jnp.asarray(ang)))
    except Exception:
        ctab = np.cos(ang.astype(np.float64)).astype(f4)
        stab = np.sin(ang.astype(np.float64)).astype(f4)
    cs = np.concatenate([ctab, stab], axis=1).astype(f4)
    hi = cs.astype(ml_dtypes.bfloat16)
    r1 = (cs - hi.astype(f4)).astype(f4)
    lo = r1.astype(ml_dtypes.bfloat16)
    r2 = (r1 - lo.astype(f4)).astype(f4)
    lo2 = r2.astype(ml_dtypes.bfloat16)
    assert np.array_equal((hi.astype(f4) + lo.astype(f4)) + lo2.astype(f4), cs)
    m["atab"] = np.concatenate([hi, lo, lo2], axis=1)
    return {k: np.ascontiguousarray(v) for k, v in m.items()}


def kernel(**inputs):
    nc, dbg = build_core()
    in_maps = [_host_inputs(inputs, core) for core in range(B)]
    res = run_bass_kernel_spmd(nc, in_maps, core_ids=list(range(B)))
    if DEBUG:
        kernel.debug = [{k: r[k] for k in dbg} for r in res.results]
    return np.stack([r["out"] for r in res.results], axis=0)


# version 15
# speedup vs baseline: 1.0132x; 1.0132x over previous
import math
import os
import traceback
from contextlib import ExitStack

import numpy as np
import ml_dtypes

import concourse.bass as bass
import concourse.mybir as mybir
from concourse.bass import IndirectOffsetOnAxis
from concourse.bass_utils import run_bass_kernel_spmd

F32 = mybir.dt.float32
BF16 = mybir.dt.bfloat16
I32 = mybir.dt.int32
AF = mybir.ActivationFunctionType
ALU = mybir.AluOpType
AX = mybir.AxisListType

B, N, D = 8, 2048, 1024
H, DH = 8, 64
NCT = 16
ND = 8
NSEL = 1024
DEBUG = bool(int(os.environ.get("MK_DEBUG", "0")))

MAGIC = float(np.float32(1.5 * 2.0 ** 23))
NEG_BIG = float(np.float32(-1.0e38))

def _split12(x):
    hi = np.float32(x)
    u = np.frombuffer(np.float32(hi).tobytes(), dtype=np.uint32)[0]
    u &= np.uint32(0xFFFFF000)
    hi = np.frombuffer(np.uint32(u).tobytes(), dtype=np.float32)[0]
    lo = np.float32(np.float64(x) - np.float64(hi))
    lo2 = np.float32(np.float64(x) - np.float64(hi) - np.float64(lo))
    return float(hi), float(lo), float(lo2)

TWO_PI_HI, TWO_PI_LO, TWO_PI_LO2 = _split12(2.0 * math.pi)
LN2_HI, LN2_LO, _ = _split12(math.log(2.0))
L2E4_HI, L2E4_LO, L2E4_LO2 = _split12(math.log2(10000.0))
INV_2PI = float(np.float32(1.0 / (2.0 * math.pi)))
PI = float(np.float32(math.pi))
HALF_PI = float(np.float32(math.pi / 2.0))

CURS = []
_c = 4.0
for _k in range(20):
    CURS.append(_c)
    _c = max(_c * 0.7, 0.03)
LOGK = math.log(1152.0)
A1 = float(np.float32(np.float32(4.0) * (np.float32(LOGK) - np.float32(math.log(2048.0)))))

ENGINES = ("tensor", "vector", "scalar", "gpsimd", "sync")


class Prog:

    def __init__(self, nc, es):
        self.nc = nc
        self.es = es
        self.sem = {e: es.enter_context(nc.semaphore("S" + e[:2])) for e in ENGINES}
        self.cnt = {e: 0 for e in ENGINES}
        self.body = {e: [] for e in ENGINES}
        self.dma_groups = {}
        self.last_w = {}
        self.readers = {}
        self.waited = {e: {} for e in ENGINES}
        self.fp32_mm_pending = False

    def _names(self, aps):
        out = []
        for ap in aps:
            if ap is None:
                continue
            t = ap.tensor if hasattr(ap, "tensor") else ap
            out.append(t.name)
        return out

    def _srckey(self, dep):
        return ("eng", dep[1]) if dep[0] == "eng" else ("dma", dep[1])

    def _collect_waits(self, eng, ins, outs, bump_dma=False):
        need = {}
        def add(dep):
            if dep is None:
                return
            k = self._srckey(dep)
            if k == ("eng", eng) and eng == "tensor":
                return
            if need.get(k, -1) < dep[2]:
                need[k] = dep[2]
        for nm in self._names(ins):
            add(self.last_w.get(nm))
        for nm in self._names(outs):
            add(self.last_w.get(nm))
            for rk, rv in self.readers.get(nm, {}).items():
                add((rk[0], rk[1], rv))
        waits = []
        for (kind, src), v in sorted(need.items(), key=lambda kv: str(kv[0])):
            if kind == "dma" and bump_dma:
                v = self.dma_groups[src][1]
            if self.waited[eng].get(src, -1) >= v:
                continue
            waits.append((self.sem[src] if kind == "eng" else self.dma_groups[src][0], v))
            self.waited[eng][src] = v
        return waits

    def _record(self, ins, outs, dep):
        for nm in self._names(ins):
            self.readers.setdefault(nm, {})[self._srckey(dep)] = dep[2]
        for nm in self._names(outs):
            self.last_w[nm] = dep
            self.readers[nm] = {}

    def op(self, eng, builder, ins=(), outs=()):
        builder._src = traceback.extract_stack()[-3]
        waits = self._collect_waits(eng, ins, outs)
        self.cnt[eng] += 1
        dep = ("eng", eng, self.cnt[eng])
        self._record(ins, outs, dep)
        self.body[eng].append((waits, builder, self.sem[eng], 1))

    def mm(self, out_ap, lhsT_ap, rhs_ap, start, stop, fp32=False):
        small_k = lhsT_ap.shape[0] < 128
        if small_k and self.fp32_mm_pending:
            v = self.cnt["tensor"]
            if self.waited["tensor"].get("tensor", -1) < v:
                self.body["tensor"].append(([(self.sem["tensor"], v)], None, None, 0))
                self.waited["tensor"]["tensor"] = v
            self.fp32_mm_pending = False
        if fp32:
            self.fp32_mm_pending = True
        self.op("tensor",
                lambda t, o=out_ap, l=lhsT_ap, r=rhs_ap, s0=start, s1=stop:
                t.matmul(o, l, r, start=s0, stop=s1),
                ins=[lhsT_ap, rhs_ap], outs=[out_ap])

    def transpose_pe(self, out_ap, in_ap, ident_ap):
        self.op("tensor", lambda t, o=out_ap, i=in_ap, d=ident_ap: t.transpose(o, i, d),
                ins=[in_ap, ident_ap], outs=[out_ap])

    def _grp(self, group):
        if group not in self.dma_groups:
            self.dma_groups[group] = [
                self.es.enter_context(self.nc.semaphore("D" + group[:6])), 0]
        return self.dma_groups[group]

    def dma(self, queue, group, out_ap, in_ap, **kw):
        g = self._grp(group)
        waits = self._collect_waits(queue, [in_ap], [out_ap], bump_dma=True)
        g[1] += 16
        self._record([in_ap], [out_ap], ("dma", group, g[1]))
        def b(eng, _o=out_ap, _i=in_ap, _kw=kw):
            return eng.dma_start(out=_o, in_=_i, **_kw)
        b._src = traceback.extract_stack()[-2]
        self.body[queue].append((waits, b, g[0], 16))

    def idma(self, group, out_ap, out_off, in_ap, in_off, **kw):
        g = self._grp(group)
        ins = [in_ap] + ([out_off.ap] if out_off is not None else []) + \
              ([in_off.ap] if in_off is not None else [])
        waits = self._collect_waits("gpsimd", ins, [out_ap], bump_dma=True)
        g[1] += 16
        self._record(ins, [out_ap], ("dma", group, g[1]))
        def b(eng, _o=out_ap, _oo=out_off, _i=in_ap, _io=in_off, _kw=kw):
            try:
                return eng.indirect_dma_start(out=_o, out_offset=_oo, in_=_i,
                                              in_offset=_io, **_kw)
            except Exception:
                print("IDMA FAIL out=", _o, "\nin=", _i,
                      "\noo=", None if _oo is None else _oo.ap,
                      "\nio=", None if _io is None else _io.ap)
                raise
        b._src = traceback.extract_stack()[-2]
        self.body["gpsimd"].append((waits, b, g[0], 16))

    def barrier_dma(self, eng, group):
        g = self.dma_groups[group]
        if self.waited[eng].get(group, -1) < g[1]:
            self.body[eng].append(([(g[0], g[1])], None, None, 0))
            self.waited[eng][group] = g[1]

    def full_barrier(self):
        for eng in ENGINES:
            waits = []
            for e2 in ENGINES:
                if e2 != eng and self.waited[eng].get(e2, -1) < self.cnt[e2] \
                        and self.cnt[e2] > 0:
                    waits.append((self.sem[e2], self.cnt[e2]))
                    self.waited[eng][e2] = self.cnt[e2]
            for gname, g in self.dma_groups.items():
                if g[1] > 0 and self.waited[eng].get(gname, -1) < g[1]:
                    waits.append((g[0], g[1]))
                    self.waited[eng][gname] = g[1]
            if waits:
                self.body[eng].append((waits, None, None, 0))

    def finalize(self, block):
        for eng in ENGINES:
            items = self.body[eng]
            def make(seq_items):
                def run(handle):
                    for waits, builder, sem, n_inc in seq_items:
                        for s, v in waits:
                            handle.wait_ge(s, v)
                        if builder is not None:
                            try:
                                inst = builder(handle)
                            except Exception as e:
                                src = getattr(builder, "_src", None)
                                raise RuntimeError(f"emit failed at {src}") from e
                            if sem is not None and inst is not None:
                                inst.then_inc(sem, n_inc)
                return run
            getattr(block, eng)(make(items))


def build_core():
    nc = bass.Bass("TRN2", target_bir_lowering=False)
    es = ExitStack()

    def din(name, shape, dtype=F32):
        return nc.dram_tensor(name, shape, dtype, kind="ExternalInput").ap()

    x_hbm = din("x", [N, D])
    wqkv_hbm = din("wqkv", [D, 1536])
    wlout_hbm = din("wlout", [512, D])
    lgam_hbm = din("lgam", [128, ND])
    lbeta_hbm = din("lbeta", [128, ND])
    qtok_row = din("qtok", [1, D])
    kvtok_row = din("kvtok", [1, D])
    hgam_hbm = din("hgam", [128, ND])
    wq_hbm = din("wq", [D, 512])
    wkv_hbm = din("wkv", [D, 1024])
    nullk_hbm = din("nullk", [DH, H])
    nullv_hbm = din("nullv", [1, H * DH])
    whout_hbm = din("whout", [512, D])
    nullq_row = din("nullq", [1, D])
    wffn_hbm = din("wffn", [D, D])
    bffn_row = din("bffn", [1, D])
    dw_hbm = din("dw", [D, 32])
    dact_hbm = din("dact", [1, 1])
    lmask_hbm = din("lmask", [128, 384])
    e0f_hbm = din("e0f", [128, 128])
    trie_hbm = din("trie", [128, 128], BF16)
    patkv_hbm = din("patkv", [1, 32])
    pm8_hbm = din("pm8", [128, 1])
    atab_hbm = din("atab", [N, 384], BF16)

    out_hbm = nc.dram_tensor("out", [N, D], F32, kind="ExternalOutput").ap()
    dbg = {}
    def dout(name, shape, dtype=F32):
        t = nc.dram_tensor(name, shape, dtype, kind="ExternalOutput").ap()
        dbg[name] = t
        return t

    d_light = nc.dram_tensor("d_light", [N, D], BF16, kind="Internal").ap()
    d_g = nc.dram_tensor("d_g", [N, D], BF16, kind="Internal").ap()
    d_selq = nc.dram_tensor("d_selq", [NSEL, D + 388], BF16, kind="Internal").ap()
    d_selkv = nc.dram_tensor("d_selkv", [NSEL, D + 388], BF16, kind="Internal").ap()
    d_heavy2 = nc.dram_tensor("d_heavy2", [N, D], BF16, kind="Internal").ap()
    if DEBUG:
        dbg["dbg_attn"] = nc.dram_tensor("dbg_attn", [128, H * 384], BF16,
                                         kind="ExternalOutput").ap()
        dbg["dbg_g"] = nc.dram_tensor("dbg_g", [N, D], BF16,
                                      kind="ExternalOutput").ap()

    with es:
        def sb(name, shape, dtype=F32):
            return es.enter_context(nc.sbuf_tensor(name, shape, dtype))
        def psum(scope, name, shape, dtype=F32):
            return scope.enter_context(nc.psum_tensor(name, shape, dtype))

        p = Prog(nc, es)

        xn = [sb(f"xn{c}", [128, D]) for c in range(NCT)]
        scr_v = sb("scr_v", [128, D])
        scr_a = sb("scr_a", [128, D])
        stat_sum = sb("stat_sum", [128, NCT])
        stat_sq = sb("stat_sq", [128, NCT])
        ln_m = sb("ln_m", [128, NCT])
        ln_r = sb("ln_r", [128, NCT])
        lmask = sb("lmask_s", [128, 384])
        p.dma("sync", "gwa", lmask[:], lmask_hbm)

        def ln_stats(src_tile, c, sumcol, sqcol):
            p.op("vector", lambda v, t=src_tile, c=c: v.tensor_scalar(
                out=scr_v[:], in0=t[:], scalar1=1.0, scalar2=None,
                op0=ALU.mult, op1=ALU.add, accum_out=sumcol),
                ins=[src_tile], outs=[scr_v, sumcol])
            p.op("scalar", lambda s, t=src_tile, c=c: s.activation(
                out=scr_a[:], in_=t[:], func=AF.Square, accum_out=sqcol),
                ins=[src_tile], outs=[scr_a, sqcol])

        def ln_finish(c, sumcol, sqcol, mcol, rcol, eps=1e-5):
            p.op("vector", lambda v: v.tensor_scalar(
                out=mcol, in0=sumcol, scalar1=float(1.0 / D), scalar2=None,
                op0=ALU.mult), ins=[sumcol], outs=[mcol])
            p.op("vector", lambda v: v.scalar_tensor_tensor(
                out=scr_v[:, 0:1], in0=mcol, scalar=-1.0, in1=mcol,
                op0=ALU.mult, op1=ALU.mult), ins=[mcol], outs=[scr_v])
            p.op("vector", lambda v: v.tensor_scalar(
                out=scr_v[:, 1:2], in0=sqcol, scalar1=float(1.0 / D),
                scalar2=float(eps), op0=ALU.mult, op1=ALU.add),
                ins=[sqcol], outs=[scr_v])
            p.op("vector", lambda v: v.tensor_tensor(
                out=scr_v[:, 2:3], in0=scr_v[:, 1:2], in1=scr_v[:, 0:1],
                op=ALU.add), ins=[scr_v], outs=[scr_v])
            p.op("scalar", lambda s: s.activation(
                out=scr_a[:, 0:1], in_=scr_v[:, 2:3], func=AF.Sqrt),
                ins=[scr_v], outs=[scr_a])
            p.op("vector", lambda v: v.reciprocal(out=rcol, in_=scr_a[:, 0:1]),
                ins=[scr_a], outs=[rcol])

        l_m = sb("l_m", [128, NCT])
        l_r = sb("l_r", [128, NCT])
        lgam_s = sb("lgam_s", [128, ND])
        lbeta_s = sb("lbeta_s", [128, ND])
        b0qk = sb("b0qk", [128, 8])
        e0b = sb("e0b", [128, 128], BF16)
        pcol = sb("pcol", [128, 1])

        sbq = sb
        wqT_g = sbq("wqT_g", [128, 8 * 1024], BF16)
        bfrow = sbq("bfrow", [128, 1024], BF16)
        dcol = sbq("dcol", [128, 1])
        qx1 = sbq("qx1", [128, D]); qx2 = sbq("qx2", [128, D])
        dwt = qx2
        qx4_box = [scr_v]
        wqst = scr_a
        ytile = [sbq("ytile0", [128, D], BF16)]
        yT = [sbq("yT0", [128, D], BF16)]
        ytile_box = [ytile]; yT_box = [yT]
        g_st = [scr_a]

        p.op("vector", lambda v: v.memset(bfrow[:], 0.0), outs=[bfrow])
        p.dma("scalar", "gw6b", scr_v[0:1, 0:D], bffn_row)
        p.barrier_dma("vector", "gw6b")
        p.op("vector", lambda v: v.tensor_copy(out=bfrow[0:1, :], in_=scr_v[0:1, 0:D]),
             ins=[scr_v], outs=[bfrow])
        p.dma("scalar", "gw6c", dcol[:], dact_hbm.broadcast_to([128, 1]))
        p.barrier_dma("scalar", "gw6c")
        p.barrier_dma("vector", "gw6c")
        p.op("scalar", lambda sc: sc.activation(out=dcol[:], in_=dcol[:],
             func=AF.Tanh, scale=4.0), ins=[dcol], outs=[dcol])
        p.op("vector", lambda v: v.tensor_scalar(out=dcol[:], in0=dcol[:],
             scalar1=0.5, scalar2=None, op0=ALU.mult), ins=[dcol], outs=[dcol])

        def quant_pipe(src_ap, dst_tile, src_names, qsc_ap=None, qrs_ap=None,
                       qnames=()):
            bc = lambda t: t[:, :, None].broadcast_to([128, 32, 32])
            p.op("vector", lambda v, src_ap=src_ap, qrs_ap=qrs_ap: v.tensor_tensor(
                 out=qx1[:].rearrange("p (b k) -> p b k", b=32), in0=src_ap,
                 in1=bc(qrs_ap), op=ALU.mult),
                 ins=list(src_names) + list(qnames), outs=[qx1])
            p.op("scalar", lambda sc: sc.activation(out=qx2[:], in_=qx1[:],
                 func=AF.Abs), ins=[qx1], outs=[qx2])
            qx4 = qx4_box[0]
            p.op("vector", lambda v, qx4=qx4: v.tensor_scalar(out=qx4[:], in0=qx2[:],
                 scalar1=0.5, scalar2=None, op0=ALU.is_gt), ins=[qx2], outs=[qx4])
            for thr, cmp in ((1.5, ALU.is_ge), (2.5, ALU.is_gt), (3.5, ALU.is_ge),
                             (5.0, ALU.is_gt), (5.0, ALU.is_gt),
                             (7.0, ALU.is_ge), (7.0, ALU.is_ge),
                             (10.0, ALU.is_gt), (10.0, ALU.is_gt),
                             (10.0, ALU.is_gt), (10.0, ALU.is_gt)):
                p.op("vector", lambda v, thr=thr, cmp=cmp, qx4=qx4: v.scalar_tensor_tensor(
                     out=qx4[:], in0=qx2[:], scalar=thr, in1=qx4[:],
                     op0=cmp, op1=ALU.add), ins=[qx2, qx4], outs=[qx4])
            p.op("scalar", lambda sc: sc.activation(out=qx2[:], in_=qx1[:],
                 func=AF.Sign), ins=[qx1], outs=[qx2])
            p.op("vector", lambda v, qx4=qx4: v.tensor_tensor(out=qx2[:], in0=qx2[:],
                 in1=qx4[:], op=ALU.mult), ins=[qx2, qx4], outs=[qx2])
            p.op("vector", lambda v, dst_tile=dst_tile, qsc_ap=qsc_ap: v.tensor_tensor(
                 out=dst_tile[:].rearrange("p (b k) -> p b k", b=32),
                 in0=qx2[:].rearrange("p (b k) -> p b k", b=32),
                 in1=bc(qsc_ap), op=ALU.mult),
                 ins=[qx2] + list(qnames), outs=[dst_tile])

        amw = qx1
        qscw = sbq("qscw", [128, 256])
        qrsw = sbq("qrsw", [128, 256])

        def w6_amax(ot):
            p.dma("scalar", "gw6", wqst[:], wffn_hbm[ot * 128:(ot + 1) * 128, :])
            p.dma("scalar", "gw6d", dwt[:, 0:32], dw_hbm[ot * 128:(ot + 1) * 128, :])
            p.barrier_dma("vector", "gw6")
            p.barrier_dma("scalar", "gw6d")
            p.op("vector", lambda v, ot=ot: v.reduce_max(
                 out=amw[:, ot * 32:(ot + 1) * 32, None],
                 in_=wqst[:].rearrange("p (b k) -> p b k", b=32),
                 axis=AX.X, apply_absolute_value=True), ins=[wqst], outs=[amw])
            p.op("scalar", lambda sc, ot=ot: sc.activation(
                 out=qscw[:, ot * 32:(ot + 1) * 32], in_=dwt[:, 0:32],
                 func=AF.Tanh, scale=4.0), ins=[dwt], outs=[qscw])

        def w6_scale_batch():
            p.op("vector", lambda v: v.tensor_scalar(out=qrsw[:], in0=qscw[:],
                 scalar1=0.5, scalar2=None, op0=ALU.mult), ins=[qscw], outs=[qrsw])
            p.op("vector", lambda v: v.memset(qscw[:], -10.0), outs=[qscw])
            for k in range(-9, 7):
                p.op("vector", lambda v, k=k: v.scalar_tensor_tensor(out=qscw[:],
                     in0=amw[:, 0:256], scalar=float(2.0 ** k), in1=qscw[:],
                     op0=ALU.is_ge, op1=ALU.add), ins=[amw, qscw], outs=[qscw])
            p.op("vector", lambda v: v.tensor_tensor(out=qrsw[:], in0=qscw[:],
                 in1=qrsw[:], op=ALU.add), ins=[qscw, qrsw], outs=[qrsw])
            p.op("vector", lambda v: v.tensor_scalar(out=qrsw[:], in0=qrsw[:],
                 scalar1=-2.0, scalar2=None, op0=ALU.add), ins=[qrsw], outs=[qrsw])
            p.op("vector", lambda v: v.tensor_scalar(out=amw[:, 0:256], in0=qrsw[:],
                 scalar1=LN2_HI, scalar2=None, op0=ALU.mult), ins=[qrsw], outs=[amw])
            p.op("vector", lambda v: v.scalar_tensor_tensor(out=amw[:, 0:256], in0=qrsw[:],
                 scalar=LN2_LO, in1=amw[:, 0:256], op0=ALU.mult, op1=ALU.add),
                 ins=[qrsw], outs=[amw])
            p.op("scalar", lambda sc: sc.activation(out=qscw[:], in_=amw[:, 0:256],
                 func=AF.Exp), ins=[amw], outs=[qscw])
            p.op("vector", lambda v: v.reciprocal(out=qrsw[:], in_=qscw[:]),
                 ins=[qscw], outs=[qrsw])
            p.op("vector", lambda v: v.tensor_tensor(out=amw[:, 0:256], in0=qscw[:],
                 in1=qrsw[:], op=ALU.mult), ins=[qscw, qrsw], outs=[amw])
            p.op("vector", lambda v: v.tensor_scalar(out=amw[:, 0:256], in0=amw[:, 0:256],
                 scalar1=-1.0, scalar2=2.0, op0=ALU.mult, op1=ALU.add),
                 ins=[amw], outs=[amw])
            p.op("vector", lambda v: v.tensor_tensor(out=qrsw[:], in0=qrsw[:],
                 in1=amw[:, 0:256], op=ALU.mult), ins=[amw, qrsw], outs=[qrsw])
            p.op("vector", lambda v: v.tensor_scalar(out=qrsw[:], in0=qrsw[:],
                 scalar1=2.0, scalar2=None, op0=ALU.mult), ins=[qrsw], outs=[qrsw])
            p.op("vector", lambda v: v.tensor_scalar(out=qscw[:], in0=qscw[:],
                 scalar1=0.5, scalar2=None, op0=ALU.mult), ins=[qscw], outs=[qscw])

        def w6_quant(ot):
            p.dma("scalar", "gw6", wqst[:], wffn_hbm[ot * 128:(ot + 1) * 128, :])
            p.barrier_dma("vector", "gw6")
            yt = ytile_box[0][ot % len(ytile_box[0])]
            quant_pipe(wqst[:].rearrange("p (b k) -> p b k", b=32), yt, [wqst],
                       qsc_ap=qscw[:, ot * 32:(ot + 1) * 32],
                       qrs_ap=qrsw[:, ot * 32:(ot + 1) * 32], qnames=(qscw, qrsw))
            p.dma("sync", "twq",
                  wqT_g[:, ot * 1024:(ot + 1) * 1024].rearrange("p (b t) -> p b t", b=8),
                  yt[:].rearrange("p (b t) -> p b t", b=8), transpose=True)

        def ffn_chunk(c, pf):
            yt = ytile_box[0][c % len(ytile_box[0])]
            quant_pipe(xn[c][:].rearrange("p (b k) -> p b k", b=32), yt, [xn[c]],
                       qsc_ap=qscx[:, c * 32:(c + 1) * 32],
                       qrs_ap=qrsx[:, c * 32:(c + 1) * 32], qnames=(qscx, qrsx))
            ytw = yT_box[0][c % len(yT_box[0])]
            p.dma("sync", f"tyq{c % 2}", ytw[:].rearrange("p (b t) -> p b t", b=8),
                  yt[:].rearrange("p (b t) -> p b t", b=8), transpose=True)
            ytr = ytw
            pp = pf[c % 2]
            for dc in range(ND):
                for sl in range(2):
                    p.mm(pp[:, sl * 512:(sl + 1) * 512],
                         ytr[:, dc * 128:(dc + 1) * 128],
                         wqT_g[:, :].rearrange("p (o b t) -> p o b t", o=8, b=8)
                             [:, sl * 4:(sl + 1) * 4, dc, :],
                         start=(dc == 0), stop=False)
            for sl in range(2):
                p.mm(pp[:, sl * 512:(sl + 1) * 512], e0b[:],
                     bfrow[:, sl * 512:(sl + 1) * 512], start=False, stop=True)
            gt = g_st[c % len(g_st)] if len(g_st) > 1 else g_st[0]
            p.op("scalar", lambda sc, pp=pp, gt=gt: sc.activation(
                 out=gt[:], in_=pp[:], func=AF.Silu), ins=[pp], outs=[gt])
            if DEBUG:
                p.dma("sync", "dbg", dbg["dbg_g"][c * 128:(c + 1) * 128, :], gt[:])
            p.dma("sync", f"sg{c % 2}", d_g[c * 128:(c + 1) * 128, :], gt[:])

        light_arena = ExitStack()
        def sbl(name, shape, dtype=F32):
            return light_arena.enter_context(nc.sbuf_tensor(name, shape, dtype))
        vtok = [sbl(f"vtok{c}", [128, 520], BF16) for c in range(NCT + 1)]
        qkT = [sbl(f"qkT{oc}", [128, N], BF16) for oc in range(8)]
        avT = sbl("avT", [128, 4 * N], BF16)
        avTv = avT[:].rearrange("p (k c t) -> p k c t", k=4, c=NCT)
        den = sbl("den", [128, H * NCT])
        rden = sbl("rden", [128, H * NCT])

        arena1 = ExitStack()
        xtile = [arena1.enter_context(nc.sbuf_tensor(f"xt{i}", [128, D], F32))
                 for i in range(2)]
        for c in range(NCT):
            xt = xtile[c % 2]
            p.dma("sync", f"gx{c % 2}", xt[:], x_hbm[c * 128:(c + 1) * 128, :])
            p.barrier_dma("vector", f"gx{c % 2}")
            p.barrier_dma("scalar", f"gx{c % 2}")
            ln_stats(xt, c, stat_sum[:, c:c + 1], stat_sq[:, c:c + 1])
            ln_finish(c, stat_sum[:, c:c + 1], stat_sq[:, c:c + 1],
                      ln_m[:, c:c + 1], ln_r[:, c:c + 1])
            p.op("vector", lambda v, xt=xt, c=c: v.tensor_scalar(
                out=xn[c][:], in0=xt[:], scalar1=ln_m[:, c:c + 1],
                scalar2=ln_r[:, c:c + 1], op0=ALU.subtract, op1=ALU.mult),
                ins=[xt, ln_m, ln_r], outs=[xn[c]])

        if DEBUG:
            o_xn = dout("dbg_xn", [N, D])
            for c in range(NCT):
                p.dma("sync", "dbg", o_xn[c * 128:(c + 1) * 128, :], xn[c][:])

        p.full_barrier()
        arena1.close()

        for c in range(NCT):
            ln_stats(xn[c], c, stat_sum[:, c:c + 1], stat_sq[:, c:c + 1])
            ln_finish(c, stat_sum[:, c:c + 1], stat_sq[:, c:c + 1],
                      l_m[:, c:c + 1], l_r[:, c:c + 1])

        p.dma("sync", "gwb", lgam_s[:], lgam_hbm)
        p.dma("sync", "gwc", lbeta_s[:], lbeta_hbm)
        arena2 = ExitStack()
        def sb2(name, shape, dtype=F32):
            return arena2.enter_context(nc.sbuf_tensor(name, shape, dtype))
        wqkvb = [sb2(f"wqkvb{dc}", [128, 1536], BF16) for dc in range(ND)]
        for dc in range(ND):
            for lo, hi in ((0, 1024), (1024, 1536)):
                p.dma("scalar", f"gw2{dc % 2}", scr_v[:, 0:hi - lo],
                      wqkv_hbm[dc * 128:(dc + 1) * 128, lo:hi])
                p.barrier_dma("vector", f"gw2{dc % 2}")
                p.barrier_dma("vector", "gwb")
                p.barrier_dma("vector", "gwc")
                p.op("vector", lambda v, dc=dc, lo=lo, hi=hi: v.tensor_scalar(
                    out=wqkvb[dc][:, lo:hi], in0=scr_v[:, 0:hi - lo],
                    scalar1=lgam_s[:, dc:dc + 1],
                    scalar2=None, op0=ALU.mult),
                    ins=[scr_v, lgam_s], outs=[wqkvb[dc]])

        z1T = [sb2("z1T0", [128, ND * 4 * 128], BF16)]
        z1Tv = [t[:].rearrange("p (c dc t) -> p dc c t", dc=ND, c=4) for t in z1T]
        z1T = z1T * 2; z1Tv = z1Tv * 2
        z1b = [sb2("z1b0", [128, D], BF16)]
        z1b = z1b * 2
        def emit_z1(c):
            sl, ci = c // 4, c % 4
            zb = z1b[c % 2]
            p.op("vector", lambda v, c=c, zb=zb: v.tensor_scalar(
                out=zb[:], in0=xn[c][:], scalar1=l_m[:, c:c + 1],
                scalar2=l_r[:, c:c + 1], op0=ALU.subtract, op1=ALU.mult),
                ins=[xn[c], l_m, l_r], outs=[zb])
            p.dma("sync", "tz", z1T[sl % 2][:, ci * 1024:(ci + 1) * 1024]
                      .rearrange("p (dc t) -> p dc t", dc=ND),
                  zb[:].rearrange("p (dc t) -> p dc t", dc=ND), transpose=True)

        lbeta_b = sb2("lbeta_b", [128, ND], BF16)
        p.op("vector", lambda v: v.reciprocal(out=scr_v[:, 0:ND], in_=lgam_s[:]),
             ins=[lgam_s], outs=[scr_v])
        p.op("vector", lambda v: v.tensor_tensor(
            out=lbeta_b[:], in0=lbeta_s[:], in1=scr_v[:, 0:ND], op=ALU.mult),
            ins=[lbeta_s, scr_v], outs=[lbeta_b])
        b0v_row = sb2("b0v_row", [128, 512], BF16)
        with ExitStack() as ph:
            pb = psum(ph, "pb0", [128, 12])
            pbv = psum(ph, "pb0v", [1, 512])
            for j in range(12):
                for dc in range(ND):
                    p.mm(pb[:, j:j + 1], wqkvb[dc][:, j * 128:(j + 1) * 128],
                         lbeta_b[:, dc:dc + 1], start=(dc == 0),
                         stop=(dc == ND - 1))
            for dc in range(ND):
                p.mm(pbv[:], lbeta_b[:, dc:dc + 1],
                     wqkvb[dc][:, 1024:1536], start=(dc == 0), stop=(dc == ND - 1))
            p.op("vector", lambda v: v.tensor_copy(out=b0qk[:], in_=pb[:, 0:8]),
                 ins=[pb], outs=[b0qk])
            p.op("vector", lambda v: v.memset(b0v_row[:], 0.0), outs=[b0v_row])
            p.op("vector", lambda v: v.tensor_copy(out=b0v_row[0:1, :], in_=pbv[:]),
                 ins=[pbv], outs=[b0v_row])

        p.op("vector", lambda v: v.memset(e0b[:], 0.0), outs=[e0b])
        p.op("vector", lambda v: v.memset(e0b[0:1, :], 1.0), ins=[e0b], outs=[e0b])

        p.op("vector", lambda v: v.memset(vtok[NCT][:], 0.0), outs=[vtok[NCT]])
        with ExitStack() as ph:
            pqk = [psum(ph, f"pqk{i}", [128, 512]) for i in range(6)]
            for sl in range(4):
                for ci in range(4):
                    emit_z1(sl * 4 + ci)
                zt = z1Tv[sl % 2]
                rhs_full = z1T[sl % 2]
                for oc in range(8):
                    pp = pqk[oc % 6]
                    for dc in range(ND):
                        p.mm(pp[:], wqkvb[dc][:, oc * 128:(oc + 1) * 128],
                             zt[:, dc, :, :],
                             start=(dc == 0), stop=(dc == ND - 1))
                    p.op("scalar", lambda s, pp=pp, oc=oc, sl=sl: s.activation(
                        out=qkT[oc][:, sl * 512:(sl + 1) * 512], in_=pp[:],
                        func=AF.Identity, bias=b0qk[:, oc:oc + 1]),
                        ins=[pp, b0qk], outs=[qkT[oc]])
                for ci in range(4):
                    c = sl * 4 + ci
                    pp = pqk[(8 + ci) % 6]
                    for dc in range(ND):
                        p.mm(pp[:], zt[:, dc, ci, :], wqkvb[dc][:, 1024:1536],
                             start=(dc == 0), stop=False)
                    p.mm(pp[:], e0b[:], b0v_row[:], start=False, stop=True)
                    p.op("vector", lambda v, pp=pp, c=c: v.tensor_copy(
                        out=vtok[c][:].rearrange("p (h e) -> p h e", e=65)[:, :, 0:64],
                        in_=pp[:].rearrange("p (h d) -> p h d", d=64)),
                        ins=[pp], outs=[vtok[c]])
                    p.op("vector", lambda v, c=c: v.memset(
                        vtok[c][:].rearrange("p (h e) -> p h e", e=65)[:, :, 64:65], 1.0),
                        ins=[vtok[c]], outs=[vtok[c]])

        if DEBUG:
            o_qk0 = dout("dbg_qk0", [128, N], BF16)
            o_qk4 = dout("dbg_qk4", [128, N], BF16)
            o_vt = dout("dbg_vt", [128, 520], BF16)
            p.dma("sync", "dbg", o_qk0, qkT[0][:])
            p.dma("sync", "dbg", o_qk4, qkT[4][:])
            p.dma("sync", "dbg", o_vt, vtok[3][:])

        p.full_barrier()
        arena2.close()

        print("SBUF remaining at light sims:", nc.sbuf_bytes_remaining)
        arena3 = ExitStack()
        def sb3(name, shape, dtype=F32):
            return arena3.enter_context(nc.sbuf_tensor(name, shape, dtype))
        attn = [sb3(f"attn{i}", [128, H * 384], BF16) for i in range(2)]
        attnT = [sb3(f"attnT{i}", [128, H * 384], BF16) for i in range(2)]
        with ExitStack() as ph:
            psim = [psum(ph, f"psim{i}", [128, 384]) for i in range(4)]
            pav = [psum(ph, f"pav{i}", [64, 1024]) for i in range(2)]
            for c in range(NCT):
                at = attn[c % 2]
                att = attnT[c % 2]
                for h in range(H):
                    pp = psim[(c * H + h) % 4]
                    qoc, qrow = h // 2, (h % 2) * 64
                    lo_dlt, hi_dlt = max(-1, -c), min(1, NCT - 1 - c)
                    if lo_dlt > -1:
                        p.op("vector", lambda v, pp=pp: v.memset(
                            pp[:, 0:128], NEG_BIG), ins=[pp], outs=[pp])
                    if hi_dlt < 1:
                        p.op("vector", lambda v, pp=pp: v.memset(
                            pp[:, 256:384], NEG_BIG), ins=[pp], outs=[pp])
                    p.mm(pp[:, (lo_dlt + 1) * 128:(hi_dlt + 2) * 128],
                         qkT[qoc][qrow:qrow + 64, c * 128:(c + 1) * 128],
                         qkT[4 + qoc][qrow:qrow + 64,
                                      (c + lo_dlt) * 128:(c + hi_dlt + 1) * 128],
                         start=True, stop=True)
                    p.op("vector", lambda v, pp=pp: v.tensor_tensor(
                        out=pp[:], in0=pp[:], in1=lmask[:], op=ALU.add),
                        ins=[pp, lmask], outs=[pp])
                    p.op("scalar", lambda s, pp=pp, at=at, h=h, c=c: s.activation(
                        out=at[:, h * 384:(h + 1) * 384], in_=pp[:], func=AF.Exp,
                        scale=0.125, accum_out=den[:, c * H + h:c * H + h + 1]),
                        ins=[pp], outs=[at, den])
                p.op("vector", lambda v, c=c: v.reciprocal(
                    out=rden[:, c * H:(c + 1) * H], in_=den[:, c * H:(c + 1) * H]),
                    ins=[den], outs=[rden])
                for h in range(H):
                    p.op("vector", lambda v, at=at, h=h, c=c: v.tensor_scalar(
                        out=at[:, h * 384:(h + 1) * 384],
                        in0=at[:, h * 384:(h + 1) * 384],
                        scalar1=rden[:, c * H + h:c * H + h + 1], scalar2=None,
                        op0=ALU.mult), ins=[at, rden], outs=[at])
                if DEBUG and c == 3:
                    p.dma("sync", "dbg", dbg["dbg_attn"], at[:])
                p.dma("sync", "tat", att[:].rearrange("p (b t) -> p b t", b=24),
                      at[:].rearrange("p (b t) -> p b t", b=24), transpose=True)
                if c < 8:
                    w6_amax(c)
                elif c == 8:
                    w6_scale_batch()
                if 8 <= c < 16:
                    w6_quant(c - 8)
                pa = pav[c % 2]
                for h in range(H):
                    first = True
                    for di, dlt in enumerate((-1, 0, 1)):
                        kc = c + dlt
                        vt = vtok[kc] if 0 <= kc < NCT else vtok[NCT]
                        p.mm(pa[:, h * 128:(h + 1) * 128],
                             vt[:, h * 65:h * 65 + 64],
                             att[:, h * 384 + di * 128: h * 384 + (di + 1) * 128],
                             start=first, stop=(di == 2))
                        first = False
                for par in range(2):
                    p.op("vector", lambda v, pa=pa, par=par, c=c: v.tensor_copy(
                        out=avTv[par * 64:(par + 1) * 64, :, c, :],
                        in_=pa[:, :].rearrange("p (h t) -> p h t", t=128)[:, par::2, :]),
                        ins=[pa], outs=[avT])

        if DEBUG:
            o_avT = dout("dbg_avT", [128, 4 * N], BF16)
            p.dma("sync", "dbg", o_avT, avT[:])

        p.full_barrier()
        arena3.close()

        arena4 = ExitStack()
        def sb4(name, shape, dtype=F32):
            return arena4.enter_context(nc.sbuf_tensor(name, shape, dtype))
        wloutb = [sb4(f"wloutb{k}", [128, D], BF16) for k in range(4)]
        for k in range(4):
            p.dma("scalar", "gw3", scr_v[:], wlout_hbm[k * 128:(k + 1) * 128, :])
            p.barrier_dma("vector", "gw3")
            p.op("vector", lambda v, k=k: v.tensor_copy(
                out=wloutb[k][:], in_=scr_v[:]), ins=[scr_v], outs=[wloutb[k]])
        lout = [sb4(f"lout{i}", [128, D], BF16) for i in range(2)]
        with ExitStack() as ph:
            plo = [psum(ph, f"plo{i}", [128, D]) for i in range(2)]
            for c in range(NCT):
                pp = plo[c % 2]
                lo = lout[c % 2]
                for k in range(4):
                    for sl in range(2):
                        p.mm(pp[:, sl * 512:(sl + 1) * 512], avTv[:, k, c, :],
                             wloutb[k][:, sl * 512:(sl + 1) * 512],
                             start=(k == 0), stop=(k == 3))
                p.op("vector", lambda v, pp=pp, lo=lo: v.tensor_copy(
                    out=lo[:], in_=pp[:]), ins=[pp], outs=[lo])
                p.dma("sync", "slight", d_light[c * 128:(c + 1) * 128, :], lo[:])

        if DEBUG:
            o_light = dout("dbg_light", [N, D], BF16)
            p.barrier_dma("sync", "slight")
            lo2 = lout[0]
            for c in range(NCT):
                p.dma("sync", "dbg2", lo2[:], d_light[c * 128:(c + 1) * 128, :])
                p.dma("sync", "dbg", o_light[c * 128:(c + 1) * 128, :], lo2[:])

        p.full_barrier()
        arena4.close()
        light_arena.close()

        print("SBUF remaining before routing:", nc.sbuf_bytes_remaining)
        e0f = sb("e0f_s", [128, 128])
        trie = sb("trie_s", [128, 128], BF16)
        patkv = sb("patkv_s", [1, 32])
        onesf = sb("onesf", [128, 1])
        onesb = sb("onesb", [128, 1], BF16)
        t32a = sb("t32a", [32, 32])
        t32b = sb("t32b", [32, 32])
        p.dma("sync", "gwd", e0f[:], e0f_hbm)
        p.dma("sync", "gwe", trie[:], trie_hbm)
        p.dma("sync", "gwf", patkv[:], patkv_hbm)
        p.op("vector", lambda v: v.memset(onesf[:], 1.0), outs=[onesf])
        p.op("vector", lambda v: v.memset(onesb[:], 1.0), outs=[onesb])
        p.op("vector", lambda v: v.memset(t32a[:], 0.0), outs=[t32a])

        s_cm = sb("s_cm", [128, 32])
        rt1 = sb("rt1", [128, 32]); rt2 = sb("rt2", [128, 32])
        ret = sb("ret", [128, 32]); rred = sb("rred", [128, 4])
        prow = sb("prow", [128, 64])
        rowA = sb("rowA", [1, 2]); rowNA = sb("rowNA", [1, 2])
        rowMB = sb("rowMB", [1, 2]); rowT = sb("rowT", [1, 2])
        rowLN = sb("rowLN", [1, 2]); rowMX = sb("rowMX", [1, 2])
        rowLO = sb("rowLO", [1, 2]); rowHI = sb("rowHI", [1, 2])
        rowMD = sb("rowMD", [1, 2]); rowGE = sb("rowGE", [1, 2])
        rowGEi = sb("rowGEi", [1, 2], I32)
        rowLTi = sb("rowLTi", [1, 2], I32)
        rowQM = sb("rowQM", [1, 2]); rowQZ = sb("rowQZ", [1, 2])
        rowN1 = sb("rowN1", [1, 4])
        cl1 = sb("cl1", [128, 32]); clz = sb("clz", [128, 32])
        clm = sb("clm", [128, 32]); smsk = sb("smsk", [128, 32])
        scx = sb("scx", [128, 32])
        selm = sb("selm", [128, 32]); sel = sb("sel", [128, 32])
        pfx1 = sb("pfx1", [128, 32]); pfxz = sb("pfxz", [128, 32])
        slot = sb("slot", [128, 32])
        maskb = sb("maskb", [128, 32], BF16)
        tfcol = sb("tfcol", [128, 1])
        gidx_i = sb("gidx_i", [128, NCT], I32)
        idxq_i = sb("idxq_i", [128, NCT], I32)
        idxkv_i = sb("idxkv_i", [128, NCT], I32)

        arena_tok = ExitStack()
        tokb = [arena_tok.enter_context(nc.sbuf_tensor(f"tokb{r}", [128, D], F32))
                for r in range(2)]
        p.dma("sync", "gtok0", tokb[0][:], qtok_row.broadcast_to([128, D]))
        p.dma("sync", "gtok1", tokb[1][:], kvtok_row.broadcast_to([128, D]))
        for r in range(2):
            for c in range(NCT):
                p.op("vector", lambda v, r=r, c=c: v.scalar_tensor_tensor(
                    out=scr_v[:], in0=xn[c][:], scalar=1.0, in1=tokb[r][:],
                    op0=ALU.mult, op1=ALU.mult,
                    accum_out=s_cm[:, r * 16 + c:r * 16 + c + 1]),
                    ins=[xn[c], tokb[r]], outs=[scr_v, s_cm])
        p.full_barrier()
        arena_tok.close()

        s_v = s_cm[:].rearrange("p (r c) -> p r c", r=2)

        p.op("vector", lambda v: v.memset(prow[:], 0.0), outs=[prow])
        p.op("vector", lambda v: v.memset(rowA[:], A1), outs=[rowA])

        ffn_arena = ExitStack()
        qx4_box[0] = ffn_arena.enter_context(nc.sbuf_tensor("qx4w", [128, D], F32))
        g_st[0] = ffn_arena.enter_context(nc.sbuf_tensor("g_w4", [128, D], BF16))
        g_st.append(ffn_arena.enter_context(nc.sbuf_tensor("g_w4b", [128, D], BF16)))
        ytile_box[0] = [ffn_arena.enter_context(
            nc.sbuf_tensor(f"ytw4_{i}", [128, D], BF16)) for i in range(2)]
        yT_box[0] = [ffn_arena.enter_context(
            nc.sbuf_tensor(f"yTw4_{i}", [128, D], BF16)) for i in range(2)]
        XW = NCT * 32
        amx = ffn_arena.enter_context(nc.sbuf_tensor("amx", [128, XW], F32))
        qscx = ffn_arena.enter_context(nc.sbuf_tensor("qscx", [128, XW], F32))
        qrsx = ffn_arena.enter_context(nc.sbuf_tensor("qrsx", [128, XW], F32))
        qt1 = ffn_arena.enter_context(nc.sbuf_tensor("qt1", [128, XW], F32))
        qt2 = ffn_arena.enter_context(nc.sbuf_tensor("qt2", [128, XW], F32))
        for c in range(NCT):
            p.op("vector", lambda v, c=c: v.reduce_max(
                 out=amx[:, c * 32:(c + 1) * 32, None],
                 in_=xn[c][:].rearrange("p (b k) -> p b k", b=32),
                 axis=AX.X, apply_absolute_value=True), ins=[xn[c]], outs=[amx])
        p.op("vector", lambda v: v.memset(qt2[:], -10.0), outs=[qt2])
        for k in range(-9, 7):
            p.op("vector", lambda v, k=k: v.scalar_tensor_tensor(out=qt2[:],
                 in0=amx[:], scalar=float(2.0 ** k), in1=qt2[:],
                 op0=ALU.is_ge, op1=ALU.add), ins=[amx, qt2], outs=[qt2])
        p.op("vector", lambda v: v.tensor_scalar(out=qt1[:], in0=qt2[:],
             scalar1=dcol[:, 0:1], scalar2=-2.0, op0=ALU.add, op1=ALU.add),
             ins=[qt2, dcol], outs=[qt1])
        p.op("vector", lambda v: v.tensor_scalar(out=qt2[:], in0=qt1[:],
             scalar1=LN2_HI, scalar2=None, op0=ALU.mult), ins=[qt1], outs=[qt2])
        p.op("vector", lambda v: v.scalar_tensor_tensor(out=qt2[:], in0=qt1[:],
             scalar=LN2_LO, in1=qt2[:], op0=ALU.mult, op1=ALU.add),
             ins=[qt1], outs=[qt2])
        p.op("scalar", lambda sc: sc.activation(out=qscx[:], in_=qt2[:],
             func=AF.Exp), ins=[qt2], outs=[qscx])
        p.op("vector", lambda v: v.reciprocal(out=qrsx[:], in_=qscx[:]),
             ins=[qscx], outs=[qrsx])
        p.op("vector", lambda v: v.tensor_tensor(out=qt1[:], in0=qscx[:],
             in1=qrsx[:], op=ALU.mult), ins=[qscx, qrsx], outs=[qt1])
        p.op("vector", lambda v: v.tensor_scalar(out=qt1[:], in0=qt1[:],
             scalar1=-1.0, scalar2=2.0, op0=ALU.mult, op1=ALU.add),
             ins=[qt1], outs=[qt1])
        p.op("vector", lambda v: v.tensor_tensor(out=qrsx[:], in0=qrsx[:],
             in1=qt1[:], op=ALU.mult), ins=[qt1, qrsx], outs=[qrsx])
        p.op("vector", lambda v: v.tensor_scalar(out=qrsx[:], in0=qrsx[:],
             scalar1=2.0, scalar2=None, op0=ALU.mult), ins=[qrsx], outs=[qrsx])
        p.op("vector", lambda v: v.tensor_scalar(out=qscx[:], in0=qscx[:],
             scalar1=0.5, scalar2=None, op0=ALU.mult), ins=[qscx], outs=[qscx])

        _ffn_pending = list(range(NCT))
        _pump_ctr = [0]

        def pump_ffn(pf):
            _pump_ctr[0] += 1
            if _pump_ctr[0] % 3 != 1:
                return
            if _ffn_pending:
                ffn_chunk(_ffn_pending.pop(0), pf)

        with ExitStack() as ph:
            pbc = psum(ph, "pbc", [128, 64])
            psc = psum(ph, "psc", [1, 4])
            pf = [psum(ph, f"pf{i}", [128, D]) for i in range(2)]
            cur = 4.0 * 0.7
            for it in range(1, 20):
                ic = float(np.float32(1.0 / cur))
                p.op("vector", lambda v: v.tensor_scalar(out=rowNA[:], in0=rowA[:],
                     scalar1=-1.0, scalar2=None, op0=ALU.mult), ins=[rowA], outs=[rowNA])
                p.op("vector", lambda v, ic=ic: v.tensor_scalar(out=rowMB[:], in0=rowNA[:],
                     scalar1=ic, scalar2=None, op0=ALU.mult), ins=[rowNA], outs=[rowMB])
                p.op("vector", lambda v: v.tensor_copy(
                     out=prow[0:1, 0:32].rearrange("p (r c) -> p r c", r=2),
                     in_=rowNA[0:1, :, None].broadcast_to([1, 2, 16])),
                     ins=[rowNA], outs=[prow])
                p.op("vector", lambda v: v.tensor_copy(
                     out=prow[0:1, 32:64].rearrange("p (r c) -> p r c", r=2),
                     in_=rowMB[0:1, :, None].broadcast_to([1, 2, 16])),
                     ins=[rowMB], outs=[prow])
                p.mm(pbc[:], e0f[:], prow[:], start=True, stop=True, fp32=True)
                p.op("vector", lambda v: v.tensor_tensor(out=rt1[:], in0=s_cm[:],
                     in1=pbc[:, 0:32], op=ALU.min), ins=[s_cm, pbc], outs=[rt1])
                p.op("vector", lambda v, ic=ic: v.scalar_tensor_tensor(out=rt2[:],
                     in0=rt1[:], scalar=ic, in1=pbc[:, 32:64], op0=ALU.mult,
                     op1=ALU.subtract), ins=[rt1, pbc], outs=[rt2])
                p.op("scalar", lambda sc: sc.activation(out=ret[:], in_=rt2[:],
                     func=AF.Exp), ins=[rt2], outs=[ret])
                p.op("vector", lambda v: v.reduce_sum(out=rred[:, 0:2, None],
                     in_=ret[:].rearrange("p (r c) -> p r c", r=2), axis=AX.X),
                     ins=[ret], outs=[rred])
                p.mm(psc[:, 0:2], onesf[:], rred[:, 0:2], start=True, stop=True, fp32=True)
                p.op("scalar", lambda sc: sc.activation(out=rowLN[:], in_=psc[:, 0:2],
                     func=AF.Ln), ins=[psc], outs=[rowLN])
                p.op("vector", lambda v: v.tensor_tensor(out=rowT[:], in0=rowLN[:],
                     in1=rowMB[:], op=ALU.add), ins=[rowLN, rowMB], outs=[rowT])
                p.op("vector", lambda v, cur=cur: v.tensor_scalar(out=rowA[:],
                     in0=rowT[:], scalar1=float(np.float32(LOGK)),
                     scalar2=float(np.float32(-cur)), op0=ALU.subtract,
                     op1=ALU.mult), ins=[rowT], outs=[rowA])
                cur = max(cur * 0.7, 0.03)

            p.op("vector", lambda v: v.tensor_copy(
                 out=prow[0:1, 0:32].rearrange("p (r c) -> p r c", r=2),
                 in_=rowA[0:1, :, None].broadcast_to([1, 2, 16])),
                 ins=[rowA], outs=[prow])
            p.op("vector", lambda v: v.memset(rowT[:, 0:1], 2.0 ** -75), outs=[rowT])
            p.op("vector", lambda v: v.tensor_tensor(out=rowT[:, 1:2], in0=rowT[:, 0:1],
                 in1=rowT[:, 0:1], op=ALU.mult), ins=[rowT], outs=[rowT])
            p.op("vector", lambda v: v.tensor_tensor(out=rowT[:, 0:1], in0=rowT[:, 1:2],
                 in1=rowT[:, 1:2], op=ALU.mult), ins=[rowT], outs=[rowT])
            p.op("vector", lambda v: v.tensor_scalar(out=rowLN[:, 0:1], in0=rowT[:, 0:1],
                 scalar1=0.0, scalar2=None, op0=ALU.is_gt), ins=[rowT], outs=[rowLN])
            p.op("vector", lambda v: v.tensor_tensor(out=rowT[:, 1:2], in0=rowT[:, 0:1],
                 in1=rowT[:, 0:1], op=ALU.mult), ins=[rowT], outs=[rowT])
            p.op("vector", lambda v: v.tensor_tensor(out=rowT[:, 1:2], in0=rowT[:, 1:2],
                 in1=rowT[:, 0:1], op=ALU.mult), ins=[rowT], outs=[rowT])
            p.op("vector", lambda v: v.tensor_scalar(out=rowT[:, 1:2], in0=rowT[:, 1:2],
                 scalar1=16.0, scalar2=-1.0, op0=ALU.mult, op1=ALU.add),
                 ins=[rowT], outs=[rowT])
            p.op("vector", lambda v: v.tensor_tensor(out=rowT[:, 1:2], in0=rowT[:, 1:2],
                 in1=rowLN[:, 0:1], op=ALU.mult), ins=[rowT, rowLN], outs=[rowT])
            p.op("vector", lambda v: v.tensor_scalar(out=rowT[:, 1:2], in0=rowT[:, 1:2],
                 scalar1=1.0, scalar2=float(2.0 ** -126), op0=ALU.add, op1=ALU.mult),
                 ins=[rowT], outs=[rowT])
            p.op("vector", lambda v: v.tensor_copy(out=prow[0:1, 32:33],
                 in_=rowT[0:1, 1:2]), ins=[rowT], outs=[prow])
            p.mm(pbc[:], e0f[:], prow[:], start=True, stop=True, fp32=True)
            p.op("vector", lambda v: v.tensor_copy(out=tfcol[:], in_=pbc[:, 32:33]),
                 ins=[pbc], outs=[tfcol])
            p.op("vector", lambda v: v.tensor_tensor(out=rt1[:], in0=s_cm[:],
                 in1=pbc[:, 0:32], op=ALU.add), ins=[s_cm, pbc], outs=[rt1])
            p.op("vector", lambda v: v.tensor_scalar(out=cl1[:], in0=rt1[:],
                 scalar1=0.0, scalar2=None, op0=ALU.is_ge), ins=[rt1], outs=[cl1])
            p.op("vector", lambda v: v.tensor_scalar(out=rt2[:], in0=rt1[:],
                 scalar1=0.0, scalar2=None, op0=ALU.min), ins=[rt1], outs=[rt2])
            p.op("scalar", lambda sc: sc.activation(out=scx[:], in_=rt2[:],
                 func=AF.Exp, scale=float(np.float32(1.0 / 0.03))),
                 ins=[rt2], outs=[scx])
            p.op("vector", lambda v: v.scalar_tensor_tensor(out=scx[:], in0=scx[:],
                 scalar=tfcol[:], in1=scx[:], op0=ALU.is_ge, op1=ALU.mult),
                 ins=[scx, tfcol], outs=[scx])
            p.op("vector", lambda v: v.tensor_scalar(out=clz[:], in0=scx[:],
                 scalar1=0.0, scalar2=None, op0=ALU.is_equal), ins=[scx], outs=[clz])
            p.op("vector", lambda v: v.tensor_scalar(out=clm[:], in0=cl1[:],
                 scalar1=-1.0, scalar2=1.0, op0=ALU.mult, op1=ALU.add),
                 ins=[cl1], outs=[clm])
            p.op("vector", lambda v: v.tensor_tensor(out=clm[:], in0=clm[:],
                 in1=clz[:], op=ALU.subtract), ins=[clm, clz], outs=[clm])
            p.op("vector", lambda v: v.reduce_sum(out=rred[:, 0:2, None],
                 in_=cl1[:].rearrange("p (r c) -> p r c", r=2), axis=AX.X),
                 ins=[cl1], outs=[rred])
            p.op("vector", lambda v: v.reduce_sum(out=rred[:, 2:4, None],
                 in_=clm[:].rearrange("p (r c) -> p r c", r=2), axis=AX.X),
                 ins=[clm], outs=[rred])
            p.mm(psc[:, 0:4], onesf[:], rred[:, 0:4], start=True, stop=True, fp32=True)
            p.op("vector", lambda v: v.tensor_copy(out=rowN1[:], in_=psc[:, 0:4]),
                 ins=[psc], outs=[rowN1])
            p.op("vector", lambda v: v.tensor_scalar(out=rowT[:], in0=rowN1[:, 0:2],
                 scalar1=-1.0, scalar2=1024.0, op0=ALU.mult, op1=ALU.add),
                 ins=[rowN1], outs=[rowT])
            p.op("vector", lambda v: v.tensor_tensor(out=rowQM[:], in0=rowT[:],
                 in1=rowN1[:, 2:4], op=ALU.min), ins=[rowT, rowN1], outs=[rowQM])
            p.op("vector", lambda v: v.tensor_scalar(out=rowQM[:], in0=rowQM[:],
                 scalar1=0.0, scalar2=None, op0=ALU.max), ins=[rowQM], outs=[rowQM])
            p.op("vector", lambda v: v.tensor_tensor(out=rowQZ[:], in0=rowT[:],
                 in1=rowN1[:, 2:4], op=ALU.subtract), ins=[rowT, rowN1], outs=[rowQZ])
            p.op("vector", lambda v: v.tensor_scalar(out=rowQZ[:], in0=rowQZ[:],
                 scalar1=0.0, scalar2=None, op0=ALU.max), ins=[rowQZ], outs=[rowQZ])
            p.op("vector", lambda v: v.tensor_scalar(out=rt2[:], in0=clm[:],
                 scalar1=float(-NEG_BIG), scalar2=NEG_BIG, op0=ALU.mult, op1=ALU.add),
                 ins=[clm], outs=[rt2])
            p.op("vector", lambda v: v.tensor_tensor(out=smsk[:], in0=s_cm[:],
                 in1=clm[:], op=ALU.mult), ins=[s_cm, clm], outs=[smsk])
            p.op("vector", lambda v: v.tensor_tensor(out=smsk[:], in0=smsk[:],
                 in1=rt2[:], op=ALU.add), ins=[smsk, rt2], outs=[smsk])
            p.op("vector", lambda v: v.tensor_scalar(out=rowLO[:], in0=rowA[:],
                 scalar1=-1.0, scalar2=-23.0, op0=ALU.mult, op1=ALU.add),
                 ins=[rowA], outs=[rowLO])
            p.op("vector", lambda v: v.tensor_scalar(out=rowHI[:], in0=rowA[:],
                 scalar1=-1.0, scalar2=1.0, op0=ALU.mult, op1=ALU.add),
                 ins=[rowA], outs=[rowHI])
            for it in range(25):
                p.op("vector", lambda v: v.tensor_tensor(out=rowMD[:], in0=rowLO[:],
                     in1=rowHI[:], op=ALU.add), ins=[rowLO, rowHI], outs=[rowMD])
                p.op("vector", lambda v: v.tensor_scalar(out=rowMD[:], in0=rowMD[:],
                     scalar1=0.5, scalar2=None, op0=ALU.mult), ins=[rowMD], outs=[rowMD])
                p.op("vector", lambda v: v.tensor_copy(
                     out=prow[0:1, 0:32].rearrange("p (r c) -> p r c", r=2),
                     in_=rowMD[0:1, :, None].broadcast_to([1, 2, 16])),
                     ins=[rowMD], outs=[prow])
                p.mm(pbc[:, 0:32], e0f[:], prow[:, 0:32], start=True, stop=True, fp32=True)
                p.op("vector", lambda v: v.tensor_tensor(out=rt1[:], in0=smsk[:],
                     in1=pbc[:, 0:32], op=ALU.is_ge), ins=[smsk, pbc], outs=[rt1])
                p.op("vector", lambda v: v.reduce_sum(out=rred[:, 0:2, None],
                     in_=rt1[:].rearrange("p (r c) -> p r c", r=2), axis=AX.X),
                     ins=[rt1], outs=[rred])
                p.mm(psc[:, 0:2], onesf[:], rred[:, 0:2], start=True, stop=True, fp32=True)
                p.op("vector", lambda v: v.tensor_tensor(out=rowGE[:], in0=psc[:, 0:2],
                     in1=rowQM[:], op=ALU.is_ge), ins=[psc, rowQM], outs=[rowGE])
                p.op("vector", lambda v: v.tensor_copy(out=rowGEi[:], in_=rowGE[:]),
                     ins=[rowGE], outs=[rowGEi])
                p.op("vector", lambda v: v.tensor_scalar(out=rowGE[:], in0=rowGE[:],
                     scalar1=-1.0, scalar2=1.0, op0=ALU.mult, op1=ALU.add),
                     ins=[rowGE], outs=[rowGE])
                p.op("vector", lambda v: v.tensor_copy(out=rowLTi[:], in_=rowGE[:]),
                     ins=[rowGE], outs=[rowLTi])
                p.op("vector", lambda v: v.copy_predicated(out=rowLO[:],
                     mask=rowGEi[:], data=rowMD[:]),
                     ins=[rowGEi, rowMD, rowLO], outs=[rowLO])
                p.op("vector", lambda v: v.copy_predicated(out=rowHI[:],
                     mask=rowLTi[:], data=rowMD[:]),
                     ins=[rowLTi, rowMD, rowHI], outs=[rowHI])
            p.op("vector", lambda v: v.tensor_copy(
                 out=prow[0:1, 0:32].rearrange("p (r c) -> p r c", r=2),
                 in_=rowLO[0:1, :, None].broadcast_to([1, 2, 16])),
                 ins=[rowLO], outs=[prow])
            p.op("vector", lambda v: v.tensor_copy(
                 out=prow[0:1, 32:64].rearrange("p (r c) -> p r c", r=2),
                 in_=rowQZ[0:1, :, None].broadcast_to([1, 2, 16])),
                 ins=[rowQZ], outs=[prow])
            p.mm(pbc[:], e0f[:], prow[:], start=True, stop=True, fp32=True)
            p.op("vector", lambda v: v.tensor_tensor(out=selm[:], in0=smsk[:],
                 in1=pbc[:, 0:32], op=ALU.is_ge), ins=[smsk, pbc], outs=[selm])

            def prefix(mask_sb, out_sb, ppf):
                p.op("vector", lambda v: v.tensor_copy(out=maskb[:], in_=mask_sb[:]),
                     ins=[mask_sb], outs=[maskb])
                p.mm(ppf[0:1, 0:32], onesb[:], maskb[:], start=True, stop=True)
                p.op("vector", lambda v: v.tensor_copy(out=scr_v[0:1, 64:96],
                     in_=ppf[0:1, 0:32]), ins=[ppf], outs=[scr_v])
                p.op("vector", lambda v: v.tensor_tensor_scan(out=scr_v[0:1, 0:32],
                     data0=scr_v[0:1, 64:96], data1=scr_v[0:1, 64:96], initial=0.0,
                     op0=ALU.add, op1=ALU.bypass), ins=[scr_v], outs=[scr_v])
                p.op("vector", lambda v: v.tensor_scalar(out=rowT[:, 0:1],
                     in0=scr_v[0:1, 15:16], scalar1=-1.0, scalar2=None,
                     op0=ALU.mult), ins=[scr_v], outs=[rowT])
                p.op("vector", lambda v: v.scalar_tensor_tensor(out=scr_v[0:1, 32:64],
                     in0=patkv[:], scalar=rowT[:, 0:1], in1=scr_v[0:1, 0:32],
                     op0=ALU.mult, op1=ALU.add), ins=[patkv, rowT, scr_v], outs=[scr_v])
                p.op("vector", lambda v: v.tensor_tensor(out=prow[0:1, 0:32],
                     in0=scr_v[0:1, 32:64], in1=scr_v[0:1, 64:96], op=ALU.subtract),
                     ins=[scr_v], outs=[prow])
                p.mm(ppf[:, 0:32], trie[:], maskb[:], start=True, stop=False)
                p.mm(ppf[:, 0:32], e0f[:], prow[:, 0:32], start=False, stop=True,
                     fp32=True)
                p.op("vector", lambda v: v.tensor_copy(out=out_sb[:], in_=ppf[:, 0:32]),
                     ins=[ppf], outs=[out_sb])

            prefix(cl1, pfx1, pbc)
            p.op("vector", lambda v: v.tensor_scalar(out=rt1[:], in0=pfx1[:],
                 scalar1=1024.0, scalar2=None, op0=ALU.is_lt), ins=[pfx1], outs=[rt1])
            p.op("vector", lambda v: v.tensor_tensor(out=sel[:], in0=cl1[:],
                 in1=rt1[:], op=ALU.mult), ins=[cl1, rt1], outs=[sel])
            p.op("vector", lambda v: v.tensor_tensor(out=sel[:], in0=sel[:],
                 in1=selm[:], op=ALU.add), ins=[sel, selm], outs=[sel])
            prefix(clz, pfxz, pbc)
            p.op("vector", lambda v: v.tensor_copy(
                 out=prow[0:1, 0:32].rearrange("p (r c) -> p r c", r=2),
                 in_=rowQZ[0:1, :, None].broadcast_to([1, 2, 16])),
                 ins=[rowQZ], outs=[prow])
            p.mm(pbc[:, 32:64], e0f[:], prow[:, 0:32], start=True, stop=True, fp32=True)
            p.op("vector", lambda v: v.tensor_tensor(out=rt1[:], in0=pfxz[:],
                 in1=pbc[:, 32:64], op=ALU.is_lt), ins=[pfxz, pbc], outs=[rt1])
            p.op("vector", lambda v: v.tensor_tensor(out=rt1[:], in0=clz[:],
                 in1=rt1[:], op=ALU.mult), ins=[clz, rt1], outs=[rt1])
            p.op("vector", lambda v: v.tensor_tensor(out=sel[:], in0=sel[:],
                 in1=rt1[:], op=ALU.add), ins=[sel, rt1], outs=[sel])
            prefix(sel, slot, pbc)
            def mk_idx(out_i, r, oob):
                p.op("vector", lambda v, r=r: v.tensor_tensor(
                     out=rt1[:, 0:16], in0=slot[:, r * 16:(r + 1) * 16],
                     in1=sel[:, r * 16:(r + 1) * 16], op=ALU.mult),
                     ins=[slot, sel], outs=[rt1])
                p.op("vector", lambda v, r=r, oob=oob: v.tensor_scalar(
                     out=rt2[:, 0:16], in0=sel[:, r * 16:(r + 1) * 16],
                     scalar1=float(-oob), scalar2=float(oob), op0=ALU.mult,
                     op1=ALU.add), ins=[sel], outs=[rt2])
                p.op("vector", lambda v: v.tensor_tensor(out=rt1[:, 0:16],
                     in0=rt1[:, 0:16], in1=rt2[:, 0:16], op=ALU.add),
                     ins=[rt1, rt2], outs=[rt1])
                p.op("vector", lambda v, out_i=out_i: v.tensor_copy(
                     out=out_i[:], in_=rt1[:, 0:16]), ins=[rt1], outs=[out_i])
            mk_idx(idxq_i, 0, 4096)
            mk_idx(idxkv_i, 1, 4096)
            mk_idx(gidx_i, 0, 1024)

        if DEBUG:
            o_sel = dout("dbg_sel", [128, 32])
            o_arow = dout("dbg_arow", [1, 8])
            p.dma("sync", "dbg", o_sel, sel[:])
            p.op("vector", lambda v: v.tensor_copy(out=scr_v[0:1, 0:2], in_=rowA[:]),
                 ins=[rowA], outs=[scr_v])
            p.op("vector", lambda v: v.tensor_copy(out=scr_v[0:1, 2:4], in_=rowQM[:]),
                 ins=[rowQM], outs=[scr_v])
            p.op("vector", lambda v: v.tensor_copy(out=scr_v[0:1, 4:6], in_=rowQZ[:]),
                 ins=[rowQZ], outs=[scr_v])
            p.op("vector", lambda v: v.tensor_copy(out=scr_v[0:1, 6:8], in_=rowLO[:]),
                 ins=[rowLO], outs=[scr_v])
            p.dma("sync", "dbg", o_arow, scr_v[0:1, 0:8])

        print("SBUF remaining before rope:", nc.sbuf_bytes_remaining)
        rtab_arena = ExitStack()
        with ExitStack() as ph:
            pioc = ph.enter_context(nc.sbuf_tensor("pioc", [128, 1], I32))
            p.op("gpsimd", lambda g: g.iota(pioc[:], [[0, 1]],
                 channel_multiplier=1), outs=[pioc])
            p.op("vector", lambda v: v.tensor_copy(out=pcol[:], in_=pioc[:]),
                 ins=[pioc], outs=[pcol])

        hs_r = rtab_arena.enter_context(nc.sbuf_tensor("hs_r", [128, NCT], F32))
        xth = [rtab_arena.enter_context(nc.sbuf_tensor(f"xth{i}", [128, D + 388], BF16))
               for i in range(2)]
        for i in range(2):
            p.op("vector", lambda v, i=i: v.memset(xth[i][:, D:D + 4], 0.0),
                 outs=[xth[i]])
        for c in range(NCT):
            p.op("scalar", lambda sc, c=c: sc.activation(out=scr_a[:], in_=xn[c][:],
                 func=AF.Square, accum_out=stat_sq[:, c:c + 1]),
                 ins=[xn[c]], outs=[scr_a, stat_sq])
            p.op("scalar", lambda sc, c=c: sc.activation(out=scr_a[:, 0:1],
                 in_=stat_sq[:, c:c + 1], func=AF.Sqrt), ins=[stat_sq], outs=[scr_a])
            p.op("vector", lambda v, c=c: v.tensor_scalar(out=scr_v[:, 0:1],
                 in0=scr_a[:, 0:1], scalar1=1e-12, scalar2=None, op0=ALU.max),
                 ins=[scr_a], outs=[scr_v])
            p.op("vector", lambda v, c=c: v.reciprocal(out=hs_r[:, c:c + 1],
                 in_=scr_v[:, 0:1]), ins=[scr_v], outs=[hs_r])
            xt = xth[c % 2]
            p.dma("sync", f"gat{c % 2}", xt[:, D + 4:D + 388],
                  atab_hbm[c * 128:(c + 1) * 128, :])
            p.op("vector", lambda v, c=c, xt=xt: v.tensor_scalar(out=xt[:, 0:D],
                 in0=xn[c][:], scalar1=hs_r[:, c:c + 1], scalar2=None,
                 op0=ALU.mult), ins=[xn[c], hs_r], outs=[xt])
            p.op("vector", lambda v, c=c, xt=xt: v.memset(xt[:, D:D + 1], float(c)),
                 ins=[xt], outs=[xt])
            p.op("vector", lambda v, xt=xt: v.tensor_copy(out=xt[:, D + 1:D + 2],
                 in_=pcol[:]), ins=[pcol, xt], outs=[xt])
            p.idma("scat", d_selq, IndirectOffsetOnAxis(ap=idxq_i[:, c:c + 1], axis=0),
                   xt[:], None, bounds_check=NSEL - 1, oob_is_err=False)
            p.idma("scat", d_selkv, IndirectOffsetOnAxis(ap=idxkv_i[:, c:c + 1], axis=0),
                   xt[:], None, bounds_check=NSEL - 1, oob_is_err=False)

        nullb = rtab_arena.enter_context(nc.sbuf_tensor("nullb", [128, D], BF16))
        p.dma("scalar", "gnq", scr_v[0:1, 0:D], nullq_row)
        with ExitStack() as ph:
            pnq = psum(ph, "pnq", [128, D])
            p.mm(pnq[:, 0:512], e0f[:], scr_v[:, 0:512], start=True, stop=True, fp32=True)
            p.mm(pnq[:, 512:1024], e0f[:], scr_v[:, 512:1024], start=True, stop=True, fp32=True)
            p.op("vector", lambda v: v.tensor_copy(out=nullb[:], in_=pnq[:]),
                 ins=[pnq], outs=[nullb])
        for c in range(NCT):
            p.dma("scalar", "pfh", d_heavy2[c * 128:(c + 1) * 128, :], nullb[:])

        with ExitStack() as ph2:
            pf2 = [psum(ph2, f"pf2{i}", [128, D]) for i in range(2)]
            for c in range(NCT):
                ffn_chunk(c, pf2)

        p.full_barrier()
        rtab_arena.close()
        ffn_arena.close()

        print("SBUF remaining before heavy:", nc.sbuf_bytes_remaining)
        arena5 = ExitStack()
        def sb5(name, shape, dtype=F32):
            return arena5.enter_context(nc.sbuf_tensor(name, shape, dtype))
        wqb = [sb5(f"wqb{dc}", [128, 512], BF16) for dc in range(ND)]
        wkvb = [sb5(f"wkvb{dc}", [128, 1024], BF16) for dc in range(ND)]
        hgam_s = sb5("hgam_s", [128, ND])
        wst = scr_v
        p.dma("sync", "gwg", hgam_s[:], hgam_hbm)
        for dc in range(ND):
            p.dma("scalar", "gw4", wst[:, 0:512], wq_hbm[dc * 128:(dc + 1) * 128, :])
            p.barrier_dma("vector", "gw4")
            p.barrier_dma("vector", "gwg")
            p.op("vector", lambda v, dc=dc: v.tensor_scalar(out=wqb[dc][:],
                 in0=wst[:, 0:512], scalar1=hgam_s[:, dc:dc + 1], scalar2=32.0,
                 op0=ALU.mult, op1=ALU.mult), ins=[wst, hgam_s], outs=[wqb[dc]])
            p.dma("scalar", "gw4", wst[:], wkv_hbm[dc * 128:(dc + 1) * 128, :])
            p.barrier_dma("vector", "gw4")
            p.op("vector", lambda v, dc=dc: v.tensor_scalar(out=wkvb[dc][:],
                 in0=wst[:], scalar1=hgam_s[:, dc:dc + 1], scalar2=32.0,
                 op0=ALU.mult, op1=ALU.mult), ins=[wst, hgam_s], outs=[wkvb[dc]])
        nk_s = sb5("nk_s", [64, H])
        nv_s = sb5("nv_s", [1, H * DH])
        kT8 = sb5("kT8", [128, 512], BF16)
        vaug8 = sb5("vaug8", [128, H * 65], BF16)
        pm8 = sb5("pm8_s", [128, 1])
        p.dma("sync", "gwi", nk_s[:], nullk_hbm)
        p.dma("sync", "gwj", nv_s[:], nullv_hbm)
        p.dma("sync", "gwk", pm8[:], pm8_hbm)
        p.barrier_dma("vector", "gwi")
        p.barrier_dma("vector", "gwj")
        p.barrier_dma("vector", "gwk")
        p.op("vector", lambda v: v.memset(kT8[:], 0.0), outs=[kT8])
        for h in range(H):
            hr8 = (h % 2) * 64
            p.op("vector", lambda v, h=h, hr8=hr8: v.tensor_copy(
                 out=kT8[hr8:hr8 + 64, (h // 2) * 128:(h // 2) * 128 + 1],
                 in_=nk_s[:, h:h + 1]), ins=[nk_s], outs=[kT8])
        p.op("vector", lambda v: v.memset(vaug8[:], 0.0), outs=[vaug8])
        p.op("vector", lambda v: v.tensor_copy(
             out=vaug8[0:1, :].rearrange("p (h e) -> p h e", e=65)[:, :, 0:64],
             in_=nv_s[:].rearrange("p (h d) -> p h d", d=64)),
             ins=[nv_s], outs=[vaug8])
        p.op("vector", lambda v: v.memset(
             vaug8[:].rearrange("p (h e) -> p h e", e=65)[:, :, 64:65], 1.0),
             ins=[vaug8], outs=[vaug8])

        tokof_keep = [sb5(f"tokof{i}", [128, 1], I32) for i in range(8)]
        if DEBUG:
            dbg_xsq = dout("dbg_xsq", [256, 512], BF16)
            dbg_roq = dout("dbg_roq", [256, 128])
            dbg_qro = dout("dbg_qro", [256, 512])
            dbg_rtq = dout("dbg_rtq", [256, 512])
            dbg_xsT = dout("dbg_xsT", [256, 1024], BF16)
        qT_g = sb5("qT_g", [128, 8 * 512], BF16)
        kT_g = sb5("kT_g", [128, 8 * 512], BF16)
        vaug = [sb5(f"vaug{j}", [128, H * 65], BF16) for j in range(8)]
        with ExitStack() as ph:
            xsel = [sb5(f"xsel{i}", [128, D + 388], BF16) for i in range(2)]
            xselT = [sb5(f"xselT{i}", [128, D], BF16) for i in range(2)]
            rop = [sb5(f"rop{i}", [128, 128]) for i in range(2)]
            rang = sb5("rang", [128, 64])
            rscr = sb5("rscr", [128, 64])
            rl1 = sb5("rl1", [128, 64])
            rl2 = sb5("rl2", [128, 64])
            posj = sb5("posj", [128, 1])

            rotq = sb5("rotq", [128, 1024])
            qro = sb5("qro", [128, 1024])
            pq = [psum(ph, f"pq{i}", [128, 512]) for i in range(2)]
            pkv = [psum(ph, f"pkv{i}", [128, 1024]) for i in range(2)]

            for side, dsel in ((0, d_selq), (1, d_selkv)):
                for j in range(8):
                    xs = xsel[j % 2]; xsT = xselT[j % 2]; ro = rop[j % 2]
                    p.dma("sync", f"ghx{j % 2}", xs[:], dsel[j * 128:(j + 1) * 128, :])
                    p.barrier_dma("sync", f"ghx{j % 2}")
                    p.dma("sync", f"thx{j % 2}", xsT[:].rearrange("p (dc t) -> p dc t", dc=ND),
                          xs[:, 0:D].rearrange("p (dc t) -> p dc t", dc=ND),
                          transpose=True)
                    p.op("vector", lambda v, xs=xs: v.tensor_copy(
                         out=rscr[:, 0:2], in_=xs[:, D:D + 2]), ins=[xs], outs=[rscr])
                    p.op("vector", lambda v: v.scalar_tensor_tensor(out=posj[:],
                         in0=rscr[:, 0:1], scalar=128.0, in1=rscr[:, 1:2],
                         op0=ALU.mult, op1=ALU.add), ins=[rscr], outs=[posj])
                    if side == 0:
                        p.op("vector", lambda v, j=j: v.tensor_copy(
                             out=tokof_keep[j][:], in_=posj[:]),
                             ins=[posj], outs=[tokof_keep[j]])
                    p.op("vector", lambda v, xs=xs, ro=ro: v.tensor_tensor(
                         out=ro[:], in0=xs[:, D + 4:D + 132],
                         in1=xs[:, D + 132:D + 260], op=ALU.add),
                         ins=[xs], outs=[ro])
                    p.op("vector", lambda v, xs=xs, ro=ro: v.tensor_tensor(
                         out=ro[:], in0=ro[:], in1=xs[:, D + 260:D + 388],
                         op=ALU.add), ins=[xs, ro], outs=[ro])
                    if side == 0:
                        pp = pq[j % 2]
                        for dc in range(ND):
                            p.mm(pp[:], xsT[:].rearrange("p (dc t) -> p dc t", dc=ND)[:, dc, :],
                                 wqb[dc][:], start=(dc == 0), stop=(dc == ND - 1))
                        p.op("vector", lambda v, pp=pp: v.tensor_scalar(
                             out=rotq[:, 0:512].rearrange("p (h d) -> p h d", d=64)[:, :, 0:32],
                             in0=pp[:].rearrange("p (h d) -> p h d", d=64)[:, :, 32:64],
                             scalar1=-1.0, scalar2=None, op0=ALU.mult),
                             ins=[pp], outs=[rotq])
                        p.op("vector", lambda v, pp=pp: v.tensor_copy(
                             out=rotq[:, 0:512].rearrange("p (h d) -> p h d", d=64)[:, :, 32:64],
                             in_=pp[:].rearrange("p (h d) -> p h d", d=64)[:, :, 0:32]),
                             ins=[pp], outs=[rotq])
                        p.op("vector", lambda v, pp=pp, ro=ro: v.tensor_tensor(
                             out=qro[:, 0:512],
                             in0=pp[:].rearrange("p (h d) -> p h d", d=64),
                             in1=ro[:, None, 0:64].broadcast_to([128, 8, 64]),
                             op=ALU.mult), ins=[pp, ro], outs=[qro])
                        p.op("vector", lambda v, ro=ro: v.tensor_tensor(
                             out=rotq[:, 0:512],
                             in0=rotq[:, 0:512].rearrange("p (h d) -> p h d", d=64),
                             in1=ro[:, None, 64:128].broadcast_to([128, 8, 64]),
                             op=ALU.mult), ins=[rotq, ro], outs=[rotq])
                        p.op("vector", lambda v, xs=xs: v.tensor_tensor(
                             out=xs[:, 0:512], in0=qro[:, 0:512], in1=rotq[:, 0:512],
                             op=ALU.add), ins=[qro, rotq], outs=[xs])
                        if DEBUG and j < 2:
                            p.dma("sync", f"dbx{j}a", dbg_xsq[j * 128:(j + 1) * 128, :],
                                  xs[:, 0:512])
                            p.dma("sync", f"dbx{j}b", dbg_roq[j * 128:(j + 1) * 128, :],
                                  ro[:])
                            p.dma("sync", f"dbx{j}c", dbg_qro[j * 128:(j + 1) * 128, :],
                                  qro[:, 0:512])
                            p.dma("sync", f"dbx{j}d", dbg_rtq[j * 128:(j + 1) * 128, :],
                                  rotq[:, 0:512])
                            p.dma("sync", f"dbx{j}e", dbg_xsT[j * 128:(j + 1) * 128, :],
                                  xsT[:])
                            for _g in "abcde":
                                p.barrier_dma("sync", f"dbx{j}{_g}")
                                p.barrier_dma("vector", f"dbx{j}{_g}")
                        p.dma("sync", "thq",
                              qT_g[:, j * 512:(j + 1) * 512].rearrange("p (b t) -> p b t", b=4),
                              xs[:, 0:512].rearrange("p (b t) -> p b t", b=4),
                              transpose=True)
                    else:
                        pp = pkv[j % 2]
                        for dc in range(ND):
                            for sl in range(2):
                                p.mm(pp[:, sl * 512:(sl + 1) * 512],
                                     xsT[:].rearrange("p (dc t) -> p dc t", dc=ND)[:, dc, :],
                                     wkvb[dc][:, sl * 512:(sl + 1) * 512],
                                     start=(dc == 0), stop=(dc == ND - 1))
                        kv_v = lambda t: t.rearrange("p (h d) -> p h d", d=128)
                        p.op("vector", lambda v, pp=pp: v.tensor_scalar(
                             out=kv_v(rotq[:])[:, :, 0:32],
                             in0=kv_v(pp[:])[:, :, 32:64],
                             scalar1=-1.0, scalar2=None, op0=ALU.mult),
                             ins=[pp], outs=[rotq])
                        p.op("vector", lambda v, pp=pp: v.tensor_copy(
                             out=kv_v(rotq[:])[:, :, 32:64],
                             in_=kv_v(pp[:])[:, :, 0:32]), ins=[pp], outs=[rotq])
                        p.op("vector", lambda v, pp=pp, ro=ro: v.tensor_tensor(
                             out=kv_v(qro[:])[:, :, 0:64],
                             in0=kv_v(pp[:])[:, :, 0:64],
                             in1=ro[:, None, 0:64].broadcast_to([128, 8, 64]),
                             op=ALU.mult), ins=[pp, ro], outs=[qro])
                        p.op("vector", lambda v, ro=ro: v.tensor_tensor(
                             out=kv_v(rotq[:])[:, :, 0:64],
                             in0=kv_v(rotq[:])[:, :, 0:64],
                             in1=ro[:, None, 64:128].broadcast_to([128, 8, 64]),
                             op=ALU.mult), ins=[rotq, ro], outs=[rotq])
                        p.op("vector", lambda v, xs=xs: v.tensor_tensor(
                             out=xs[:, 0:512].rearrange("p (h d) -> p h d", d=64),
                             in0=kv_v(qro[:])[:, :, 0:64],
                             in1=kv_v(rotq[:])[:, :, 0:64],
                             op=ALU.add), ins=[qro, rotq], outs=[xs])
                        p.op("vector", lambda v, pp=pp, j=j: v.tensor_copy(
                             out=vaug[j][:].rearrange("p (h e) -> p h e", e=65)[:, :, 0:64],
                             in_=kv_v(pp[:])[:, :, 64:128]), ins=[pp], outs=[vaug[j]])
                        p.op("vector", lambda v, j=j: v.memset(
                             vaug[j][:].rearrange("p (h e) -> p h e", e=65)[:, :, 64:65],
                             1.0), ins=[vaug[j]], outs=[vaug[j]])
                        p.dma("sync", "thkv",
                              kT_g[:, j * 512:(j + 1) * 512].rearrange("p (b t) -> p b t", b=4),
                              xs[:, 0:512].rearrange("p (b t) -> p b t", b=4),
                              transpose=True)

        p.full_barrier()

        avTh = sb5("avTh", [128, 4 * 1024], BF16)
        if DEBUG:
            o_qTg = dout("dbg_qTg", [128, 8 * 512], BF16)
            o_kTg = dout("dbg_kTg", [128, 8 * 512], BF16)
            _unused = (dbg_xsq, dbg_roq)
            p.dma("sync", "dbg", o_qTg, qT_g[:])
            p.dma("sync", "dbg", o_kTg, kT_g[:])
        whoutb = [sb5(f"whoutb{k}", [128, D], BF16) for k in range(4)]
        for k in range(4):
            p.dma("scalar", "gw5", wst[:], whout_hbm[k * 128:(k + 1) * 128, :])
            p.barrier_dma("vector", "gw5")
            p.op("vector", lambda v, k=k: v.tensor_copy(out=whoutb[k][:],
                 in_=wst[:]), ins=[wst], outs=[whoutb[k]])
        att_s = [sb5(f"att_s{i}", [128, 1024], BF16) for i in range(2)]
        avu = sb5("avu", [64, 1024])
        rdst = [scr_a]
        p.op("vector", lambda v: v.memset(rdst[0][:], 0.0), outs=[rdst[0]])
        with ExitStack() as ph:
            psm = [psum(ph, f"psm{i}", [128, 1024]) for i in range(2)]
            pav = psum(ph, "pav", [65, 1024])
            pbc4 = psum(ph, "pbc4", [64, 1024])
            for h in range(H):
                hh, hrow = h // 2, (h % 2) * 64
                for kt in range(9):
                    pp = psm[kt % 2]
                    hp = h // 2
                    lhs = (kT_g[hrow:hrow + 64,
                                kt * 512 + hp * 128: kt * 512 + hp * 128 + 128]
                           if kt < 8 else
                           kT8[hrow:hrow + 64, hp * 128:hp * 128 + 128])
                    for sl in range(2):
                        p.mm(pp[:, sl * 512:(sl + 1) * 512], lhs,
                             qT_g[hrow:hrow + 64, :].rearrange(
                                 "p (j b t) -> p j b t", j=8, b=4)
                                 [:, sl * 4:(sl + 1) * 4, hh, :],
                             start=True, stop=True)
                    if kt == 8:
                        p.op("vector", lambda v, pp=pp: v.tensor_tensor(
                             out=pp[:], in0=pp[:],
                             in1=pm8[:].broadcast_to([128, 1024]), op=ALU.add),
                             ins=[pp, pm8], outs=[pp])
                    at = att_s[kt % 2]
                    p.op("scalar", lambda sc, pp=pp, at=at: sc.activation(
                         out=at[:], in_=pp[:], func=AF.Exp, scale=0.125),
                         ins=[pp], outs=[at])
                    vg = vaug[kt] if kt < 8 else vaug8
                    for sl in range(2):
                        p.mm(pav[:, sl * 512:(sl + 1) * 512],
                             vg[:, h * 65:h * 65 + 65],
                             at[:, sl * 512:(sl + 1) * 512],
                             start=(kt == 0), stop=(kt == 8))
                rd = rdst[0]
                p.op("vector", lambda v: v.reciprocal(out=avu[0:1, :],
                     in_=pav[64:65, :]), ins=[pav], outs=[avu])
                p.op("vector", lambda v, rd=rd: v.tensor_copy(out=rd[0:1, :],
                     in_=avu[0:1, :]), ins=[avu], outs=[rd])
                p.mm(pbc4[:, 0:512], e0f[:, 0:64], rd[:, 0:512], start=True,
                     stop=True, fp32=True)
                p.mm(pbc4[:, 512:1024], e0f[:, 0:64], rd[:, 512:1024], start=True,
                     stop=True, fp32=True)
                p.op("vector", lambda v: v.tensor_copy(out=avu[:], in_=pav[0:64, :]),
                     ins=[pav], outs=[avu])
                p.op("vector", lambda v, h=h, hh=hh, hrow=hrow: v.tensor_tensor(
                     out=avTh[hrow:hrow + 64, hh * 1024:(hh + 1) * 1024],
                     in0=avu[:], in1=pbc4[:], op=ALU.mult),
                     ins=[avu, pbc4], outs=[avTh])

        hvout = [ytile[0], yT[0]]
        with ExitStack() as ph:
            po = [psum(ph, f"po{i}", [128, D]) for i in range(2)]
            for j in range(8):
                pp = po[j % 2]
                for k in range(4):
                    for sl in range(2):
                        p.mm(pp[:, sl * 512:(sl + 1) * 512],
                             avTh[:, k * 1024 + j * 128: k * 1024 + j * 128 + 128],
                             whoutb[k][:, sl * 512:(sl + 1) * 512],
                             start=(k == 0), stop=(k == 3))
                hv = hvout[j % 2]
                p.op("vector", lambda v, pp=pp, hv=hv: v.tensor_copy(
                     out=hv[:], in_=pp[:]), ins=[pp], outs=[hv])
                p.idma("shv", d_heavy2, IndirectOffsetOnAxis(ap=tokof_keep[j][:], axis=0),
                       hv[:], None)

        p.full_barrier()
        arena5.close()

        arena6 = ExitStack()
        def sb6(name, shape, dtype=F32):
            return arena6.enter_context(nc.sbuf_tensor(name, shape, dtype))
        g_t = [sb6(f"g_t{i}", [128, D], BF16) for i in range(2)]
        lt_t = [sb6(f"lt_t{i}", [128, D], BF16) for i in range(2)]
        hv_t = [sb6(f"hv_t{i}", [128, D], BF16) for i in range(2)]
        xx_t = [sb6(f"xx_t{i}", [128, D]) for i in range(2)]
        o_t = [sb6(f"o_t{i}", [128, D]) for i in range(2)]
        rln = sb6("rln", [128, NCT])
        p.op("vector", lambda v: v.reciprocal(out=rln[:], in_=ln_r[:]),
             ins=[ln_r], outs=[rln])
        for c in range(NCT):
            gt = g_t[c % 2]
            lt = lt_t[c % 2]; hv = hv_t[c % 2]; xx = xx_t[c % 2]; ot_ = o_t[c % 2]
            p.dma("gpsimd", f"gg{c % 2}", gt[:], d_g[c * 128:(c + 1) * 128, :])
            p.dma("scalar", f"glt{c % 2}", lt[:], d_light[c * 128:(c + 1) * 128, :])
            p.dma("sync", f"ghv2{c % 2}", hv[:], d_heavy2[c * 128:(c + 1) * 128, :])
            p.barrier_dma("vector", f"gg{c % 2}")
            p.barrier_dma("vector", f"glt{c % 2}")
            p.barrier_dma("vector", f"ghv2{c % 2}")
            p.op("vector", lambda v, c=c, xx=xx: v.tensor_scalar(
                 out=xx[:], in0=xn[c][:], scalar1=rln[:, c:c + 1],
                 scalar2=ln_m[:, c:c + 1], op0=ALU.mult, op1=ALU.add),
                 ins=[xn[c], rln, ln_m], outs=[xx])
            p.op("vector", lambda v, lt=lt, hv=hv, ot_=ot_: v.tensor_tensor(
                 out=ot_[:], in0=lt[:], in1=hv[:], op=ALU.add),
                 ins=[lt, hv], outs=[ot_])
            p.op("vector", lambda v, gt=gt, ot_=ot_: v.tensor_tensor(
                 out=ot_[:], in0=ot_[:], in1=gt[:], op=ALU.mult),
                 ins=[ot_, gt], outs=[ot_])
            p.op("vector", lambda v, xx=xx, ot_=ot_: v.tensor_tensor(
                 out=ot_[:], in0=ot_[:], in1=xx[:], op=ALU.add),
                 ins=[ot_, xx], outs=[ot_])
            p.dma("sync", "sout", out_hbm[c * 128:(c + 1) * 128, :], ot_[:])

        if DEBUG:
            o_hv = dout("dbg_heavy", [N, D], BF16)
            for c in range(NCT):
                p.dma("scalar", "dbgh", lt_t[0][:], d_heavy2[c * 128:(c + 1) * 128, :])
                p.dma("sync", "dbg", o_hv[c * 128:(c + 1) * 128, :], lt_t[0][:])
        arena6.close()
        blk = es.enter_context(nc.Block())
        for grp in list(p.dma_groups):
            p.barrier_dma("sync", grp)
        p.finalize(blk)

    return nc, dbg


def _host_inputs(inp, core):
    m = {}
    m["x"] = inp["L0_attn_x"][core]
    m["wqkv"] = inp["L0_attn_light_qkv_w"]
    m["wlout"] = inp["L0_attn_light_out_w"]
    m["lgam"] = inp["L0_attn_ln_gamma"].reshape(ND, 128).T
    m["lbeta"] = inp["L0_attn_ln_beta"].reshape(ND, 128).T
    m["qtok"] = inp["L0_attn_q_route_tok"].reshape(1, D)
    m["kvtok"] = inp["L0_attn_kv_route_tok"].reshape(1, D)
    m["hgam"] = inp["L0_attn_heavy_gamma"].reshape(ND, 128).T
    m["wq"] = inp["L0_attn_heavy_q_w"]
    m["wkv"] = np.transpose(inp["L0_attn_heavy_kv_w"], (2, 0, 1)).reshape(D, H * 2 * DH)
    m["nullk"] = inp["L0_attn_heavy_null_kv"][0].T
    m["nullv"] = inp["L0_attn_heavy_null_kv"][1].reshape(1, H * DH)
    m["whout"] = inp["L0_attn_heavy_out_w"]
    m["nullq"] = inp["L0_attn_null_q_token"].reshape(1, D)
    m["wffn"] = inp["L0_ffn_weight"]
    m["bffn"] = inp["L0_ffn_bias"].reshape(1, D)
    m["dw"] = inp["L0_ffn_delta_w"]
    m["dact"] = inp["L0_ffn_delta_act"].reshape(1, 1)
    pq = np.arange(128)[:, None]
    mask = np.full((128, 384), NEG_BIG, np.float32)
    for di in range(3):
        j = np.arange(128)[None, :]
        valid = np.abs(128 * (di - 1) + j - pq) <= 64
        mask[:, di * 128:(di + 1) * 128] = np.where(valid, 0.0, NEG_BIG)
    m["lmask"] = mask
    e0 = np.zeros((128, 128), np.float32); e0[0, :] = 1.0
    m["e0f"] = e0
    m["trie"] = (np.arange(128)[:, None] < np.arange(128)[None, :]).astype(ml_dtypes.bfloat16)
    pk = np.zeros((1, 32), np.float32); pk[0, 16:] = 1.0
    m["patkv"] = pk
    pm8 = np.full((128, 1), NEG_BIG, np.float32); pm8[0, 0] = 0.0
    m["pm8"] = pm8
    f4 = np.float32
    j32 = (np.arange(0, 64, 2).astype(f4) / f4(64.0)).astype(f4)
    powv = np.exp(j32.astype(np.float64) * np.log(10000.0)).astype(f4)
    invf32 = (f4(1.0) / powv).astype(f4)
    tab = (np.arange(N).astype(f4)[:, None] * invf32[None]).astype(f4)
    ang = np.concatenate([tab, tab], axis=1)
    try:
        import jax as _jax
        import jax.numpy as _jnp
        _cpu = _jax.devices("cpu")[0]
        with _jax.default_device(_cpu):
            ctab = np.asarray(_jnp.cos(_jnp.asarray(ang)))
            stab = np.asarray(_jnp.sin(_jnp.asarray(ang)))
    except Exception:
        ctab = np.cos(ang.astype(np.float64)).astype(f4)
        stab = np.sin(ang.astype(np.float64)).astype(f4)
    cs = np.concatenate([ctab, stab], axis=1).astype(f4)
    hi = cs.astype(ml_dtypes.bfloat16)
    r1 = (cs - hi.astype(f4)).astype(f4)
    lo = r1.astype(ml_dtypes.bfloat16)
    r2 = (r1 - lo.astype(f4)).astype(f4)
    lo2 = r2.astype(ml_dtypes.bfloat16)
    assert np.array_equal((hi.astype(f4) + lo.astype(f4)) + lo2.astype(f4), cs)
    m["atab"] = np.concatenate([hi, lo, lo2], axis=1)
    return {k: np.ascontiguousarray(v) for k, v in m.items()}


def kernel(**inputs):
    nc, dbg = build_core()
    in_maps = [_host_inputs(inputs, core) for core in range(B)]
    res = run_bass_kernel_spmd(nc, in_maps, core_ids=list(range(B)))
    if DEBUG:
        kernel.debug = [{k: r[k] for k in dbg} for r in res.results]
    return np.stack([r["out"] for r in res.results], axis=0)


# version 16
# speedup vs baseline: 1.0271x; 1.0138x over previous
import math
import os
import traceback
from contextlib import ExitStack

import numpy as np
import ml_dtypes

import concourse.bass as bass
import concourse.mybir as mybir
from concourse.bass import IndirectOffsetOnAxis
from concourse.bass_utils import run_bass_kernel_spmd

F32 = mybir.dt.float32
BF16 = mybir.dt.bfloat16
I32 = mybir.dt.int32
AF = mybir.ActivationFunctionType
ALU = mybir.AluOpType
AX = mybir.AxisListType

B, N, D = 8, 2048, 1024
H, DH = 8, 64
NCT = 16
ND = 8
NSEL = 1024
DEBUG = bool(int(os.environ.get("MK_DEBUG", "0")))

MAGIC = float(np.float32(1.5 * 2.0 ** 23))
NEG_BIG = float(np.float32(-1.0e38))

def _split12(x):
    hi = np.float32(x)
    u = np.frombuffer(np.float32(hi).tobytes(), dtype=np.uint32)[0]
    u &= np.uint32(0xFFFFF000)
    hi = np.frombuffer(np.uint32(u).tobytes(), dtype=np.float32)[0]
    lo = np.float32(np.float64(x) - np.float64(hi))
    lo2 = np.float32(np.float64(x) - np.float64(hi) - np.float64(lo))
    return float(hi), float(lo), float(lo2)

TWO_PI_HI, TWO_PI_LO, TWO_PI_LO2 = _split12(2.0 * math.pi)
LN2_HI, LN2_LO, _ = _split12(math.log(2.0))
L2E4_HI, L2E4_LO, L2E4_LO2 = _split12(math.log2(10000.0))
INV_2PI = float(np.float32(1.0 / (2.0 * math.pi)))
PI = float(np.float32(math.pi))
HALF_PI = float(np.float32(math.pi / 2.0))

CURS = []
_c = 4.0
for _k in range(20):
    CURS.append(_c)
    _c = max(_c * 0.7, 0.03)
LOGK = math.log(1152.0)
A1 = float(np.float32(np.float32(4.0) * (np.float32(LOGK) - np.float32(math.log(2048.0)))))

ENGINES = ("tensor", "vector", "scalar", "gpsimd", "sync")


class Prog:

    def __init__(self, nc, es):
        self.nc = nc
        self.es = es
        self.sem = {e: es.enter_context(nc.semaphore("S" + e[:2])) for e in ENGINES}
        self.cnt = {e: 0 for e in ENGINES}
        self.body = {e: [] for e in ENGINES}
        self.dma_groups = {}
        self.last_w = {}
        self.readers = {}
        self.waited = {e: {} for e in ENGINES}
        self.fp32_mm_pending = False

    def _names(self, aps):
        out = []
        for ap in aps:
            if ap is None:
                continue
            t = ap.tensor if hasattr(ap, "tensor") else ap
            out.append(t.name)
        return out

    def _srckey(self, dep):
        return ("eng", dep[1]) if dep[0] == "eng" else ("dma", dep[1])

    def _collect_waits(self, eng, ins, outs, bump_dma=False):
        need = {}
        def add(dep):
            if dep is None:
                return
            k = self._srckey(dep)
            if k == ("eng", eng) and eng == "tensor":
                return
            if need.get(k, -1) < dep[2]:
                need[k] = dep[2]
        for nm in self._names(ins):
            add(self.last_w.get(nm))
        for nm in self._names(outs):
            add(self.last_w.get(nm))
            for rk, rv in self.readers.get(nm, {}).items():
                add((rk[0], rk[1], rv))
        waits = []
        for (kind, src), v in sorted(need.items(), key=lambda kv: str(kv[0])):
            if kind == "dma" and bump_dma:
                v = self.dma_groups[src][1]
            if self.waited[eng].get(src, -1) >= v:
                continue
            waits.append((self.sem[src] if kind == "eng" else self.dma_groups[src][0], v))
            self.waited[eng][src] = v
        return waits

    def _record(self, ins, outs, dep):
        for nm in self._names(ins):
            self.readers.setdefault(nm, {})[self._srckey(dep)] = dep[2]
        for nm in self._names(outs):
            self.last_w[nm] = dep
            self.readers[nm] = {}

    def op(self, eng, builder, ins=(), outs=()):
        builder._src = traceback.extract_stack()[-3]
        waits = self._collect_waits(eng, ins, outs)
        self.cnt[eng] += 1
        dep = ("eng", eng, self.cnt[eng])
        self._record(ins, outs, dep)
        self.body[eng].append((waits, builder, self.sem[eng], 1))

    def mm(self, out_ap, lhsT_ap, rhs_ap, start, stop, fp32=False):
        small_k = lhsT_ap.shape[0] < 128
        if small_k and self.fp32_mm_pending:
            v = self.cnt["tensor"]
            if self.waited["tensor"].get("tensor", -1) < v:
                self.body["tensor"].append(([(self.sem["tensor"], v)], None, None, 0))
                self.waited["tensor"]["tensor"] = v
            self.fp32_mm_pending = False
        if fp32:
            self.fp32_mm_pending = True
        self.op("tensor",
                lambda t, o=out_ap, l=lhsT_ap, r=rhs_ap, s0=start, s1=stop:
                t.matmul(o, l, r, start=s0, stop=s1),
                ins=[lhsT_ap, rhs_ap], outs=[out_ap])

    def transpose_pe(self, out_ap, in_ap, ident_ap):
        self.op("tensor", lambda t, o=out_ap, i=in_ap, d=ident_ap: t.transpose(o, i, d),
                ins=[in_ap, ident_ap], outs=[out_ap])

    def _grp(self, group):
        if group not in self.dma_groups:
            self.dma_groups[group] = [
                self.es.enter_context(self.nc.semaphore("D" + group[:6])), 0]
        return self.dma_groups[group]

    def dma(self, queue, group, out_ap, in_ap, **kw):
        g = self._grp(group)
        waits = self._collect_waits(queue, [in_ap], [out_ap], bump_dma=True)
        g[1] += 16
        self._record([in_ap], [out_ap], ("dma", group, g[1]))
        def b(eng, _o=out_ap, _i=in_ap, _kw=kw):
            return eng.dma_start(out=_o, in_=_i, **_kw)
        b._src = traceback.extract_stack()[-2]
        self.body[queue].append((waits, b, g[0], 16))

    def idma(self, group, out_ap, out_off, in_ap, in_off, **kw):
        g = self._grp(group)
        ins = [in_ap] + ([out_off.ap] if out_off is not None else []) + \
              ([in_off.ap] if in_off is not None else [])
        waits = self._collect_waits("gpsimd", ins, [out_ap], bump_dma=True)
        g[1] += 16
        self._record(ins, [out_ap], ("dma", group, g[1]))
        def b(eng, _o=out_ap, _oo=out_off, _i=in_ap, _io=in_off, _kw=kw):
            try:
                return eng.indirect_dma_start(out=_o, out_offset=_oo, in_=_i,
                                              in_offset=_io, **_kw)
            except Exception:
                print("IDMA FAIL out=", _o, "\nin=", _i,
                      "\noo=", None if _oo is None else _oo.ap,
                      "\nio=", None if _io is None else _io.ap)
                raise
        b._src = traceback.extract_stack()[-2]
        self.body["gpsimd"].append((waits, b, g[0], 16))

    def barrier_dma(self, eng, group):
        g = self.dma_groups[group]
        if self.waited[eng].get(group, -1) < g[1]:
            self.body[eng].append(([(g[0], g[1])], None, None, 0))
            self.waited[eng][group] = g[1]

    def full_barrier(self):
        for eng in ENGINES:
            waits = []
            for e2 in ENGINES:
                if e2 != eng and self.waited[eng].get(e2, -1) < self.cnt[e2] \
                        and self.cnt[e2] > 0:
                    waits.append((self.sem[e2], self.cnt[e2]))
                    self.waited[eng][e2] = self.cnt[e2]
            for gname, g in self.dma_groups.items():
                if g[1] > 0 and self.waited[eng].get(gname, -1) < g[1]:
                    waits.append((g[0], g[1]))
                    self.waited[eng][gname] = g[1]
            if waits:
                self.body[eng].append((waits, None, None, 0))

    def finalize(self, block):
        for eng in ENGINES:
            items = self.body[eng]
            def make(seq_items):
                def run(handle):
                    for waits, builder, sem, n_inc in seq_items:
                        for s, v in waits:
                            handle.wait_ge(s, v)
                        if builder is not None:
                            try:
                                inst = builder(handle)
                            except Exception as e:
                                src = getattr(builder, "_src", None)
                                raise RuntimeError(f"emit failed at {src}") from e
                            if sem is not None and inst is not None:
                                inst.then_inc(sem, n_inc)
                return run
            getattr(block, eng)(make(items))


def build_core():
    nc = bass.Bass("TRN2", target_bir_lowering=False)
    es = ExitStack()

    def din(name, shape, dtype=F32):
        return nc.dram_tensor(name, shape, dtype, kind="ExternalInput").ap()

    x_hbm = din("x", [N, D])
    wqkv_hbm = din("wqkv", [D, 1536])
    wlout_hbm = din("wlout", [512, D])
    lgam_hbm = din("lgam", [128, ND])
    lbeta_hbm = din("lbeta", [128, ND])
    qtok_row = din("qtok", [1, D])
    kvtok_row = din("kvtok", [1, D])
    hgam_hbm = din("hgam", [128, ND])
    wq_hbm = din("wq", [D, 512])
    wkv_hbm = din("wkv", [D, 1024])
    nullk_hbm = din("nullk", [DH, H])
    nullv_hbm = din("nullv", [1, H * DH])
    whout_hbm = din("whout", [512, D])
    nullq_row = din("nullq", [1, D])
    wffn_hbm = din("wffn", [D, D])
    bffn_row = din("bffn", [1, D])
    dw_hbm = din("dw", [D, 32])
    dact_hbm = din("dact", [1, 1])
    lmask_hbm = din("lmask", [128, 384], BF16)
    identb_hbm = din("identb", [128, 128], BF16)
    e0f_hbm = din("e0f", [128, 128])
    trie_hbm = din("trie", [128, 128], BF16)
    patkv_hbm = din("patkv", [1, 32])
    pm8_hbm = din("pm8", [128, 1])
    atab_hbm = din("atab", [N, 384], BF16)

    out_hbm = nc.dram_tensor("out", [N, D], F32, kind="ExternalOutput").ap()
    dbg = {}
    def dout(name, shape, dtype=F32):
        t = nc.dram_tensor(name, shape, dtype, kind="ExternalOutput").ap()
        dbg[name] = t
        return t

    d_light = nc.dram_tensor("d_light", [N, D], BF16, kind="Internal").ap()
    d_g = nc.dram_tensor("d_g", [N, D], BF16, kind="Internal").ap()
    d_selq = nc.dram_tensor("d_selq", [NSEL, D + 388], BF16, kind="Internal").ap()
    d_selkv = nc.dram_tensor("d_selkv", [NSEL, D + 388], BF16, kind="Internal").ap()
    d_heavy2 = nc.dram_tensor("d_heavy2", [N, D], BF16, kind="Internal").ap()
    if DEBUG:
        dbg["dbg_attn"] = nc.dram_tensor("dbg_attn", [128, H * 384], BF16,
                                         kind="ExternalOutput").ap()
        dbg["dbg_g"] = nc.dram_tensor("dbg_g", [N, D], BF16,
                                      kind="ExternalOutput").ap()

    with es:
        def sb(name, shape, dtype=F32):
            return es.enter_context(nc.sbuf_tensor(name, shape, dtype))
        def psum(scope, name, shape, dtype=F32):
            return scope.enter_context(nc.psum_tensor(name, shape, dtype))

        p = Prog(nc, es)

        xn = [sb(f"xn{c}", [128, D]) for c in range(NCT)]
        scr_v = sb("scr_v", [128, D])
        scr_a = sb("scr_a", [128, D])
        stat_sum = sb("stat_sum", [128, NCT])
        stat_sq = sb("stat_sq", [128, NCT])
        ln_m = sb("ln_m", [128, NCT])
        ln_r = sb("ln_r", [128, NCT])
        lmask = sb("lmask_s", [128, 384], BF16)
        identb = sb("identb_s", [128, 128], BF16)
        p.dma("sync", "gwa", lmask[:], lmask_hbm)
        p.dma("sync", "gwa2", identb[:], identb_hbm)

        def ln_stats(src_tile, c, sumcol, sqcol):
            p.op("vector", lambda v, t=src_tile, c=c: v.tensor_scalar(
                out=scr_v[:], in0=t[:], scalar1=1.0, scalar2=None,
                op0=ALU.mult, op1=ALU.add, accum_out=sumcol),
                ins=[src_tile], outs=[scr_v, sumcol])
            p.op("scalar", lambda s, t=src_tile, c=c: s.activation(
                out=scr_a[:], in_=t[:], func=AF.Square, accum_out=sqcol),
                ins=[src_tile], outs=[scr_a, sqcol])

        def ln_finish(c, sumcol, sqcol, mcol, rcol, eps=1e-5):
            p.op("vector", lambda v: v.tensor_scalar(
                out=mcol, in0=sumcol, scalar1=float(1.0 / D), scalar2=None,
                op0=ALU.mult), ins=[sumcol], outs=[mcol])
            p.op("vector", lambda v: v.scalar_tensor_tensor(
                out=scr_v[:, 0:1], in0=mcol, scalar=-1.0, in1=mcol,
                op0=ALU.mult, op1=ALU.mult), ins=[mcol], outs=[scr_v])
            p.op("vector", lambda v: v.tensor_scalar(
                out=scr_v[:, 1:2], in0=sqcol, scalar1=float(1.0 / D),
                scalar2=float(eps), op0=ALU.mult, op1=ALU.add),
                ins=[sqcol], outs=[scr_v])
            p.op("vector", lambda v: v.tensor_tensor(
                out=scr_v[:, 2:3], in0=scr_v[:, 1:2], in1=scr_v[:, 0:1],
                op=ALU.add), ins=[scr_v], outs=[scr_v])
            p.op("scalar", lambda s: s.activation(
                out=scr_a[:, 0:1], in_=scr_v[:, 2:3], func=AF.Sqrt),
                ins=[scr_v], outs=[scr_a])
            p.op("vector", lambda v: v.reciprocal(out=rcol, in_=scr_a[:, 0:1]),
                ins=[scr_a], outs=[rcol])

        l_m = sb("l_m", [128, NCT])
        l_r = sb("l_r", [128, NCT])
        lgam_s = sb("lgam_s", [128, ND])
        lbeta_s = sb("lbeta_s", [128, ND])
        b0qk = sb("b0qk", [128, 8])
        e0b = sb("e0b", [128, 128], BF16)
        pcol = sb("pcol", [128, 1])

        sbq = sb
        wqT_g = sbq("wqT_g", [128, 8 * 1024], BF16)
        bfrow = sbq("bfrow", [128, 1024], BF16)
        dcol = sbq("dcol", [128, 1])
        qx1 = sbq("qx1", [128, D]); qx2 = sbq("qx2", [128, D])
        dwt = qx2
        qx4_box = [scr_v]
        wqst = scr_a
        ytile = [sbq("ytile0", [128, D], BF16)]
        yT = [sbq("yT0", [128, D], BF16)]
        ytile_box = [ytile]; yT_box = [yT]
        g_st = [scr_a]

        p.op("vector", lambda v: v.memset(bfrow[:], 0.0), outs=[bfrow])
        p.dma("scalar", "gw6b", scr_v[0:1, 0:D], bffn_row)
        p.barrier_dma("vector", "gw6b")
        p.op("vector", lambda v: v.tensor_copy(out=bfrow[0:1, :], in_=scr_v[0:1, 0:D]),
             ins=[scr_v], outs=[bfrow])
        p.dma("scalar", "gw6c", dcol[:], dact_hbm.broadcast_to([128, 1]))
        p.barrier_dma("scalar", "gw6c")
        p.barrier_dma("vector", "gw6c")
        p.op("scalar", lambda sc: sc.activation(out=dcol[:], in_=dcol[:],
             func=AF.Tanh, scale=4.0), ins=[dcol], outs=[dcol])
        p.op("vector", lambda v: v.tensor_scalar(out=dcol[:], in0=dcol[:],
             scalar1=0.5, scalar2=None, op0=ALU.mult), ins=[dcol], outs=[dcol])

        def quant_pipe(src_ap, dst_tile, src_names, qsc_ap=None, qrs_ap=None,
                       qnames=()):
            bc = lambda t: t[:, :, None].broadcast_to([128, 32, 32])
            p.op("vector", lambda v, src_ap=src_ap, qrs_ap=qrs_ap: v.tensor_tensor(
                 out=qx1[:].rearrange("p (b k) -> p b k", b=32), in0=src_ap,
                 in1=bc(qrs_ap), op=ALU.mult),
                 ins=list(src_names) + list(qnames), outs=[qx1])
            p.op("scalar", lambda sc: sc.activation(out=qx2[:], in_=qx1[:],
                 func=AF.Abs), ins=[qx1], outs=[qx2])
            qx4 = qx4_box[0]
            p.op("vector", lambda v, qx4=qx4: v.tensor_scalar(out=qx4[:], in0=qx2[:],
                 scalar1=0.5, scalar2=None, op0=ALU.is_gt), ins=[qx2], outs=[qx4])
            for thr, cmp in ((1.5, ALU.is_ge), (2.5, ALU.is_gt), (3.5, ALU.is_ge),
                             (5.0, ALU.is_gt), (5.0, ALU.is_gt),
                             (7.0, ALU.is_ge), (7.0, ALU.is_ge),
                             (10.0, ALU.is_gt), (10.0, ALU.is_gt),
                             (10.0, ALU.is_gt), (10.0, ALU.is_gt)):
                p.op("vector", lambda v, thr=thr, cmp=cmp, qx4=qx4: v.scalar_tensor_tensor(
                     out=qx4[:], in0=qx2[:], scalar=thr, in1=qx4[:],
                     op0=cmp, op1=ALU.add), ins=[qx2, qx4], outs=[qx4])
            p.op("scalar", lambda sc: sc.activation(out=qx2[:], in_=qx1[:],
                 func=AF.Sign), ins=[qx1], outs=[qx2])
            p.op("vector", lambda v, qx4=qx4: v.tensor_tensor(out=qx2[:], in0=qx2[:],
                 in1=qx4[:], op=ALU.mult), ins=[qx2, qx4], outs=[qx2])
            p.op("vector", lambda v, dst_tile=dst_tile, qsc_ap=qsc_ap: v.tensor_tensor(
                 out=dst_tile[:].rearrange("p (b k) -> p b k", b=32),
                 in0=qx2[:].rearrange("p (b k) -> p b k", b=32),
                 in1=bc(qsc_ap), op=ALU.mult),
                 ins=[qx2] + list(qnames), outs=[dst_tile])

        amw = qx1
        qscw = sbq("qscw", [128, 256])
        qrsw = sbq("qrsw", [128, 256])

        def w6_amax(ot):
            p.dma("scalar", "gw6", wqst[:], wffn_hbm[ot * 128:(ot + 1) * 128, :])
            p.dma("scalar", "gw6d", dwt[:, 0:32], dw_hbm[ot * 128:(ot + 1) * 128, :])
            p.barrier_dma("vector", "gw6")
            p.barrier_dma("scalar", "gw6d")
            p.op("vector", lambda v, ot=ot: v.reduce_max(
                 out=amw[:, ot * 32:(ot + 1) * 32, None],
                 in_=wqst[:].rearrange("p (b k) -> p b k", b=32),
                 axis=AX.X, apply_absolute_value=True), ins=[wqst], outs=[amw])
            p.op("scalar", lambda sc, ot=ot: sc.activation(
                 out=qscw[:, ot * 32:(ot + 1) * 32], in_=dwt[:, 0:32],
                 func=AF.Tanh, scale=4.0), ins=[dwt], outs=[qscw])

        def w6_scale_batch():
            p.op("vector", lambda v: v.tensor_scalar(out=qrsw[:], in0=qscw[:],
                 scalar1=0.5, scalar2=None, op0=ALU.mult), ins=[qscw], outs=[qrsw])
            p.op("vector", lambda v: v.memset(qscw[:], -10.0), outs=[qscw])
            for k in range(-9, 7):
                p.op("vector", lambda v, k=k: v.scalar_tensor_tensor(out=qscw[:],
                     in0=amw[:, 0:256], scalar=float(2.0 ** k), in1=qscw[:],
                     op0=ALU.is_ge, op1=ALU.add), ins=[amw, qscw], outs=[qscw])
            p.op("vector", lambda v: v.tensor_tensor(out=qrsw[:], in0=qscw[:],
                 in1=qrsw[:], op=ALU.add), ins=[qscw, qrsw], outs=[qrsw])
            p.op("vector", lambda v: v.tensor_scalar(out=qrsw[:], in0=qrsw[:],
                 scalar1=-2.0, scalar2=None, op0=ALU.add), ins=[qrsw], outs=[qrsw])
            p.op("vector", lambda v: v.tensor_scalar(out=amw[:, 0:256], in0=qrsw[:],
                 scalar1=LN2_HI, scalar2=None, op0=ALU.mult), ins=[qrsw], outs=[amw])
            p.op("vector", lambda v: v.scalar_tensor_tensor(out=amw[:, 0:256], in0=qrsw[:],
                 scalar=LN2_LO, in1=amw[:, 0:256], op0=ALU.mult, op1=ALU.add),
                 ins=[qrsw], outs=[amw])
            p.op("scalar", lambda sc: sc.activation(out=qscw[:], in_=amw[:, 0:256],
                 func=AF.Exp), ins=[amw], outs=[qscw])
            p.op("vector", lambda v: v.reciprocal(out=qrsw[:], in_=qscw[:]),
                 ins=[qscw], outs=[qrsw])
            p.op("vector", lambda v: v.tensor_tensor(out=amw[:, 0:256], in0=qscw[:],
                 in1=qrsw[:], op=ALU.mult), ins=[qscw, qrsw], outs=[amw])
            p.op("vector", lambda v: v.tensor_scalar(out=amw[:, 0:256], in0=amw[:, 0:256],
                 scalar1=-1.0, scalar2=2.0, op0=ALU.mult, op1=ALU.add),
                 ins=[amw], outs=[amw])
            p.op("vector", lambda v: v.tensor_tensor(out=qrsw[:], in0=qrsw[:],
                 in1=amw[:, 0:256], op=ALU.mult), ins=[amw, qrsw], outs=[qrsw])
            p.op("vector", lambda v: v.tensor_scalar(out=qrsw[:], in0=qrsw[:],
                 scalar1=2.0, scalar2=None, op0=ALU.mult), ins=[qrsw], outs=[qrsw])
            p.op("vector", lambda v: v.tensor_scalar(out=qscw[:], in0=qscw[:],
                 scalar1=0.5, scalar2=None, op0=ALU.mult), ins=[qscw], outs=[qscw])

        def w6_quant(ot):
            p.dma("scalar", "gw6", wqst[:], wffn_hbm[ot * 128:(ot + 1) * 128, :])
            p.barrier_dma("vector", "gw6")
            yt = ytile_box[0][ot % len(ytile_box[0])]
            quant_pipe(wqst[:].rearrange("p (b k) -> p b k", b=32), yt, [wqst],
                       qsc_ap=qscw[:, ot * 32:(ot + 1) * 32],
                       qrs_ap=qrsw[:, ot * 32:(ot + 1) * 32], qnames=(qscw, qrsw))
            p.dma("sync", "twq",
                  wqT_g[:, ot * 1024:(ot + 1) * 1024].rearrange("p (b t) -> p b t", b=8),
                  yt[:].rearrange("p (b t) -> p b t", b=8), transpose=True)

        def ffn_chunk(c, pf):
            yt = ytile_box[0][c % len(ytile_box[0])]
            quant_pipe(xn[c][:].rearrange("p (b k) -> p b k", b=32), yt, [xn[c]],
                       qsc_ap=qscx[:, c * 32:(c + 1) * 32],
                       qrs_ap=qrsx[:, c * 32:(c + 1) * 32], qnames=(qscx, qrsx))
            ytw = yT_box[0][c % len(yT_box[0])]
            p.dma("sync", f"tyq{c % 2}", ytw[:].rearrange("p (b t) -> p b t", b=8),
                  yt[:].rearrange("p (b t) -> p b t", b=8), transpose=True)
            ytr = ytw
            pp = pf[c % 2]
            for dc in range(ND):
                for sl in range(2):
                    p.mm(pp[:, sl * 512:(sl + 1) * 512],
                         ytr[:, dc * 128:(dc + 1) * 128],
                         wqT_g[:, :].rearrange("p (o b t) -> p o b t", o=8, b=8)
                             [:, sl * 4:(sl + 1) * 4, dc, :],
                         start=(dc == 0), stop=False)
            for sl in range(2):
                p.mm(pp[:, sl * 512:(sl + 1) * 512], e0b[:],
                     bfrow[:, sl * 512:(sl + 1) * 512], start=False, stop=True)
            gt = g_st[c % len(g_st)] if len(g_st) > 1 else g_st[0]
            p.op("scalar", lambda sc, pp=pp, gt=gt: sc.activation(
                 out=gt[:], in_=pp[:], func=AF.Silu), ins=[pp], outs=[gt])
            if DEBUG:
                p.dma("sync", "dbg", dbg["dbg_g"][c * 128:(c + 1) * 128, :], gt[:])
            p.dma("sync", f"sg{c % 2}", d_g[c * 128:(c + 1) * 128, :], gt[:])

        light_arena = ExitStack()
        def sbl(name, shape, dtype=F32):
            return light_arena.enter_context(nc.sbuf_tensor(name, shape, dtype))
        vtok = [sbl(f"vtok{c}", [128, 520], BF16) for c in range(NCT + 1)]
        qkT = [sbl(f"qkT{oc}", [128, N], BF16) for oc in range(8)]
        avT = sbl("avT", [128, 4 * N], BF16)
        avTv = avT[:].rearrange("p (k c t) -> p k c t", k=4, c=NCT)
        den = sbl("den", [128, H * NCT])
        rden = sbl("rden", [128, H * NCT])

        arena1 = ExitStack()
        xtile = [arena1.enter_context(nc.sbuf_tensor(f"xt{i}", [128, D], F32))
                 for i in range(2)]
        for c in range(NCT):
            xt = xtile[c % 2]
            p.dma("sync", f"gx{c % 2}", xt[:], x_hbm[c * 128:(c + 1) * 128, :])
            p.barrier_dma("vector", f"gx{c % 2}")
            p.barrier_dma("scalar", f"gx{c % 2}")
            ln_stats(xt, c, stat_sum[:, c:c + 1], stat_sq[:, c:c + 1])
            ln_finish(c, stat_sum[:, c:c + 1], stat_sq[:, c:c + 1],
                      ln_m[:, c:c + 1], ln_r[:, c:c + 1])
            p.op("vector", lambda v, xt=xt, c=c: v.tensor_scalar(
                out=xn[c][:], in0=xt[:], scalar1=ln_m[:, c:c + 1],
                scalar2=ln_r[:, c:c + 1], op0=ALU.subtract, op1=ALU.mult),
                ins=[xt, ln_m, ln_r], outs=[xn[c]])

        if DEBUG:
            o_xn = dout("dbg_xn", [N, D])
            for c in range(NCT):
                p.dma("sync", "dbg", o_xn[c * 128:(c + 1) * 128, :], xn[c][:])

        p.full_barrier()
        arena1.close()

        for c in range(NCT):
            ln_stats(xn[c], c, stat_sum[:, c:c + 1], stat_sq[:, c:c + 1])
            ln_finish(c, stat_sum[:, c:c + 1], stat_sq[:, c:c + 1],
                      l_m[:, c:c + 1], l_r[:, c:c + 1])

        p.dma("sync", "gwb", lgam_s[:], lgam_hbm)
        p.dma("sync", "gwc", lbeta_s[:], lbeta_hbm)
        arena2 = ExitStack()
        def sb2(name, shape, dtype=F32):
            return arena2.enter_context(nc.sbuf_tensor(name, shape, dtype))
        wqkvb = [sb2(f"wqkvb{dc}", [128, 1536], BF16) for dc in range(ND)]
        for dc in range(ND):
            for lo, hi in ((0, 1024), (1024, 1536)):
                p.dma("scalar", f"gw2{dc % 2}", scr_v[:, 0:hi - lo],
                      wqkv_hbm[dc * 128:(dc + 1) * 128, lo:hi])
                p.barrier_dma("vector", f"gw2{dc % 2}")
                p.barrier_dma("vector", "gwb")
                p.barrier_dma("vector", "gwc")
                p.op("vector", lambda v, dc=dc, lo=lo, hi=hi: v.tensor_scalar(
                    out=wqkvb[dc][:, lo:hi], in0=scr_v[:, 0:hi - lo],
                    scalar1=lgam_s[:, dc:dc + 1],
                    scalar2=None, op0=ALU.mult),
                    ins=[scr_v, lgam_s], outs=[wqkvb[dc]])

        z1T = [sb2("z1T0", [128, ND * 4 * 128], BF16)]
        z1Tv = [t[:].rearrange("p (c dc t) -> p dc c t", dc=ND, c=4) for t in z1T]
        z1T = z1T * 2; z1Tv = z1Tv * 2
        z1b = [sb2("z1b0", [128, D], BF16)]
        z1b = z1b * 2
        def emit_z1(c):
            sl, ci = c // 4, c % 4
            zb = z1b[c % 2]
            p.op("vector", lambda v, c=c, zb=zb: v.tensor_scalar(
                out=zb[:], in0=xn[c][:], scalar1=l_m[:, c:c + 1],
                scalar2=l_r[:, c:c + 1], op0=ALU.subtract, op1=ALU.mult),
                ins=[xn[c], l_m, l_r], outs=[zb])
            p.dma("sync", "tz", z1T[sl % 2][:, ci * 1024:(ci + 1) * 1024]
                      .rearrange("p (dc t) -> p dc t", dc=ND),
                  zb[:].rearrange("p (dc t) -> p dc t", dc=ND), transpose=True)

        lbeta_b = sb2("lbeta_b", [128, ND], BF16)
        p.op("vector", lambda v: v.reciprocal(out=scr_v[:, 0:ND], in_=lgam_s[:]),
             ins=[lgam_s], outs=[scr_v])
        p.op("vector", lambda v: v.tensor_tensor(
            out=lbeta_b[:], in0=lbeta_s[:], in1=scr_v[:, 0:ND], op=ALU.mult),
            ins=[lbeta_s, scr_v], outs=[lbeta_b])
        b0v_row = sb2("b0v_row", [128, 512], BF16)
        with ExitStack() as ph:
            pb = psum(ph, "pb0", [128, 12])
            pbv = psum(ph, "pb0v", [1, 512])
            for j in range(12):
                for dc in range(ND):
                    p.mm(pb[:, j:j + 1], wqkvb[dc][:, j * 128:(j + 1) * 128],
                         lbeta_b[:, dc:dc + 1], start=(dc == 0),
                         stop=(dc == ND - 1))
            for dc in range(ND):
                p.mm(pbv[:], lbeta_b[:, dc:dc + 1],
                     wqkvb[dc][:, 1024:1536], start=(dc == 0), stop=(dc == ND - 1))
            p.op("vector", lambda v: v.tensor_copy(out=b0qk[:], in_=pb[:, 0:8]),
                 ins=[pb], outs=[b0qk])
            p.op("vector", lambda v: v.memset(b0v_row[:], 0.0), outs=[b0v_row])
            p.op("vector", lambda v: v.tensor_copy(out=b0v_row[0:1, :], in_=pbv[:]),
                 ins=[pbv], outs=[b0v_row])

        p.op("vector", lambda v: v.memset(e0b[:], 0.0), outs=[e0b])
        p.op("vector", lambda v: v.memset(e0b[0:1, :], 1.0), ins=[e0b], outs=[e0b])

        p.op("vector", lambda v: v.memset(vtok[NCT][:], 0.0), outs=[vtok[NCT]])
        with ExitStack() as ph:
            pqk = [psum(ph, f"pqk{i}", [128, 512]) for i in range(6)]
            for sl in range(4):
                for ci in range(4):
                    emit_z1(sl * 4 + ci)
                zt = z1Tv[sl % 2]
                rhs_full = z1T[sl % 2]
                for oc in range(8):
                    pp = pqk[oc % 6]
                    for dc in range(ND):
                        p.mm(pp[:], wqkvb[dc][:, oc * 128:(oc + 1) * 128],
                             zt[:, dc, :, :],
                             start=(dc == 0), stop=(dc == ND - 1))
                    p.op("scalar", lambda s, pp=pp, oc=oc, sl=sl: s.activation(
                        out=qkT[oc][:, sl * 512:(sl + 1) * 512], in_=pp[:],
                        func=AF.Identity, bias=b0qk[:, oc:oc + 1]),
                        ins=[pp, b0qk], outs=[qkT[oc]])
                for ci in range(4):
                    c = sl * 4 + ci
                    pp = pqk[(8 + ci) % 6]
                    for dc in range(ND):
                        p.mm(pp[:], zt[:, dc, ci, :], wqkvb[dc][:, 1024:1536],
                             start=(dc == 0), stop=False)
                    p.mm(pp[:], e0b[:], b0v_row[:], start=False, stop=True)
                    p.op("vector", lambda v, pp=pp, c=c: v.tensor_copy(
                        out=vtok[c][:].rearrange("p (h e) -> p h e", e=65)[:, :, 0:64],
                        in_=pp[:].rearrange("p (h d) -> p h d", d=64)),
                        ins=[pp], outs=[vtok[c]])
                    p.op("vector", lambda v, c=c: v.memset(
                        vtok[c][:].rearrange("p (h e) -> p h e", e=65)[:, :, 64:65], 1.0),
                        ins=[vtok[c]], outs=[vtok[c]])

        if DEBUG:
            o_qk0 = dout("dbg_qk0", [128, N], BF16)
            o_qk4 = dout("dbg_qk4", [128, N], BF16)
            o_vt = dout("dbg_vt", [128, 520], BF16)
            p.dma("sync", "dbg", o_qk0, qkT[0][:])
            p.dma("sync", "dbg", o_qk4, qkT[4][:])
            p.dma("sync", "dbg", o_vt, vtok[3][:])

        p.full_barrier()
        arena2.close()

        print("SBUF remaining at light sims:", nc.sbuf_bytes_remaining)
        arena3 = ExitStack()
        def sb3(name, shape, dtype=F32):
            return arena3.enter_context(nc.sbuf_tensor(name, shape, dtype))
        attn = [sb3(f"attn{i}", [128, H * 384], BF16) for i in range(2)]
        attnT = [sb3(f"attnT{i}", [128, H * 384], BF16) for i in range(2)]
        with ExitStack() as ph:
            psim = [psum(ph, f"psim{i}", [128, 384]) for i in range(4)]
            pav = [psum(ph, f"pav{i}", [64, 1024]) for i in range(2)]
            for c in range(NCT):
                at = attn[c % 2]
                att = attnT[c % 2]
                for h in range(H):
                    pp = psim[(c * H + h) % 4]
                    qoc, qrow = h // 2, (h % 2) * 64
                    lo_dlt, hi_dlt = max(-1, -c), min(1, NCT - 1 - c)
                    if lo_dlt > -1:
                        p.op("vector", lambda v, pp=pp: v.memset(
                            pp[:, 0:128], NEG_BIG), ins=[pp], outs=[pp])
                    if hi_dlt < 1:
                        p.op("vector", lambda v, pp=pp: v.memset(
                            pp[:, 256:384], NEG_BIG), ins=[pp], outs=[pp])
                    interior = (lo_dlt == -1 and hi_dlt == 1)
                    p.mm(pp[:, (lo_dlt + 1) * 128:(hi_dlt + 2) * 128],
                         qkT[qoc][qrow:qrow + 64, c * 128:(c + 1) * 128],
                         qkT[4 + qoc][qrow:qrow + 64,
                                      (c + lo_dlt) * 128:(c + hi_dlt + 1) * 128],
                         start=True, stop=not interior)
                    if interior:
                        p.mm(pp[:], identb[:], lmask[:], start=False, stop=True)
                    else:
                        p.op("vector", lambda v, pp=pp: v.tensor_tensor(
                            out=pp[:], in0=pp[:], in1=lmask[:], op=ALU.add),
                            ins=[pp, lmask], outs=[pp])
                    p.op("scalar", lambda s, pp=pp, at=at, h=h, c=c: s.activation(
                        out=at[:, h * 384:(h + 1) * 384], in_=pp[:], func=AF.Exp,
                        scale=0.125, accum_out=den[:, c * H + h:c * H + h + 1]),
                        ins=[pp], outs=[at, den])
                p.op("vector", lambda v, c=c: v.reciprocal(
                    out=rden[:, c * H:(c + 1) * H], in_=den[:, c * H:(c + 1) * H]),
                    ins=[den], outs=[rden])
                for h in range(H):
                    p.op("vector", lambda v, at=at, h=h, c=c: v.tensor_scalar(
                        out=at[:, h * 384:(h + 1) * 384],
                        in0=at[:, h * 384:(h + 1) * 384],
                        scalar1=rden[:, c * H + h:c * H + h + 1], scalar2=None,
                        op0=ALU.mult), ins=[at, rden], outs=[at])
                if DEBUG and c == 3:
                    p.dma("sync", "dbg", dbg["dbg_attn"], at[:])
                p.dma("sync", "tat", att[:].rearrange("p (b t) -> p b t", b=24),
                      at[:].rearrange("p (b t) -> p b t", b=24), transpose=True)
                if c < 8:
                    w6_amax(c)
                elif c == 8:
                    w6_scale_batch()
                if 8 <= c < 16:
                    w6_quant(c - 8)
                pa = pav[c % 2]
                for h in range(H):
                    first = True
                    for di, dlt in enumerate((-1, 0, 1)):
                        kc = c + dlt
                        vt = vtok[kc] if 0 <= kc < NCT else vtok[NCT]
                        p.mm(pa[:, h * 128:(h + 1) * 128],
                             vt[:, h * 65:h * 65 + 64],
                             att[:, h * 384 + di * 128: h * 384 + (di + 1) * 128],
                             start=first, stop=(di == 2))
                        first = False
                for par in range(2):
                    p.op("vector", lambda v, pa=pa, par=par, c=c: v.tensor_copy(
                        out=avTv[par * 64:(par + 1) * 64, :, c, :],
                        in_=pa[:, :].rearrange("p (h t) -> p h t", t=128)[:, par::2, :]),
                        ins=[pa], outs=[avT])

        if DEBUG:
            o_avT = dout("dbg_avT", [128, 4 * N], BF16)
            p.dma("sync", "dbg", o_avT, avT[:])

        p.full_barrier()
        arena3.close()

        arena4 = ExitStack()
        def sb4(name, shape, dtype=F32):
            return arena4.enter_context(nc.sbuf_tensor(name, shape, dtype))
        wloutb = [sb4(f"wloutb{k}", [128, D], BF16) for k in range(4)]
        for k in range(4):
            p.dma("scalar", "gw3", scr_v[:], wlout_hbm[k * 128:(k + 1) * 128, :])
            p.barrier_dma("vector", "gw3")
            p.op("vector", lambda v, k=k: v.tensor_copy(
                out=wloutb[k][:], in_=scr_v[:]), ins=[scr_v], outs=[wloutb[k]])
        lout = [sb4(f"lout{i}", [128, D], BF16) for i in range(2)]
        with ExitStack() as ph:
            plo = [psum(ph, f"plo{i}", [128, D]) for i in range(2)]
            for c in range(NCT):
                pp = plo[c % 2]
                lo = lout[c % 2]
                for k in range(4):
                    for sl in range(2):
                        p.mm(pp[:, sl * 512:(sl + 1) * 512], avTv[:, k, c, :],
                             wloutb[k][:, sl * 512:(sl + 1) * 512],
                             start=(k == 0), stop=(k == 3))
                p.op("vector", lambda v, pp=pp, lo=lo: v.tensor_copy(
                    out=lo[:], in_=pp[:]), ins=[pp], outs=[lo])
                p.dma("sync", "slight", d_light[c * 128:(c + 1) * 128, :], lo[:])

        if DEBUG:
            o_light = dout("dbg_light", [N, D], BF16)
            p.barrier_dma("sync", "slight")
            lo2 = lout[0]
            for c in range(NCT):
                p.dma("sync", "dbg2", lo2[:], d_light[c * 128:(c + 1) * 128, :])
                p.dma("sync", "dbg", o_light[c * 128:(c + 1) * 128, :], lo2[:])

        p.full_barrier()
        arena4.close()
        light_arena.close()

        print("SBUF remaining before routing:", nc.sbuf_bytes_remaining)
        e0f = sb("e0f_s", [128, 128])
        trie = sb("trie_s", [128, 128], BF16)
        patkv = sb("patkv_s", [1, 32])
        onesf = sb("onesf", [128, 1])
        onesb = sb("onesb", [128, 1], BF16)
        t32a = sb("t32a", [32, 32])
        t32b = sb("t32b", [32, 32])
        p.dma("sync", "gwd", e0f[:], e0f_hbm)
        p.dma("sync", "gwe", trie[:], trie_hbm)
        p.dma("sync", "gwf", patkv[:], patkv_hbm)
        p.op("vector", lambda v: v.memset(onesf[:], 1.0), outs=[onesf])
        p.op("vector", lambda v: v.memset(onesb[:], 1.0), outs=[onesb])
        p.op("vector", lambda v: v.memset(t32a[:], 0.0), outs=[t32a])

        s_cm = sb("s_cm", [128, 32])
        rt1 = sb("rt1", [128, 32]); rt2 = sb("rt2", [128, 32])
        ret = sb("ret", [128, 32]); rred = sb("rred", [128, 4])
        prow = sb("prow", [128, 64])
        rowA = sb("rowA", [1, 2]); rowNA = sb("rowNA", [1, 2])
        rowMB = sb("rowMB", [1, 2]); rowT = sb("rowT", [1, 2])
        rowLN = sb("rowLN", [1, 2]); rowMX = sb("rowMX", [1, 2])
        rowLO = sb("rowLO", [1, 2]); rowHI = sb("rowHI", [1, 2])
        rowMD = sb("rowMD", [1, 2]); rowGE = sb("rowGE", [1, 2])
        rowGEi = sb("rowGEi", [1, 2], I32)
        rowLTi = sb("rowLTi", [1, 2], I32)
        rowQM = sb("rowQM", [1, 2]); rowQZ = sb("rowQZ", [1, 2])
        rowN1 = sb("rowN1", [1, 4])
        cl1 = sb("cl1", [128, 32]); clz = sb("clz", [128, 32])
        clm = sb("clm", [128, 32]); smsk = sb("smsk", [128, 32])
        scx = sb("scx", [128, 32])
        selm = sb("selm", [128, 32]); sel = sb("sel", [128, 32])
        pfx1 = sb("pfx1", [128, 32]); pfxz = sb("pfxz", [128, 32])
        slot = sb("slot", [128, 32])
        maskb = sb("maskb", [128, 32], BF16)
        tfcol = sb("tfcol", [128, 1])
        gidx_i = sb("gidx_i", [128, NCT], I32)
        idxq_i = sb("idxq_i", [128, NCT], I32)
        idxkv_i = sb("idxkv_i", [128, NCT], I32)

        arena_tok = ExitStack()
        tokb = [arena_tok.enter_context(nc.sbuf_tensor(f"tokb{r}", [128, D], F32))
                for r in range(2)]
        p.dma("sync", "gtok0", tokb[0][:], qtok_row.broadcast_to([128, D]))
        p.dma("sync", "gtok1", tokb[1][:], kvtok_row.broadcast_to([128, D]))
        for r in range(2):
            for c in range(NCT):
                p.op("vector", lambda v, r=r, c=c: v.scalar_tensor_tensor(
                    out=scr_v[:], in0=xn[c][:], scalar=1.0, in1=tokb[r][:],
                    op0=ALU.mult, op1=ALU.mult,
                    accum_out=s_cm[:, r * 16 + c:r * 16 + c + 1]),
                    ins=[xn[c], tokb[r]], outs=[scr_v, s_cm])
        p.full_barrier()
        arena_tok.close()

        s_v = s_cm[:].rearrange("p (r c) -> p r c", r=2)

        p.op("vector", lambda v: v.memset(prow[:], 0.0), outs=[prow])
        p.op("vector", lambda v: v.memset(rowA[:], A1), outs=[rowA])

        ffn_arena = ExitStack()
        qx4_box[0] = ffn_arena.enter_context(nc.sbuf_tensor("qx4w", [128, D], F32))
        g_st[0] = ffn_arena.enter_context(nc.sbuf_tensor("g_w4", [128, D], BF16))
        g_st.append(ffn_arena.enter_context(nc.sbuf_tensor("g_w4b", [128, D], BF16)))
        ytile_box[0] = [ffn_arena.enter_context(
            nc.sbuf_tensor(f"ytw4_{i}", [128, D], BF16)) for i in range(2)]
        yT_box[0] = [ffn_arena.enter_context(
            nc.sbuf_tensor(f"yTw4_{i}", [128, D], BF16)) for i in range(2)]
        XW = NCT * 32
        amx = ffn_arena.enter_context(nc.sbuf_tensor("amx", [128, XW], F32))
        qscx = ffn_arena.enter_context(nc.sbuf_tensor("qscx", [128, XW], F32))
        qrsx = ffn_arena.enter_context(nc.sbuf_tensor("qrsx", [128, XW], F32))
        qt1 = ffn_arena.enter_context(nc.sbuf_tensor("qt1", [128, XW], F32))
        qt2 = ffn_arena.enter_context(nc.sbuf_tensor("qt2", [128, XW], F32))
        for c in range(NCT):
            p.op("vector", lambda v, c=c: v.reduce_max(
                 out=amx[:, c * 32:(c + 1) * 32, None],
                 in_=xn[c][:].rearrange("p (b k) -> p b k", b=32),
                 axis=AX.X, apply_absolute_value=True), ins=[xn[c]], outs=[amx])
        p.op("vector", lambda v: v.memset(qt2[:], -10.0), outs=[qt2])
        for k in range(-9, 7):
            p.op("vector", lambda v, k=k: v.scalar_tensor_tensor(out=qt2[:],
                 in0=amx[:], scalar=float(2.0 ** k), in1=qt2[:],
                 op0=ALU.is_ge, op1=ALU.add), ins=[amx, qt2], outs=[qt2])
        p.op("vector", lambda v: v.tensor_scalar(out=qt1[:], in0=qt2[:],
             scalar1=dcol[:, 0:1], scalar2=-2.0, op0=ALU.add, op1=ALU.add),
             ins=[qt2, dcol], outs=[qt1])
        p.op("vector", lambda v: v.tensor_scalar(out=qt2[:], in0=qt1[:],
             scalar1=LN2_HI, scalar2=None, op0=ALU.mult), ins=[qt1], outs=[qt2])
        p.op("vector", lambda v: v.scalar_tensor_tensor(out=qt2[:], in0=qt1[:],
             scalar=LN2_LO, in1=qt2[:], op0=ALU.mult, op1=ALU.add),
             ins=[qt1], outs=[qt2])
        p.op("scalar", lambda sc: sc.activation(out=qscx[:], in_=qt2[:],
             func=AF.Exp), ins=[qt2], outs=[qscx])
        p.op("vector", lambda v: v.reciprocal(out=qrsx[:], in_=qscx[:]),
             ins=[qscx], outs=[qrsx])
        p.op("vector", lambda v: v.tensor_tensor(out=qt1[:], in0=qscx[:],
             in1=qrsx[:], op=ALU.mult), ins=[qscx, qrsx], outs=[qt1])
        p.op("vector", lambda v: v.tensor_scalar(out=qt1[:], in0=qt1[:],
             scalar1=-1.0, scalar2=2.0, op0=ALU.mult, op1=ALU.add),
             ins=[qt1], outs=[qt1])
        p.op("vector", lambda v: v.tensor_tensor(out=qrsx[:], in0=qrsx[:],
             in1=qt1[:], op=ALU.mult), ins=[qt1, qrsx], outs=[qrsx])
        p.op("vector", lambda v: v.tensor_scalar(out=qrsx[:], in0=qrsx[:],
             scalar1=2.0, scalar2=None, op0=ALU.mult), ins=[qrsx], outs=[qrsx])
        p.op("vector", lambda v: v.tensor_scalar(out=qscx[:], in0=qscx[:],
             scalar1=0.5, scalar2=None, op0=ALU.mult), ins=[qscx], outs=[qscx])

        _ffn_pending = list(range(NCT))
        _pump_ctr = [0]

        def pump_ffn(pf):
            _pump_ctr[0] += 1
            if _pump_ctr[0] % 3 != 1:
                return
            if _ffn_pending:
                ffn_chunk(_ffn_pending.pop(0), pf)

        with ExitStack() as ph:
            pbc = psum(ph, "pbc", [128, 64])
            psc = psum(ph, "psc", [1, 4])
            pf = [psum(ph, f"pf{i}", [128, D]) for i in range(2)]
            cur = 4.0 * 0.7
            for it in range(1, 20):
                ic = float(np.float32(1.0 / cur))
                p.op("vector", lambda v: v.tensor_scalar(out=rowNA[:], in0=rowA[:],
                     scalar1=-1.0, scalar2=None, op0=ALU.mult), ins=[rowA], outs=[rowNA])
                p.op("vector", lambda v, ic=ic: v.tensor_scalar(out=rowMB[:], in0=rowNA[:],
                     scalar1=ic, scalar2=None, op0=ALU.mult), ins=[rowNA], outs=[rowMB])
                p.op("vector", lambda v: v.tensor_copy(
                     out=prow[0:1, 0:32].rearrange("p (r c) -> p r c", r=2),
                     in_=rowNA[0:1, :, None].broadcast_to([1, 2, 16])),
                     ins=[rowNA], outs=[prow])
                p.op("vector", lambda v: v.tensor_copy(
                     out=prow[0:1, 32:64].rearrange("p (r c) -> p r c", r=2),
                     in_=rowMB[0:1, :, None].broadcast_to([1, 2, 16])),
                     ins=[rowMB], outs=[prow])
                p.mm(pbc[:], e0f[:], prow[:], start=True, stop=True, fp32=True)
                p.op("vector", lambda v: v.tensor_tensor(out=rt1[:], in0=s_cm[:],
                     in1=pbc[:, 0:32], op=ALU.min), ins=[s_cm, pbc], outs=[rt1])
                p.op("vector", lambda v, ic=ic: v.scalar_tensor_tensor(out=rt2[:],
                     in0=rt1[:], scalar=ic, in1=pbc[:, 32:64], op0=ALU.mult,
                     op1=ALU.subtract), ins=[rt1, pbc], outs=[rt2])
                p.op("scalar", lambda sc: sc.activation(out=ret[:], in_=rt2[:],
                     func=AF.Exp), ins=[rt2], outs=[ret])
                p.op("vector", lambda v: v.reduce_sum(out=rred[:, 0:2, None],
                     in_=ret[:].rearrange("p (r c) -> p r c", r=2), axis=AX.X),
                     ins=[ret], outs=[rred])
                p.mm(psc[:, 0:2], onesf[:], rred[:, 0:2], start=True, stop=True, fp32=True)
                p.op("scalar", lambda sc: sc.activation(out=rowLN[:], in_=psc[:, 0:2],
                     func=AF.Ln), ins=[psc], outs=[rowLN])
                p.op("vector", lambda v: v.tensor_tensor(out=rowT[:], in0=rowLN[:],
                     in1=rowMB[:], op=ALU.add), ins=[rowLN, rowMB], outs=[rowT])
                p.op("vector", lambda v, cur=cur: v.tensor_scalar(out=rowA[:],
                     in0=rowT[:], scalar1=float(np.float32(LOGK)),
                     scalar2=float(np.float32(-cur)), op0=ALU.subtract,
                     op1=ALU.mult), ins=[rowT], outs=[rowA])
                cur = max(cur * 0.7, 0.03)

            p.op("vector", lambda v: v.tensor_copy(
                 out=prow[0:1, 0:32].rearrange("p (r c) -> p r c", r=2),
                 in_=rowA[0:1, :, None].broadcast_to([1, 2, 16])),
                 ins=[rowA], outs=[prow])
            p.op("vector", lambda v: v.memset(rowT[:, 0:1], 2.0 ** -75), outs=[rowT])
            p.op("vector", lambda v: v.tensor_tensor(out=rowT[:, 1:2], in0=rowT[:, 0:1],
                 in1=rowT[:, 0:1], op=ALU.mult), ins=[rowT], outs=[rowT])
            p.op("vector", lambda v: v.tensor_tensor(out=rowT[:, 0:1], in0=rowT[:, 1:2],
                 in1=rowT[:, 1:2], op=ALU.mult), ins=[rowT], outs=[rowT])
            p.op("vector", lambda v: v.tensor_scalar(out=rowLN[:, 0:1], in0=rowT[:, 0:1],
                 scalar1=0.0, scalar2=None, op0=ALU.is_gt), ins=[rowT], outs=[rowLN])
            p.op("vector", lambda v: v.tensor_tensor(out=rowT[:, 1:2], in0=rowT[:, 0:1],
                 in1=rowT[:, 0:1], op=ALU.mult), ins=[rowT], outs=[rowT])
            p.op("vector", lambda v: v.tensor_tensor(out=rowT[:, 1:2], in0=rowT[:, 1:2],
                 in1=rowT[:, 0:1], op=ALU.mult), ins=[rowT], outs=[rowT])
            p.op("vector", lambda v: v.tensor_scalar(out=rowT[:, 1:2], in0=rowT[:, 1:2],
                 scalar1=16.0, scalar2=-1.0, op0=ALU.mult, op1=ALU.add),
                 ins=[rowT], outs=[rowT])
            p.op("vector", lambda v: v.tensor_tensor(out=rowT[:, 1:2], in0=rowT[:, 1:2],
                 in1=rowLN[:, 0:1], op=ALU.mult), ins=[rowT, rowLN], outs=[rowT])
            p.op("vector", lambda v: v.tensor_scalar(out=rowT[:, 1:2], in0=rowT[:, 1:2],
                 scalar1=1.0, scalar2=float(2.0 ** -126), op0=ALU.add, op1=ALU.mult),
                 ins=[rowT], outs=[rowT])
            p.op("vector", lambda v: v.tensor_copy(out=prow[0:1, 32:33],
                 in_=rowT[0:1, 1:2]), ins=[rowT], outs=[prow])
            p.mm(pbc[:], e0f[:], prow[:], start=True, stop=True, fp32=True)
            p.op("vector", lambda v: v.tensor_copy(out=tfcol[:], in_=pbc[:, 32:33]),
                 ins=[pbc], outs=[tfcol])
            p.op("vector", lambda v: v.tensor_tensor(out=rt1[:], in0=s_cm[:],
                 in1=pbc[:, 0:32], op=ALU.add), ins=[s_cm, pbc], outs=[rt1])
            p.op("vector", lambda v: v.tensor_scalar(out=cl1[:], in0=rt1[:],
                 scalar1=0.0, scalar2=None, op0=ALU.is_ge), ins=[rt1], outs=[cl1])
            p.op("vector", lambda v: v.tensor_scalar(out=rt2[:], in0=rt1[:],
                 scalar1=0.0, scalar2=None, op0=ALU.min), ins=[rt1], outs=[rt2])
            p.op("scalar", lambda sc: sc.activation(out=scx[:], in_=rt2[:],
                 func=AF.Exp, scale=float(np.float32(1.0 / 0.03))),
                 ins=[rt2], outs=[scx])
            p.op("vector", lambda v: v.scalar_tensor_tensor(out=scx[:], in0=scx[:],
                 scalar=tfcol[:], in1=scx[:], op0=ALU.is_ge, op1=ALU.mult),
                 ins=[scx, tfcol], outs=[scx])
            p.op("vector", lambda v: v.tensor_scalar(out=clz[:], in0=scx[:],
                 scalar1=0.0, scalar2=None, op0=ALU.is_equal), ins=[scx], outs=[clz])
            p.op("vector", lambda v: v.tensor_scalar(out=clm[:], in0=cl1[:],
                 scalar1=-1.0, scalar2=1.0, op0=ALU.mult, op1=ALU.add),
                 ins=[cl1], outs=[clm])
            p.op("vector", lambda v: v.tensor_tensor(out=clm[:], in0=clm[:],
                 in1=clz[:], op=ALU.subtract), ins=[clm, clz], outs=[clm])
            p.op("vector", lambda v: v.reduce_sum(out=rred[:, 0:2, None],
                 in_=cl1[:].rearrange("p (r c) -> p r c", r=2), axis=AX.X),
                 ins=[cl1], outs=[rred])
            p.op("vector", lambda v: v.reduce_sum(out=rred[:, 2:4, None],
                 in_=clm[:].rearrange("p (r c) -> p r c", r=2), axis=AX.X),
                 ins=[clm], outs=[rred])
            p.mm(psc[:, 0:4], onesf[:], rred[:, 0:4], start=True, stop=True, fp32=True)
            p.op("vector", lambda v: v.tensor_copy(out=rowN1[:], in_=psc[:, 0:4]),
                 ins=[psc], outs=[rowN1])
            p.op("vector", lambda v: v.tensor_scalar(out=rowT[:], in0=rowN1[:, 0:2],
                 scalar1=-1.0, scalar2=1024.0, op0=ALU.mult, op1=ALU.add),
                 ins=[rowN1], outs=[rowT])
            p.op("vector", lambda v: v.tensor_tensor(out=rowQM[:], in0=rowT[:],
                 in1=rowN1[:, 2:4], op=ALU.min), ins=[rowT, rowN1], outs=[rowQM])
            p.op("vector", lambda v: v.tensor_scalar(out=rowQM[:], in0=rowQM[:],
                 scalar1=0.0, scalar2=None, op0=ALU.max), ins=[rowQM], outs=[rowQM])
            p.op("vector", lambda v: v.tensor_tensor(out=rowQZ[:], in0=rowT[:],
                 in1=rowN1[:, 2:4], op=ALU.subtract), ins=[rowT, rowN1], outs=[rowQZ])
            p.op("vector", lambda v: v.tensor_scalar(out=rowQZ[:], in0=rowQZ[:],
                 scalar1=0.0, scalar2=None, op0=ALU.max), ins=[rowQZ], outs=[rowQZ])
            p.op("vector", lambda v: v.tensor_scalar(out=rt2[:], in0=clm[:],
                 scalar1=float(-NEG_BIG), scalar2=NEG_BIG, op0=ALU.mult, op1=ALU.add),
                 ins=[clm], outs=[rt2])
            p.op("vector", lambda v: v.tensor_tensor(out=smsk[:], in0=s_cm[:],
                 in1=clm[:], op=ALU.mult), ins=[s_cm, clm], outs=[smsk])
            p.op("vector", lambda v: v.tensor_tensor(out=smsk[:], in0=smsk[:],
                 in1=rt2[:], op=ALU.add), ins=[smsk, rt2], outs=[smsk])
            p.op("vector", lambda v: v.tensor_scalar(out=rowLO[:], in0=rowA[:],
                 scalar1=-1.0, scalar2=-23.0, op0=ALU.mult, op1=ALU.add),
                 ins=[rowA], outs=[rowLO])
            p.op("vector", lambda v: v.tensor_scalar(out=rowHI[:], in0=rowA[:],
                 scalar1=-1.0, scalar2=1.0, op0=ALU.mult, op1=ALU.add),
                 ins=[rowA], outs=[rowHI])
            for it in range(25):
                p.op("vector", lambda v: v.tensor_tensor(out=rowMD[:], in0=rowLO[:],
                     in1=rowHI[:], op=ALU.add), ins=[rowLO, rowHI], outs=[rowMD])
                p.op("vector", lambda v: v.tensor_scalar(out=rowMD[:], in0=rowMD[:],
                     scalar1=0.5, scalar2=None, op0=ALU.mult), ins=[rowMD], outs=[rowMD])
                p.op("vector", lambda v: v.tensor_copy(
                     out=prow[0:1, 0:32].rearrange("p (r c) -> p r c", r=2),
                     in_=rowMD[0:1, :, None].broadcast_to([1, 2, 16])),
                     ins=[rowMD], outs=[prow])
                p.mm(pbc[:, 0:32], e0f[:], prow[:, 0:32], start=True, stop=True, fp32=True)
                p.op("vector", lambda v: v.tensor_tensor(out=rt1[:], in0=smsk[:],
                     in1=pbc[:, 0:32], op=ALU.is_ge), ins=[smsk, pbc], outs=[rt1])
                p.op("vector", lambda v: v.reduce_sum(out=rred[:, 0:2, None],
                     in_=rt1[:].rearrange("p (r c) -> p r c", r=2), axis=AX.X),
                     ins=[rt1], outs=[rred])
                p.mm(psc[:, 0:2], onesf[:], rred[:, 0:2], start=True, stop=True, fp32=True)
                p.op("vector", lambda v: v.tensor_tensor(out=rowGE[:], in0=psc[:, 0:2],
                     in1=rowQM[:], op=ALU.is_ge), ins=[psc, rowQM], outs=[rowGE])
                p.op("vector", lambda v: v.tensor_copy(out=rowGEi[:], in_=rowGE[:]),
                     ins=[rowGE], outs=[rowGEi])
                p.op("vector", lambda v: v.tensor_scalar(out=rowGE[:], in0=rowGE[:],
                     scalar1=-1.0, scalar2=1.0, op0=ALU.mult, op1=ALU.add),
                     ins=[rowGE], outs=[rowGE])
                p.op("vector", lambda v: v.tensor_copy(out=rowLTi[:], in_=rowGE[:]),
                     ins=[rowGE], outs=[rowLTi])
                p.op("vector", lambda v: v.copy_predicated(out=rowLO[:],
                     mask=rowGEi[:], data=rowMD[:]),
                     ins=[rowGEi, rowMD, rowLO], outs=[rowLO])
                p.op("vector", lambda v: v.copy_predicated(out=rowHI[:],
                     mask=rowLTi[:], data=rowMD[:]),
                     ins=[rowLTi, rowMD, rowHI], outs=[rowHI])
            p.op("vector", lambda v: v.tensor_copy(
                 out=prow[0:1, 0:32].rearrange("p (r c) -> p r c", r=2),
                 in_=rowLO[0:1, :, None].broadcast_to([1, 2, 16])),
                 ins=[rowLO], outs=[prow])
            p.op("vector", lambda v: v.tensor_copy(
                 out=prow[0:1, 32:64].rearrange("p (r c) -> p r c", r=2),
                 in_=rowQZ[0:1, :, None].broadcast_to([1, 2, 16])),
                 ins=[rowQZ], outs=[prow])
            p.mm(pbc[:], e0f[:], prow[:], start=True, stop=True, fp32=True)
            p.op("vector", lambda v: v.tensor_tensor(out=selm[:], in0=smsk[:],
                 in1=pbc[:, 0:32], op=ALU.is_ge), ins=[smsk, pbc], outs=[selm])

            def prefix(mask_sb, out_sb, ppf):
                p.op("vector", lambda v: v.tensor_copy(out=maskb[:], in_=mask_sb[:]),
                     ins=[mask_sb], outs=[maskb])
                p.mm(ppf[0:1, 0:32], onesb[:], maskb[:], start=True, stop=True)
                p.op("vector", lambda v: v.tensor_copy(out=scr_v[0:1, 64:96],
                     in_=ppf[0:1, 0:32]), ins=[ppf], outs=[scr_v])
                p.op("vector", lambda v: v.tensor_tensor_scan(out=scr_v[0:1, 0:32],
                     data0=scr_v[0:1, 64:96], data1=scr_v[0:1, 64:96], initial=0.0,
                     op0=ALU.add, op1=ALU.bypass), ins=[scr_v], outs=[scr_v])
                p.op("vector", lambda v: v.tensor_scalar(out=rowT[:, 0:1],
                     in0=scr_v[0:1, 15:16], scalar1=-1.0, scalar2=None,
                     op0=ALU.mult), ins=[scr_v], outs=[rowT])
                p.op("vector", lambda v: v.scalar_tensor_tensor(out=scr_v[0:1, 32:64],
                     in0=patkv[:], scalar=rowT[:, 0:1], in1=scr_v[0:1, 0:32],
                     op0=ALU.mult, op1=ALU.add), ins=[patkv, rowT, scr_v], outs=[scr_v])
                p.op("vector", lambda v: v.tensor_tensor(out=prow[0:1, 0:32],
                     in0=scr_v[0:1, 32:64], in1=scr_v[0:1, 64:96], op=ALU.subtract),
                     ins=[scr_v], outs=[prow])
                p.mm(ppf[:, 0:32], trie[:], maskb[:], start=True, stop=False)
                p.mm(ppf[:, 0:32], e0f[:], prow[:, 0:32], start=False, stop=True,
                     fp32=True)
                p.op("vector", lambda v: v.tensor_copy(out=out_sb[:], in_=ppf[:, 0:32]),
                     ins=[ppf], outs=[out_sb])

            prefix(cl1, pfx1, pbc)
            p.op("vector", lambda v: v.tensor_scalar(out=rt1[:], in0=pfx1[:],
                 scalar1=1024.0, scalar2=None, op0=ALU.is_lt), ins=[pfx1], outs=[rt1])
            p.op("vector", lambda v: v.tensor_tensor(out=sel[:], in0=cl1[:],
                 in1=rt1[:], op=ALU.mult), ins=[cl1, rt1], outs=[sel])
            p.op("vector", lambda v: v.tensor_tensor(out=sel[:], in0=sel[:],
                 in1=selm[:], op=ALU.add), ins=[sel, selm], outs=[sel])
            prefix(clz, pfxz, pbc)
            p.op("vector", lambda v: v.tensor_copy(
                 out=prow[0:1, 0:32].rearrange("p (r c) -> p r c", r=2),
                 in_=rowQZ[0:1, :, None].broadcast_to([1, 2, 16])),
                 ins=[rowQZ], outs=[prow])
            p.mm(pbc[:, 32:64], e0f[:], prow[:, 0:32], start=True, stop=True, fp32=True)
            p.op("vector", lambda v: v.tensor_tensor(out=rt1[:], in0=pfxz[:],
                 in1=pbc[:, 32:64], op=ALU.is_lt), ins=[pfxz, pbc], outs=[rt1])
            p.op("vector", lambda v: v.tensor_tensor(out=rt1[:], in0=clz[:],
                 in1=rt1[:], op=ALU.mult), ins=[clz, rt1], outs=[rt1])
            p.op("vector", lambda v: v.tensor_tensor(out=sel[:], in0=sel[:],
                 in1=rt1[:], op=ALU.add), ins=[sel, rt1], outs=[sel])
            prefix(sel, slot, pbc)
            def mk_idx(out_i, r, oob):
                p.op("vector", lambda v, r=r: v.tensor_tensor(
                     out=rt1[:, 0:16], in0=slot[:, r * 16:(r + 1) * 16],
                     in1=sel[:, r * 16:(r + 1) * 16], op=ALU.mult),
                     ins=[slot, sel], outs=[rt1])
                p.op("vector", lambda v, r=r, oob=oob: v.tensor_scalar(
                     out=rt2[:, 0:16], in0=sel[:, r * 16:(r + 1) * 16],
                     scalar1=float(-oob), scalar2=float(oob), op0=ALU.mult,
                     op1=ALU.add), ins=[sel], outs=[rt2])
                p.op("vector", lambda v: v.tensor_tensor(out=rt1[:, 0:16],
                     in0=rt1[:, 0:16], in1=rt2[:, 0:16], op=ALU.add),
                     ins=[rt1, rt2], outs=[rt1])
                p.op("vector", lambda v, out_i=out_i: v.tensor_copy(
                     out=out_i[:], in_=rt1[:, 0:16]), ins=[rt1], outs=[out_i])
            mk_idx(idxq_i, 0, 4096)
            mk_idx(idxkv_i, 1, 4096)
            mk_idx(gidx_i, 0, 1024)

        if DEBUG:
            o_sel = dout("dbg_sel", [128, 32])
            o_arow = dout("dbg_arow", [1, 8])
            p.dma("sync", "dbg", o_sel, sel[:])
            p.op("vector", lambda v: v.tensor_copy(out=scr_v[0:1, 0:2], in_=rowA[:]),
                 ins=[rowA], outs=[scr_v])
            p.op("vector", lambda v: v.tensor_copy(out=scr_v[0:1, 2:4], in_=rowQM[:]),
                 ins=[rowQM], outs=[scr_v])
            p.op("vector", lambda v: v.tensor_copy(out=scr_v[0:1, 4:6], in_=rowQZ[:]),
                 ins=[rowQZ], outs=[scr_v])
            p.op("vector", lambda v: v.tensor_copy(out=scr_v[0:1, 6:8], in_=rowLO[:]),
                 ins=[rowLO], outs=[scr_v])
            p.dma("sync", "dbg", o_arow, scr_v[0:1, 0:8])

        print("SBUF remaining before rope:", nc.sbuf_bytes_remaining)
        rtab_arena = ExitStack()
        with ExitStack() as ph:
            pioc = ph.enter_context(nc.sbuf_tensor("pioc", [128, 1], I32))
            p.op("gpsimd", lambda g: g.iota(pioc[:], [[0, 1]],
                 channel_multiplier=1), outs=[pioc])
            p.op("vector", lambda v: v.tensor_copy(out=pcol[:], in_=pioc[:]),
                 ins=[pioc], outs=[pcol])

        hs_r = rtab_arena.enter_context(nc.sbuf_tensor("hs_r", [128, NCT], F32))
        xth = [rtab_arena.enter_context(nc.sbuf_tensor(f"xth{i}", [128, D + 388], BF16))
               for i in range(2)]
        for i in range(2):
            p.op("vector", lambda v, i=i: v.memset(xth[i][:, D:D + 4], 0.0),
                 outs=[xth[i]])
        for c in range(NCT):
            p.op("scalar", lambda sc, c=c: sc.activation(out=scr_a[:], in_=xn[c][:],
                 func=AF.Square, accum_out=stat_sq[:, c:c + 1]),
                 ins=[xn[c]], outs=[scr_a, stat_sq])
            p.op("scalar", lambda sc, c=c: sc.activation(out=scr_a[:, 0:1],
                 in_=stat_sq[:, c:c + 1], func=AF.Sqrt), ins=[stat_sq], outs=[scr_a])
            p.op("vector", lambda v, c=c: v.tensor_scalar(out=scr_v[:, 0:1],
                 in0=scr_a[:, 0:1], scalar1=1e-12, scalar2=None, op0=ALU.max),
                 ins=[scr_a], outs=[scr_v])
            p.op("vector", lambda v, c=c: v.reciprocal(out=hs_r[:, c:c + 1],
                 in_=scr_v[:, 0:1]), ins=[scr_v], outs=[hs_r])
            xt = xth[c % 2]
            p.dma("sync", f"gat{c % 2}", xt[:, D + 4:D + 388],
                  atab_hbm[c * 128:(c + 1) * 128, :])
            p.op("vector", lambda v, c=c, xt=xt: v.tensor_scalar(out=xt[:, 0:D],
                 in0=xn[c][:], scalar1=hs_r[:, c:c + 1], scalar2=None,
                 op0=ALU.mult), ins=[xn[c], hs_r], outs=[xt])
            p.op("vector", lambda v, c=c, xt=xt: v.memset(xt[:, D:D + 1], float(c)),
                 ins=[xt], outs=[xt])
            p.op("vector", lambda v, xt=xt: v.tensor_copy(out=xt[:, D + 1:D + 2],
                 in_=pcol[:]), ins=[pcol, xt], outs=[xt])
            p.idma("scat", d_selq, IndirectOffsetOnAxis(ap=idxq_i[:, c:c + 1], axis=0),
                   xt[:], None, bounds_check=NSEL - 1, oob_is_err=False)
            p.idma("scat", d_selkv, IndirectOffsetOnAxis(ap=idxkv_i[:, c:c + 1], axis=0),
                   xt[:], None, bounds_check=NSEL - 1, oob_is_err=False)

        nullb = rtab_arena.enter_context(nc.sbuf_tensor("nullb", [128, D], BF16))
        p.dma("scalar", "gnq", scr_v[0:1, 0:D], nullq_row)
        with ExitStack() as ph:
            pnq = psum(ph, "pnq", [128, D])
            p.mm(pnq[:, 0:512], e0f[:], scr_v[:, 0:512], start=True, stop=True, fp32=True)
            p.mm(pnq[:, 512:1024], e0f[:], scr_v[:, 512:1024], start=True, stop=True, fp32=True)
            p.op("vector", lambda v: v.tensor_copy(out=nullb[:], in_=pnq[:]),
                 ins=[pnq], outs=[nullb])
        for c in range(NCT):
            p.dma("scalar", "pfh", d_heavy2[c * 128:(c + 1) * 128, :], nullb[:])

        with ExitStack() as ph2:
            pf2 = [psum(ph2, f"pf2{i}", [128, D]) for i in range(2)]
            for c in range(NCT):
                ffn_chunk(c, pf2)

        p.full_barrier()
        rtab_arena.close()
        ffn_arena.close()

        print("SBUF remaining before heavy:", nc.sbuf_bytes_remaining)
        arena5 = ExitStack()
        def sb5(name, shape, dtype=F32):
            return arena5.enter_context(nc.sbuf_tensor(name, shape, dtype))
        wqb = [sb5(f"wqb{dc}", [128, 512], BF16) for dc in range(ND)]
        wkvb = [sb5(f"wkvb{dc}", [128, 1024], BF16) for dc in range(ND)]
        hgam_s = sb5("hgam_s", [128, ND])
        wst = scr_v
        p.dma("sync", "gwg", hgam_s[:], hgam_hbm)
        for dc in range(ND):
            p.dma("scalar", "gw4", wst[:, 0:512], wq_hbm[dc * 128:(dc + 1) * 128, :])
            p.barrier_dma("vector", "gw4")
            p.barrier_dma("vector", "gwg")
            p.op("vector", lambda v, dc=dc: v.tensor_scalar(out=wqb[dc][:],
                 in0=wst[:, 0:512], scalar1=hgam_s[:, dc:dc + 1], scalar2=32.0,
                 op0=ALU.mult, op1=ALU.mult), ins=[wst, hgam_s], outs=[wqb[dc]])
            p.dma("scalar", "gw4", wst[:], wkv_hbm[dc * 128:(dc + 1) * 128, :])
            p.barrier_dma("vector", "gw4")
            p.op("vector", lambda v, dc=dc: v.tensor_scalar(out=wkvb[dc][:],
                 in0=wst[:], scalar1=hgam_s[:, dc:dc + 1], scalar2=32.0,
                 op0=ALU.mult, op1=ALU.mult), ins=[wst, hgam_s], outs=[wkvb[dc]])
        nk_s = sb5("nk_s", [64, H])
        nv_s = sb5("nv_s", [1, H * DH])
        kT8 = sb5("kT8", [128, 512], BF16)
        vaug8 = sb5("vaug8", [128, H * 65], BF16)
        pm8 = sb5("pm8_s", [128, 1])
        p.dma("sync", "gwi", nk_s[:], nullk_hbm)
        p.dma("sync", "gwj", nv_s[:], nullv_hbm)
        p.dma("sync", "gwk", pm8[:], pm8_hbm)
        p.barrier_dma("vector", "gwi")
        p.barrier_dma("vector", "gwj")
        p.barrier_dma("vector", "gwk")
        p.op("vector", lambda v: v.memset(kT8[:], 0.0), outs=[kT8])
        for h in range(H):
            hr8 = (h % 2) * 64
            p.op("vector", lambda v, h=h, hr8=hr8: v.tensor_copy(
                 out=kT8[hr8:hr8 + 64, (h // 2) * 128:(h // 2) * 128 + 1],
                 in_=nk_s[:, h:h + 1]), ins=[nk_s], outs=[kT8])
        p.op("vector", lambda v: v.memset(vaug8[:], 0.0), outs=[vaug8])
        p.op("vector", lambda v: v.tensor_copy(
             out=vaug8[0:1, :].rearrange("p (h e) -> p h e", e=65)[:, :, 0:64],
             in_=nv_s[:].rearrange("p (h d) -> p h d", d=64)),
             ins=[nv_s], outs=[vaug8])
        p.op("vector", lambda v: v.memset(
             vaug8[:].rearrange("p (h e) -> p h e", e=65)[:, :, 64:65], 1.0),
             ins=[vaug8], outs=[vaug8])

        tokof_keep = [sb5(f"tokof{i}", [128, 1], I32) for i in range(8)]
        if DEBUG:
            dbg_xsq = dout("dbg_xsq", [256, 512], BF16)
            dbg_roq = dout("dbg_roq", [256, 128])
            dbg_qro = dout("dbg_qro", [256, 512])
            dbg_rtq = dout("dbg_rtq", [256, 512])
            dbg_xsT = dout("dbg_xsT", [256, 1024], BF16)
        qT_g = sb5("qT_g", [128, 8 * 512], BF16)
        kT_g = sb5("kT_g", [128, 8 * 512], BF16)
        vaug = [sb5(f"vaug{j}", [128, H * 65], BF16) for j in range(8)]
        with ExitStack() as ph:
            xsel = [sb5(f"xsel{i}", [128, D + 388], BF16) for i in range(2)]
            xselT = [sb5(f"xselT{i}", [128, D], BF16) for i in range(2)]
            rop = [sb5(f"rop{i}", [128, 128]) for i in range(2)]
            rang = sb5("rang", [128, 64])
            rscr = sb5("rscr", [128, 64])
            rl1 = sb5("rl1", [128, 64])
            rl2 = sb5("rl2", [128, 64])
            posj = sb5("posj", [128, 1])

            rotq = sb5("rotq", [128, 1024])
            qro = sb5("qro", [128, 1024])
            pq = [psum(ph, f"pq{i}", [128, 512]) for i in range(2)]
            pkv = [psum(ph, f"pkv{i}", [128, 1024]) for i in range(2)]

            for side, dsel in ((0, d_selq), (1, d_selkv)):
                for j in range(8):
                    xs = xsel[j % 2]; xsT = xselT[j % 2]; ro = rop[j % 2]
                    p.dma("sync", f"ghx{j % 2}", xs[:], dsel[j * 128:(j + 1) * 128, :])
                    p.barrier_dma("sync", f"ghx{j % 2}")
                    p.dma("sync", f"thx{j % 2}", xsT[:].rearrange("p (dc t) -> p dc t", dc=ND),
                          xs[:, 0:D].rearrange("p (dc t) -> p dc t", dc=ND),
                          transpose=True)
                    p.op("vector", lambda v, xs=xs: v.tensor_copy(
                         out=rscr[:, 0:2], in_=xs[:, D:D + 2]), ins=[xs], outs=[rscr])
                    p.op("vector", lambda v: v.scalar_tensor_tensor(out=posj[:],
                         in0=rscr[:, 0:1], scalar=128.0, in1=rscr[:, 1:2],
                         op0=ALU.mult, op1=ALU.add), ins=[rscr], outs=[posj])
                    if side == 0:
                        p.op("vector", lambda v, j=j: v.tensor_copy(
                             out=tokof_keep[j][:], in_=posj[:]),
                             ins=[posj], outs=[tokof_keep[j]])
                    p.op("vector", lambda v, xs=xs, ro=ro: v.tensor_tensor(
                         out=ro[:], in0=xs[:, D + 4:D + 132],
                         in1=xs[:, D + 132:D + 260], op=ALU.add),
                         ins=[xs], outs=[ro])
                    p.op("vector", lambda v, xs=xs, ro=ro: v.tensor_tensor(
                         out=ro[:], in0=ro[:], in1=xs[:, D + 260:D + 388],
                         op=ALU.add), ins=[xs, ro], outs=[ro])
                    if side == 0:
                        pp = pq[j % 2]
                        for dc in range(ND):
                            p.mm(pp[:], xsT[:].rearrange("p (dc t) -> p dc t", dc=ND)[:, dc, :],
                                 wqb[dc][:], start=(dc == 0), stop=(dc == ND - 1))
                        p.op("vector", lambda v, pp=pp: v.tensor_scalar(
                             out=rotq[:, 0:512].rearrange("p (h d) -> p h d", d=64)[:, :, 0:32],
                             in0=pp[:].rearrange("p (h d) -> p h d", d=64)[:, :, 32:64],
                             scalar1=-1.0, scalar2=None, op0=ALU.mult),
                             ins=[pp], outs=[rotq])
                        p.op("vector", lambda v, pp=pp: v.tensor_copy(
                             out=rotq[:, 0:512].rearrange("p (h d) -> p h d", d=64)[:, :, 32:64],
                             in_=pp[:].rearrange("p (h d) -> p h d", d=64)[:, :, 0:32]),
                             ins=[pp], outs=[rotq])
                        p.op("vector", lambda v, pp=pp, ro=ro: v.tensor_tensor(
                             out=qro[:, 0:512],
                             in0=pp[:].rearrange("p (h d) -> p h d", d=64),
                             in1=ro[:, None, 0:64].broadcast_to([128, 8, 64]),
                             op=ALU.mult), ins=[pp, ro], outs=[qro])
                        p.op("vector", lambda v, ro=ro: v.tensor_tensor(
                             out=rotq[:, 0:512],
                             in0=rotq[:, 0:512].rearrange("p (h d) -> p h d", d=64),
                             in1=ro[:, None, 64:128].broadcast_to([128, 8, 64]),
                             op=ALU.mult), ins=[rotq, ro], outs=[rotq])
                        p.op("vector", lambda v, xs=xs: v.tensor_tensor(
                             out=xs[:, 0:512], in0=qro[:, 0:512], in1=rotq[:, 0:512],
                             op=ALU.add), ins=[qro, rotq], outs=[xs])
                        if DEBUG and j < 2:
                            p.dma("sync", f"dbx{j}a", dbg_xsq[j * 128:(j + 1) * 128, :],
                                  xs[:, 0:512])
                            p.dma("sync", f"dbx{j}b", dbg_roq[j * 128:(j + 1) * 128, :],
                                  ro[:])
                            p.dma("sync", f"dbx{j}c", dbg_qro[j * 128:(j + 1) * 128, :],
                                  qro[:, 0:512])
                            p.dma("sync", f"dbx{j}d", dbg_rtq[j * 128:(j + 1) * 128, :],
                                  rotq[:, 0:512])
                            p.dma("sync", f"dbx{j}e", dbg_xsT[j * 128:(j + 1) * 128, :],
                                  xsT[:])
                            for _g in "abcde":
                                p.barrier_dma("sync", f"dbx{j}{_g}")
                                p.barrier_dma("vector", f"dbx{j}{_g}")
                        p.dma("sync", "thq",
                              qT_g[:, j * 512:(j + 1) * 512].rearrange("p (b t) -> p b t", b=4),
                              xs[:, 0:512].rearrange("p (b t) -> p b t", b=4),
                              transpose=True)
                    else:
                        pp = pkv[j % 2]
                        for dc in range(ND):
                            for sl in range(2):
                                p.mm(pp[:, sl * 512:(sl + 1) * 512],
                                     xsT[:].rearrange("p (dc t) -> p dc t", dc=ND)[:, dc, :],
                                     wkvb[dc][:, sl * 512:(sl + 1) * 512],
                                     start=(dc == 0), stop=(dc == ND - 1))
                        kv_v = lambda t: t.rearrange("p (h d) -> p h d", d=128)
                        p.op("vector", lambda v, pp=pp: v.tensor_scalar(
                             out=kv_v(rotq[:])[:, :, 0:32],
                             in0=kv_v(pp[:])[:, :, 32:64],
                             scalar1=-1.0, scalar2=None, op0=ALU.mult),
                             ins=[pp], outs=[rotq])
                        p.op("vector", lambda v, pp=pp: v.tensor_copy(
                             out=kv_v(rotq[:])[:, :, 32:64],
                             in_=kv_v(pp[:])[:, :, 0:32]), ins=[pp], outs=[rotq])
                        p.op("vector", lambda v, pp=pp, ro=ro: v.tensor_tensor(
                             out=kv_v(qro[:])[:, :, 0:64],
                             in0=kv_v(pp[:])[:, :, 0:64],
                             in1=ro[:, None, 0:64].broadcast_to([128, 8, 64]),
                             op=ALU.mult), ins=[pp, ro], outs=[qro])
                        p.op("vector", lambda v, ro=ro: v.tensor_tensor(
                             out=kv_v(rotq[:])[:, :, 0:64],
                             in0=kv_v(rotq[:])[:, :, 0:64],
                             in1=ro[:, None, 64:128].broadcast_to([128, 8, 64]),
                             op=ALU.mult), ins=[rotq, ro], outs=[rotq])
                        p.op("vector", lambda v, xs=xs: v.tensor_tensor(
                             out=xs[:, 0:512].rearrange("p (h d) -> p h d", d=64),
                             in0=kv_v(qro[:])[:, :, 0:64],
                             in1=kv_v(rotq[:])[:, :, 0:64],
                             op=ALU.add), ins=[qro, rotq], outs=[xs])
                        p.op("vector", lambda v, pp=pp, j=j: v.tensor_copy(
                             out=vaug[j][:].rearrange("p (h e) -> p h e", e=65)[:, :, 0:64],
                             in_=kv_v(pp[:])[:, :, 64:128]), ins=[pp], outs=[vaug[j]])
                        p.op("vector", lambda v, j=j: v.memset(
                             vaug[j][:].rearrange("p (h e) -> p h e", e=65)[:, :, 64:65],
                             1.0), ins=[vaug[j]], outs=[vaug[j]])
                        p.dma("sync", "thkv",
                              kT_g[:, j * 512:(j + 1) * 512].rearrange("p (b t) -> p b t", b=4),
                              xs[:, 0:512].rearrange("p (b t) -> p b t", b=4),
                              transpose=True)

        p.full_barrier()

        avTh = sb5("avTh", [128, 4 * 1024], BF16)
        if DEBUG:
            o_qTg = dout("dbg_qTg", [128, 8 * 512], BF16)
            o_kTg = dout("dbg_kTg", [128, 8 * 512], BF16)
            _unused = (dbg_xsq, dbg_roq)
            p.dma("sync", "dbg", o_qTg, qT_g[:])
            p.dma("sync", "dbg", o_kTg, kT_g[:])
        whoutb = [sb5(f"whoutb{k}", [128, D], BF16) for k in range(4)]
        for k in range(4):
            p.dma("scalar", "gw5", wst[:], whout_hbm[k * 128:(k + 1) * 128, :])
            p.barrier_dma("vector", "gw5")
            p.op("vector", lambda v, k=k: v.tensor_copy(out=whoutb[k][:],
                 in_=wst[:]), ins=[wst], outs=[whoutb[k]])
        att_s = [sb5(f"att_s{i}", [128, 1024], BF16) for i in range(2)]
        avu = sb5("avu", [64, 1024])
        rdst = [scr_a]
        p.op("vector", lambda v: v.memset(rdst[0][:], 0.0), outs=[rdst[0]])
        with ExitStack() as ph:
            psm = [psum(ph, f"psm{i}", [128, 1024]) for i in range(2)]
            pav = psum(ph, "pav", [65, 1024])
            pbc4 = psum(ph, "pbc4", [64, 1024])
            for h in range(H):
                hh, hrow = h // 2, (h % 2) * 64
                for kt in range(9):
                    pp = psm[kt % 2]
                    hp = h // 2
                    lhs = (kT_g[hrow:hrow + 64,
                                kt * 512 + hp * 128: kt * 512 + hp * 128 + 128]
                           if kt < 8 else
                           kT8[hrow:hrow + 64, hp * 128:hp * 128 + 128])
                    for sl in range(2):
                        p.mm(pp[:, sl * 512:(sl + 1) * 512], lhs,
                             qT_g[hrow:hrow + 64, :].rearrange(
                                 "p (j b t) -> p j b t", j=8, b=4)
                                 [:, sl * 4:(sl + 1) * 4, hh, :],
                             start=True, stop=True)
                    if kt == 8:
                        p.op("vector", lambda v, pp=pp: v.tensor_tensor(
                             out=pp[:], in0=pp[:],
                             in1=pm8[:].broadcast_to([128, 1024]), op=ALU.add),
                             ins=[pp, pm8], outs=[pp])
                    at = att_s[kt % 2]
                    p.op("scalar", lambda sc, pp=pp, at=at: sc.activation(
                         out=at[:], in_=pp[:], func=AF.Exp, scale=0.125),
                         ins=[pp], outs=[at])
                    vg = vaug[kt] if kt < 8 else vaug8
                    for sl in range(2):
                        p.mm(pav[:, sl * 512:(sl + 1) * 512],
                             vg[:, h * 65:h * 65 + 65],
                             at[:, sl * 512:(sl + 1) * 512],
                             start=(kt == 0), stop=(kt == 8))
                rd = rdst[0]
                p.op("vector", lambda v: v.reciprocal(out=avu[0:1, :],
                     in_=pav[64:65, :]), ins=[pav], outs=[avu])
                p.op("vector", lambda v, rd=rd: v.tensor_copy(out=rd[0:1, :],
                     in_=avu[0:1, :]), ins=[avu], outs=[rd])
                p.mm(pbc4[:, 0:512], e0f[:, 0:64], rd[:, 0:512], start=True,
                     stop=True, fp32=True)
                p.mm(pbc4[:, 512:1024], e0f[:, 0:64], rd[:, 512:1024], start=True,
                     stop=True, fp32=True)
                p.op("vector", lambda v: v.tensor_copy(out=avu[:], in_=pav[0:64, :]),
                     ins=[pav], outs=[avu])
                p.op("vector", lambda v, h=h, hh=hh, hrow=hrow: v.tensor_tensor(
                     out=avTh[hrow:hrow + 64, hh * 1024:(hh + 1) * 1024],
                     in0=avu[:], in1=pbc4[:], op=ALU.mult),
                     ins=[avu, pbc4], outs=[avTh])

        hvout = [ytile[0], yT[0]]
        with ExitStack() as ph:
            po = [psum(ph, f"po{i}", [128, D]) for i in range(2)]
            for j in range(8):
                pp = po[j % 2]
                for k in range(4):
                    for sl in range(2):
                        p.mm(pp[:, sl * 512:(sl + 1) * 512],
                             avTh[:, k * 1024 + j * 128: k * 1024 + j * 128 + 128],
                             whoutb[k][:, sl * 512:(sl + 1) * 512],
                             start=(k == 0), stop=(k == 3))
                hv = hvout[j % 2]
                p.op("vector", lambda v, pp=pp, hv=hv: v.tensor_copy(
                     out=hv[:], in_=pp[:]), ins=[pp], outs=[hv])
                p.idma("shv", d_heavy2, IndirectOffsetOnAxis(ap=tokof_keep[j][:], axis=0),
                       hv[:], None)

        p.full_barrier()
        arena5.close()

        arena6 = ExitStack()
        def sb6(name, shape, dtype=F32):
            return arena6.enter_context(nc.sbuf_tensor(name, shape, dtype))
        g_t = [sb6(f"g_t{i}", [128, D], BF16) for i in range(2)]
        lt_t = [sb6(f"lt_t{i}", [128, D], BF16) for i in range(2)]
        hv_t = [sb6(f"hv_t{i}", [128, D], BF16) for i in range(2)]
        xx_t = [sb6(f"xx_t{i}", [128, D]) for i in range(2)]
        o_t = [sb6(f"o_t{i}", [128, D]) for i in range(2)]
        rln = sb6("rln", [128, NCT])
        p.op("vector", lambda v: v.reciprocal(out=rln[:], in_=ln_r[:]),
             ins=[ln_r], outs=[rln])
        for c in range(NCT):
            gt = g_t[c % 2]
            lt = lt_t[c % 2]; hv = hv_t[c % 2]; xx = xx_t[c % 2]; ot_ = o_t[c % 2]
            p.dma("gpsimd", f"gg{c % 2}", gt[:], d_g[c * 128:(c + 1) * 128, :])
            p.dma("scalar", f"glt{c % 2}", lt[:], d_light[c * 128:(c + 1) * 128, :])
            p.dma("sync", f"ghv2{c % 2}", hv[:], d_heavy2[c * 128:(c + 1) * 128, :])
            p.barrier_dma("vector", f"gg{c % 2}")
            p.barrier_dma("vector", f"glt{c % 2}")
            p.barrier_dma("vector", f"ghv2{c % 2}")
            p.op("vector", lambda v, c=c, xx=xx: v.tensor_scalar(
                 out=xx[:], in0=xn[c][:], scalar1=rln[:, c:c + 1],
                 scalar2=ln_m[:, c:c + 1], op0=ALU.mult, op1=ALU.add),
                 ins=[xn[c], rln, ln_m], outs=[xx])
            p.op("vector", lambda v, lt=lt, hv=hv, ot_=ot_: v.tensor_tensor(
                 out=ot_[:], in0=lt[:], in1=hv[:], op=ALU.add),
                 ins=[lt, hv], outs=[ot_])
            p.op("vector", lambda v, gt=gt, ot_=ot_: v.tensor_tensor(
                 out=ot_[:], in0=ot_[:], in1=gt[:], op=ALU.mult),
                 ins=[ot_, gt], outs=[ot_])
            p.op("vector", lambda v, xx=xx, ot_=ot_: v.tensor_tensor(
                 out=ot_[:], in0=ot_[:], in1=xx[:], op=ALU.add),
                 ins=[ot_, xx], outs=[ot_])
            p.dma("sync", "sout", out_hbm[c * 128:(c + 1) * 128, :], ot_[:])

        if DEBUG:
            o_hv = dout("dbg_heavy", [N, D], BF16)
            for c in range(NCT):
                p.dma("scalar", "dbgh", lt_t[0][:], d_heavy2[c * 128:(c + 1) * 128, :])
                p.dma("sync", "dbg", o_hv[c * 128:(c + 1) * 128, :], lt_t[0][:])
        arena6.close()
        blk = es.enter_context(nc.Block())
        for grp in list(p.dma_groups):
            p.barrier_dma("sync", grp)
        p.finalize(blk)

    return nc, dbg


def _host_inputs(inp, core):
    m = {}
    m["x"] = inp["L0_attn_x"][core]
    m["wqkv"] = inp["L0_attn_light_qkv_w"]
    m["wlout"] = inp["L0_attn_light_out_w"]
    m["lgam"] = inp["L0_attn_ln_gamma"].reshape(ND, 128).T
    m["lbeta"] = inp["L0_attn_ln_beta"].reshape(ND, 128).T
    m["qtok"] = inp["L0_attn_q_route_tok"].reshape(1, D)
    m["kvtok"] = inp["L0_attn_kv_route_tok"].reshape(1, D)
    m["hgam"] = inp["L0_attn_heavy_gamma"].reshape(ND, 128).T
    m["wq"] = inp["L0_attn_heavy_q_w"]
    m["wkv"] = np.transpose(inp["L0_attn_heavy_kv_w"], (2, 0, 1)).reshape(D, H * 2 * DH)
    m["nullk"] = inp["L0_attn_heavy_null_kv"][0].T
    m["nullv"] = inp["L0_attn_heavy_null_kv"][1].reshape(1, H * DH)
    m["whout"] = inp["L0_attn_heavy_out_w"]
    m["nullq"] = inp["L0_attn_null_q_token"].reshape(1, D)
    m["wffn"] = inp["L0_ffn_weight"]
    m["bffn"] = inp["L0_ffn_bias"].reshape(1, D)
    m["dw"] = inp["L0_ffn_delta_w"]
    m["dact"] = inp["L0_ffn_delta_act"].reshape(1, 1)
    pq = np.arange(128)[:, None]
    mask = np.full((128, 384), NEG_BIG, np.float32)
    for di in range(3):
        j = np.arange(128)[None, :]
        valid = np.abs(128 * (di - 1) + j - pq) <= 64
        mask[:, di * 128:(di + 1) * 128] = np.where(valid, 0.0, NEG_BIG)
    m["lmask"] = mask.astype(ml_dtypes.bfloat16)
    m["identb"] = np.eye(128, dtype=ml_dtypes.bfloat16)
    e0 = np.zeros((128, 128), np.float32); e0[0, :] = 1.0
    m["e0f"] = e0
    m["trie"] = (np.arange(128)[:, None] < np.arange(128)[None, :]).astype(ml_dtypes.bfloat16)
    pk = np.zeros((1, 32), np.float32); pk[0, 16:] = 1.0
    m["patkv"] = pk
    pm8 = np.full((128, 1), NEG_BIG, np.float32); pm8[0, 0] = 0.0
    m["pm8"] = pm8
    f4 = np.float32
    j32 = (np.arange(0, 64, 2).astype(f4) / f4(64.0)).astype(f4)
    powv = np.exp(j32.astype(np.float64) * np.log(10000.0)).astype(f4)
    invf32 = (f4(1.0) / powv).astype(f4)
    tab = (np.arange(N).astype(f4)[:, None] * invf32[None]).astype(f4)
    ang = np.concatenate([tab, tab], axis=1)
    try:
        import jax as _jax
        import jax.numpy as _jnp
        _cpu = _jax.devices("cpu")[0]
        with _jax.default_device(_cpu):
            ctab = np.asarray(_jnp.cos(_jnp.asarray(ang)))
            stab = np.asarray(_jnp.sin(_jnp.asarray(ang)))
    except Exception:
        ctab = np.cos(ang.astype(np.float64)).astype(f4)
        stab = np.sin(ang.astype(np.float64)).astype(f4)
    cs = np.concatenate([ctab, stab], axis=1).astype(f4)
    hi = cs.astype(ml_dtypes.bfloat16)
    r1 = (cs - hi.astype(f4)).astype(f4)
    lo = r1.astype(ml_dtypes.bfloat16)
    r2 = (r1 - lo.astype(f4)).astype(f4)
    lo2 = r2.astype(ml_dtypes.bfloat16)
    assert np.array_equal((hi.astype(f4) + lo.astype(f4)) + lo2.astype(f4), cs)
    m["atab"] = np.concatenate([hi, lo, lo2], axis=1)
    return {k: np.ascontiguousarray(v) for k, v in m.items()}


def kernel(**inputs):
    nc, dbg = build_core()
    in_maps = [_host_inputs(inputs, core) for core in range(B)]
    res = run_bass_kernel_spmd(nc, in_maps, core_ids=list(range(B)))
    if DEBUG:
        kernel.debug = [{k: r[k] for k in dbg} for r in res.results]
    return np.stack([r["out"] for r in res.results], axis=0)
